# version 07
# baseline (speedup 1.0000x reference)
import numpy as np
from contextlib import ExitStack

import concourse.bass as bass
import concourse.mybir as mybir
from concourse.bass_utils import run_bass_kernel_spmd

F32 = mybir.dt.float32
BF16 = mybir.dt.bfloat16
AF = mybir.ActivationFunctionType
ALU = mybir.AluOpType
AX = mybir.AxisListType

NCORES = 8
B, L, D = 2, 8192, 1024
DI, NST, DTR = 2048, 16, 64
E, HID = 8, 4096
TOK = 2048
GT = B * L
DSL = DI // NCORES
NGT = 16
GTW = GT // NGT
EPS = 1e-5
BIGNEG = 100.0


def build_nc(dbg=False):
    nc = bass.Bass("TRN2", target_bir_lowering=False, debug=False)

    def din(name, shape, dt=F32):
        return nc.dram_tensor(name, shape, dt, kind="ExternalInput")

    dkind = "ExternalOutput" if dbg else "Internal"

    xT = din("xT", [D, TOK])
    ident = din("ident", [128, 128])
    idx128 = din("idx128", [128, E])
    gwT = din("gwT", [D, E])
    gb = din("gb", [1, E])
    normw_r = din("normw_r", [128, 8])
    wxiT = din("wxiT", [D, DSL])
    wzT = din("wzT", [D, DSL])
    convw_r = din("convw_r", [128, 8])
    convb_r = din("convb_r", [128, 2])
    xpwT = din("xpwT", [DSL, 96])
    dtw = din("dtw", [DTR, DSL])
    dtb_r = din("dtb_r", [128, 2])
    alog_r = din("alog_r", [128, 32])
    dp_r = din("dp_r", [128, 2])
    opwT_sl = din("opwT_sl", [DSL, D])
    b1T = din("b1T", [128, E * 32])
    b2T = din("b2T", [128, E * 8])
    w1r = din("w1r", [E, 128, 32768])
    w2T = din("w2T", [E, HID, D])

    out_d = nc.dram_tensor("out", [D, TOK], F32, kind="ExternalOutput")

    d_xnbf = nc.dram_tensor("d_xnbf", [D, TOK], BF16, kind=dkind)
    d_m = nc.dram_tensor("d_m", [E, TOK], BF16, kind=dkind)
    d_pm = nc.dram_tensor("d_pm", [E, TOK], BF16, kind=dkind)
    d_rows = nc.dram_tensor("d_rows", [4, TOK], F32)
    ag1_in = nc.dram_tensor("ag1_in", [D, TOK], BF16)
    ag1_out = nc.dram_tensor("ag1_out", [NCORES * D, TOK], BF16)
    ar_in = nc.dram_tensor("ar_in", [96, GT], F32)
    ar_out = nc.dram_tensor("ar_out", [96, GT], F32)
    d_bc = nc.dram_tensor("d_bc", [32, GT], BF16)
    d_xc = nc.dram_tensor("d_xc", [DSL, GT], BF16, kind=dkind)
    d_delta = nc.dram_tensor("d_delta", [DSL, GT], BF16)
    d_silz = nc.dram_tensor("d_silz", [DSL, GT], BF16, kind=dkind)
    d_w1bf = nc.dram_tensor("d_w1bf", [E, 128, 32768], BF16)
    d_w2bf = nc.dram_tensor("d_w2bf", [E, HID, D], BF16)
    rs_in = nc.dram_tensor("rs_in", [NCORES, D, TOK], F32)
    rs_out = nc.dram_tensor("rs_out", [D, TOK], F32)
    if dbg:
        dbg_ar = nc.dram_tensor("dbg_ar", [96, GT], F32, kind="ExternalOutput")
        dbg_rs = nc.dram_tensor("dbg_rs", [D, TOK], F32, kind="ExternalOutput")
        dbg_moe = nc.dram_tensor("dbg_moe", [128, 16384], BF16, kind="ExternalOutput")
        dbg_rsp = nc.dram_tensor("dbg_rsp", [D, TOK], F32, kind="ExternalOutput")
        dbg_bc = nc.dram_tensor("dbg_bc", [32, GT], BF16, kind="ExternalOutput")
        dbg_sc = nc.dram_tensor("dbg_sc", [8, 128, 1024], BF16, kind="ExternalOutput")
        dbg_m2 = nc.dram_tensor("dbg_m2", [4, 128, 4096], BF16, kind="ExternalOutput")

    with ExitStack() as stk:
        EE = stk.enter_context

        A1 = EE(nc.sbuf_tensor("A1", [128, 12288], BF16))
        A2 = EE(nc.sbuf_tensor("A2", [128, 16384], BF16))
        A3 = EE(nc.sbuf_tensor("A3", [128, 16384], BF16))
        A4 = EE(nc.sbuf_tensor("A4", [128, 43008], BF16))
        AM = EE(nc.sbuf_tensor("AM", [128, 11264], BF16))
        AW2 = EE(nc.sbuf_tensor("AW2", [128, 2048], BF16))
        opw_s = EE(nc.sbuf_tensor("opw_s", [128, 2048], BF16))

        ones_s = EE(nc.sbuf_tensor("ones_s", [128, 1], F32))
        ident_s = EE(nc.sbuf_tensor("ident_s", [128, 128], F32))
        idx_s = EE(nc.sbuf_tensor("idx_s", [128, E], F32))
        gw_s = EE(nc.sbuf_tensor("gw_s", [128, 64], F32))
        gbb_s = EE(nc.sbuf_tensor("gbb_s", [128, E], F32))
        normw_s = EE(nc.sbuf_tensor("normw_s", [128, 8], F32))
        convw_s = EE(nc.sbuf_tensor("convw_s", [128, 8], F32))
        convb_s = EE(nc.sbuf_tensor("convb_s", [128, 2], F32))
        dtw_s = EE(nc.sbuf_tensor("dtw_s", [64, DSL], F32))
        dtb_s = EE(nc.sbuf_tensor("dtb_s", [128, 2], F32))
        alog_s = EE(nc.sbuf_tensor("alog_s", [128, 32], F32))
        a_s = EE(nc.sbuf_tensor("a_s", [128, 32], F32))
        dp_s = EE(nc.sbuf_tensor("dp_s", [128, 2], F32))
        b1_s = EE(nc.sbuf_tensor("b1_s", [128, E * 32], F32))
        b2_s = EE(nc.sbuf_tensor("b2_s", [128, E * 8], F32))
        state_s = EE(nc.sbuf_tensor("state_s", [128, 32], BF16))
        cst_s = EE(nc.sbuf_tensor("cst_s", [128, 2], F32))

        ps = [EE(nc.psum_tensor(f"psb{i}", [128, 512], F32)) for i in range(8)]

        sem_names = ["s0", "mm", "vv", "aq", "gg", "cc", "gw1p", "gw2p",
                     "gbc", "smk", "w1r0", "w1r1", "w1r2", "w1r3", "w1r4", "w1r5",
                     "w2r0", "w2r1", "w2r2", "w2r3", "xnb", "pmm",
                     "sr0", "sr1", "sar", "stl0", "stl1",
                     "sb0", "sb1", "sb2", "sb3", "srs0", "srs1",
                     "sf0", "sf1", "sxc", "gp1", "sar1", "ssz"]
        S = {}
        for name in sem_names:
            S[name] = EE(nc.semaphore(name))
        C = {k: 0 for k in S}

        block = EE(nc.Block())
        prog = {"tensor": [], "vector": [], "scalar": [], "gpsimd": [], "sync": []}

        def em(eng, fn):
            prog[eng].append(fn)

        def bump(name, by=1):
            C[name] += by
            return C[name]

        def dma(eng_name, out_ap, in_ap, sem="s0"):
            h = S[sem]
            t = bump(sem, 16)

            def fn(e, out_ap=out_ap, in_ap=in_ap, h=h):
                e.dma_start(out=out_ap, in_=in_ap).then_inc(h, 16)
            em(eng_name, fn)
            return t

        def wait(eng_name, sem, tgt):
            if tgt <= 0:
                return
            h = S[sem]

            def fn(e, h=h, tgt=tgt):
                e.wait_ge(h, tgt)
            em(eng_name, fn)

        def vop(fb, sem=None):
            if sem is None:
                em("vector", lambda v, f=fb: f(v))
                return None
            h = S[sem]
            t = bump(sem)

            def fn(v, f=fb, h=h):
                f(v).then_inc(h, 1)
            em("vector", fn)
            return t

        def aop(fb, sem=None):
            if sem is None:
                em("scalar", lambda s, f=fb: f(s))
                return None
            h = S[sem]
            t = bump(sem)

            def fn(s, f=fb, h=h):
                f(s).then_inc(h, 1)
            em("scalar", fn)
            return t

        def gop(fb, sem=None):
            if sem is None:
                em("gpsimd", lambda g, f=fb: f(g))
                return None
            h = S[sem]
            t = bump(sem)

            def fn(g, f=fb, h=h):
                f(g).then_inc(h, 1)
            em("gpsimd", fn)
            return t

        def vdr():
            em("vector", lambda v: v.drain())

        def adr():
            em("scalar", lambda s: s.drain())

        def mm_group(mms, sem="mm"):
            t = None
            for i, mb in enumerate(mms):
                if i == len(mms) - 1:
                    h = S[sem]
                    t = bump(sem)

                    def fn(tt_, f=mb, h=h):
                        f(tt_).then_inc(h, 1)
                    em("tensor", fn)
                else:
                    em("tensor", lambda tt_, f=mb: f(tt_))
            return t

        A1b, A2b, A3b, A4b = A1.ap(), A2.ap(), A3.ap(), A4.ap()
        AMb, AW2b = AM.ap(), AW2.ap()
        A1f = A1.ap().bitcast(F32)
        A2f = A2.ap().bitcast(F32)
        A3f = A3.ap().bitcast(F32)
        A4f = A4.ap().bitcast(F32)
        AMf = AM.ap().bitcast(F32)
        opwf = opw_s.ap().bitcast(F32)

        h_sb = A2b
        out_acc = A3b
        xe_sl = [AMb[:, 0:4096], AMb[:, 4096:8192]]
        pm_e = AMb[:, 8192:8704]
        m_e = AMb[:, 8704:9216]
        ystg = [AMb[:, 9216 + i * 512:9216 + (i + 1) * 512] for i in range(2)]
        t1_b = [AMb[:, 10240 + i * 512:10240 + (i + 1) * 512] for i in range(2)]
        w1sl = [A1b[:, i * 2048:(i + 1) * 2048] for i in range(6)]
        w2b = [AW2b[:, i * 512:(i + 1) * 512] for i in range(4)]

        delta = [[A4b[:, (par * 2 + d2) * 1024:(par * 2 + d2 + 1) * 1024]
                  for d2 in range(2)] for par in range(2)]
        dA_pp = [A4f[:, 2048 + i * 1024:2048 + (i + 1) * 1024] for i in range(4)]
        y_t = [A4f[:, 6144 + d2 * 1024:6144 + (d2 + 1) * 1024] for d2 in range(2)]
        du_t = [A4b[:, 16384 + d2 * 1024:16384 + (d2 + 1) * 1024] for d2 in range(2)]
        dBu_t = [A4b[:, 18432 + d2 * 1024:18432 + (d2 + 1) * 1024] for d2 in range(2)]
        hs_t = [A4b[:, 20480 + d2 * 1024:20480 + (d2 + 1) * 1024] for d2 in range(2)]
        hC_t = [A4b[:, 22528 + d2 * 1024:22528 + (d2 + 1) * 1024] for d2 in range(2)]
        Bnb = [A4b[:, 24576 + i * 1024:24576 + (i + 1) * 1024] for i in range(4)]
        Cnb = [A4b[:, 28672 + i * 1024:28672 + (i + 1) * 1024] for i in range(4)]
        xc2_t = [[A4b[:, 32768 + (par * 2 + d2) * 1024:
                      32768 + (par * 2 + d2 + 1) * 1024] for d2 in range(2)]
                 for par in range(2)]
        sz2_t = [A4b[:, 36864 + d2 * 1024:36864 + (d2 + 1) * 1024] for d2 in range(2)]
        yop_t = [A4b[:, 38912 + d2 * 1024:38912 + (d2 + 1) * 1024] for d2 in range(2)]
        po2 = [A4f[:, 20480 + i * 512:20480 + (i + 1) * 512] for i in range(2)]
        es_t = po2[0]
        dstage = A4b[:, 41984:42496]
        dt_sb = A4f[0:64, 16384:17408]

        def x_chunk(dc):
            if dc < 3:
                return A1f[:, dc * 2048:(dc + 1) * 2048]
            if dc < 7:
                return A2f[:, (dc - 3) * 2048:(dc - 2) * 2048]
            return A4f[:, 16384:18432]

        def xn_chunk(dc):
            return A4f[:, dc * 2048:(dc + 1) * 2048]

        rstd_b = AMf[:, 0:2048]
        nmr_b = AMf[:, 2048:4096]

        def rw(i):
            return A3f[0:1, i * 2048:(i + 1) * 2048]
        gscr = A3f[:, 0:1024]
        msb = A3b[0:16, 2048:4096]
        xnstg = [A3b[:, 4096 + i * 2048:4096 + (i + 1) * 2048] for i in range(2)]
        ag1stg = [A3b[:, 8192 + i * 2048:8192 + (i + 1) * 2048] for i in range(2)]

        rhs_t = [[A4b[:, (bu * 8 + k) * 1024:(bu * 8 + k + 1) * 1024]
                  for k in range(8)] for bu in range(2)]
        wxi_s = A4b[:, 16384:18432]
        wz_s = A4b[:, 18432:20480]
        xc_t = [A4b[:, 20480 + i * 1024:20480 + (i + 1) * 1024] for i in range(2)]
        sz_st = A4b[:, 22528:23552]
        xpw_s = A4b[:, 23552:23744]
        xi_ext = [A4f[:, 11872:12899], A4f[:, 12899:13926]]
        cv_a = A4f[:, 13926:14950]
        cv_b = A4f[:, 14950:15974]
        po_st = [opwf[:, 0:512], opwf[:, 512:1024]]

        f_rs = [A1f[:, i * 512:(i + 1) * 512] for i in range(2)]
        f_x = [A1f[:, 1024 + i * 512:1024 + (i + 1) * 512] for i in range(2)]
        f_g = [A1f[:, 2048 + i * 512:2048 + (i + 1) * 512] for i in range(2)]
        f_s = [A1f[:, 3072 + i * 512:3072 + (i + 1) * 512] for i in range(2)]

        vop(lambda v: v.memset(ones_s[:, :], 1.0))
        vop(lambda v: v.memset(state_s[:, :], 0.0))
        vop(lambda v: v.memset(cst_s[:, 0:1], EPS))
        cst_vv = vop(lambda v: v.memset(cst_s[:, 1:2], 1.0), sem="vv")
        wait("scalar", "vv", cst_vv)
        dma("sync", ident_s[:, :], ident[:, :])
        dma("sync", idx_s[:, :], idx128[:, :])
        for dc in range(8):
            dma("sync", gw_s[:, dc * 8:(dc + 1) * 8], gwT[dc * 128:(dc + 1) * 128, :])
        dma("sync", gbb_s[:, :], gb.ap().partition_broadcast(128)[:, 0, :])
        dma("sync", normw_s[:, :], normw_r[:, :])
        dma("sync", convw_s[:, :], convw_r[:, :])
        dma("sync", convb_s[:, :], convb_r[:, :])
        dma("sync", dtw_s[:, :], dtw[:, :])
        dma("sync", dtb_s[:, :], dtb_r[:, :])
        dma("sync", alog_s[:, :], alog_r[:, :])
        dma("sync", dp_s[:, :], dp_r[:, :])
        dma("sync", b1_s[:, :], b1T[:, :])
        dma("sync", b2_s[:, :], b2T[:, :])
        consts_ld = C["s0"]

        wait("scalar", "s0", consts_ld)
        a_aq = aop(lambda s: s.activation(out=a_s[:, :], in_=alog_s[:, :],
                                          func=AF.Exp), sem="aq")
        wait("vector", "aq", a_aq)
        vop(lambda v: v.tensor_scalar(out=a_s[:, :], in0=a_s[:, :], scalar1=-1.0,
                                      scalar2=None, op0=ALU.mult))
        vdr()

        w1p_mark = {}
        w2p_mark = {}

        def emit_precast(e_):
            for hf in range(2):
                dma("gpsimd", d_w1bf[e_, :, hf * 16384:(hf + 1) * 16384],
                    w1r[e_, :, hf * 16384:(hf + 1) * 16384], sem="gw1p")
            w1p_mark[e_] = C["gw1p"]
            for hc4 in range(0, 32, 8):
                dma("gpsimd", d_w2bf[e_, hc4 * 128:(hc4 + 8) * 128, :],
                    w2T[e_, hc4 * 128:(hc4 + 8) * 128, :], sem="gw2p")
            w2p_mark[e_] = C["gw2p"]

        emit_precast(0)
        emit_precast(1)

        for dc in range(8):
            dma("sync", x_chunk(dc), xT[dc * 128:(dc + 1) * 128, :])
        x_ld = C["s0"]

        wait("scalar", "s0", x_ld)
        for dc in range(8):
            aop(lambda s, dc=dc: s.activation(out=xn_chunk(dc), in_=x_chunk(dc),
                                              func=AF.Square))
        adr()
        sq_aq = aop(lambda s: s.nop(), sem="aq")

        wait("tensor", "s0", x_ld)
        wait("tensor", "aq", sq_aq)
        for nt in range(4):
            mms = []
            for dc in range(8):
                mms.append(lambda t, nt=nt, dc=dc: t.matmul(
                    ps[6][0:1, :], ones_s[:, :],
                    x_chunk(dc)[:, nt * 512:(nt + 1) * 512],
                    start=(dc == 0), stop=(dc == 7)))
            for dc in range(8):
                mms.append(lambda t, nt=nt, dc=dc: t.matmul(
                    ps[7][0:1, :], ones_s[:, :],
                    xn_chunk(dc)[:, nt * 512:(nt + 1) * 512],
                    start=(dc == 0), stop=(dc == 7)))
            mmv = mm_group(mms)
            wait("vector", "mm", mmv)
            vop(lambda v, nt=nt: v.tensor_copy(
                rw(0)[:, nt * 512:(nt + 1) * 512], ps[6][0:1, :]))
            bank_vv = vop(lambda v, nt=nt: v.tensor_copy(
                rw(1)[:, nt * 512:(nt + 1) * 512], ps[7][0:1, :]), sem="vv")
            wait("tensor", "vv", bank_vv)
        stats_mm = C["mm"]

        vop(lambda v: v.tensor_scalar(out=rw(0), in0=rw(0), scalar1=1.0 / D,
                                      scalar2=None, op0=ALU.mult))
        vdr()
        vop(lambda v: v.tensor_scalar(out=rw(1), in0=rw(1), scalar1=1.0 / D,
                                      scalar2=None, op0=ALU.mult))
        vdr()
        vop(lambda v: v.tensor_tensor(out=rw(2), in0=rw(0), in1=rw(0), op=ALU.mult))
        vdr()
        var_vv = vop(lambda v: v.tensor_tensor(out=rw(1), in0=rw(1), in1=rw(2),
                                               op=ALU.subtract), sem="vv")
        wait("scalar", "vv", var_vv)
        aop(lambda s: s.activation(out=rw(2), in_=rw(1), func=AF.Ln,
                                   bias=cst_s[0:1, 0:1]))
        adr()
        rstd_aq = aop(lambda s: s.activation(out=rw(2), in_=rw(2), func=AF.Exp,
                                             scale=-0.5), sem="aq")
        wait("vector", "aq", rstd_aq)
        vop(lambda v: v.tensor_tensor(out=rw(3), in0=rw(2), in1=rw(2), op=ALU.mult))
        vdr()
        t5_vv = vop(lambda v: v.tensor_tensor(out=rw(3), in0=rw(3), in1=rw(1),
                                              op=ALU.mult), sem="vv")
        wait("scalar", "vv", t5_vv)
        aop(lambda s: s.activation(out=rw(3), in_=rw(3), func=AF.Ln,
                                   bias=cst_s[0:1, 0:1]))
        adr()
        rrms_aq = aop(lambda s: s.activation(out=rw(3), in_=rw(3), func=AF.Exp,
                                             scale=-0.5), sem="aq")
        wait("vector", "aq", rrms_aq)
        nmr_vv = vop(lambda v: v.scalar_tensor_tensor(
            out=rw(0), in0=rw(0), scalar=-1.0, in1=rw(2), op0=ALU.mult,
            op1=ALU.mult), sem="vv")
        wait("sync", "vv", nmr_vv)
        dma("sync", d_rows[0:1, :], rw(2))
        dma("sync", d_rows[1:2, :], rw(0))
        dma("sync", d_rows[2:3, :], rw(3))
        rows_st = C["s0"]
        wait("sync", "s0", rows_st)
        dma("sync", rstd_b, d_rows.ap()[0:1, :].partition_broadcast(128)[:, 0, :])
        dma("sync", nmr_b, d_rows.ap()[1:2, :].partition_broadcast(128)[:, 0, :])
        bco_ld = C["s0"]

        wait("vector", "s0", bco_ld)
        wait("vector", "mm", stats_mm)
        for dc in range(8):
            vop(lambda v, dc=dc: v.tensor_tensor(
                out=xn_chunk(dc), in0=x_chunk(dc), in1=rstd_b, op=ALU.mult))
            vdr()
            vop(lambda v, dc=dc: v.tensor_tensor(
                out=xn_chunk(dc), in0=xn_chunk(dc), in1=nmr_b, op=ALU.add))
            vdr()
        xn_vv = vop(lambda v: v.nop(), sem="vv")

        wait("sync", "vv", xn_vv)
        dma("sync", rstd_b, d_rows.ap()[2:3, :].partition_broadcast(128)[:, 0, :])
        rrms_ld2 = C["s0"]
        wait("vector", "s0", rrms_ld2)
        rrms_b = rstd_b

        wait("vector", "s0", rows_st)
        stg_stq = [0, 0]
        for dc in range(8):
            bu = dc % 2
            if dc >= 2:
                wait("vector", "s0", C["s0"])
            vop(lambda v, dc=dc, bu=bu: v.tensor_copy(
                xnstg[bu][:, :], xn_chunk(dc)))
            st_vv = vop(lambda v, dc=dc, bu=bu: v.scalar_tensor_tensor(
                out=ag1stg[bu][:, :], in0=xn_chunk(dc),
                scalar=normw_s[:, dc:dc + 1], in1=rrms_b, op0=ALU.mult,
                op1=ALU.mult), sem="vv")
            wait("sync", "vv", st_vv)
            dma("sync", d_xnbf[dc * 128:(dc + 1) * 128, :], xnstg[bu][:, :])
            dma("sync", ag1_in[dc * 128:(dc + 1) * 128, :], ag1stg[bu][:, :])
            stg_stq[bu] = C["s0"]
        ag1in_st = C["s0"]

        wait("vector", "s0", rows_st)
        wait("tensor", "vv", xn_vv)
        for tc in range(16):
            mms = []
            for dc in range(8):
                mms.append(lambda t, tc=tc, dc=dc: t.matmul(
                    ps[5][:, 0:8],
                    xn_chunk(dc)[:, tc * 128:(tc + 1) * 128],
                    gw_s[:, dc * 8:(dc + 1) * 8], start=(dc == 0), stop=(dc == 7)))
            mmv = mm_group(mms)
            wait("vector", "mm", mmv)

            def gsl(a, b_):
                return gscr[:, a:b_]
            vop(lambda v: v.tensor_tensor(out=gsl(0, 8), in0=ps[5][:, 0:8],
                                          in1=gbb_s[:, :], op=ALU.add))
            vdr()
            bank_vv = vop(lambda v: v.nop(), sem="vv")
            wait("tensor", "vv", bank_vv)
            vop(lambda v: v.tensor_reduce(out=gsl(8, 9), in_=gsl(0, 8),
                                          axis=AX.X, op=ALU.max, negate=True))
            vdr()
            sm_vv = vop(lambda v: v.nop(), sem="vv")
            wait("scalar", "vv", sm_vv)
            e_aq = aop(lambda s: s.activation(out=gsl(0, 8), in_=gsl(0, 8),
                                              func=AF.Exp, bias=gsl(8, 9)),
                       sem="aq")
            wait("vector", "aq", e_aq)
            vop(lambda v: v.tensor_reduce(out=gsl(8, 9), in_=gsl(0, 8),
                                          axis=AX.X, op=ALU.add))
            vdr()
            vop(lambda v: v.reciprocal(gsl(9, 10), gsl(8, 9)))
            vdr()
            vop(lambda v: v.tensor_scalar(out=gsl(0, 8), in0=gsl(0, 8),
                                          scalar1=gsl(9, 10), scalar2=None,
                                          op0=ALU.mult))
            vdr()
            vop(lambda v: v.tensor_reduce(out=gsl(8, 9), in_=gsl(0, 8),
                                          axis=AX.X, op=ALU.max))
            vdr()
            vop(lambda v: v.tensor_scalar(out=gsl(10, 18), in0=gsl(0, 8),
                                          scalar1=gsl(8, 9), scalar2=None,
                                          op0=ALU.is_ge))
            vdr()
            vop(lambda v: v.scalar_tensor_tensor(
                out=gsl(18, 26), in0=gsl(10, 18), scalar=-BIGNEG, in1=gsl(0, 8),
                op0=ALU.mult, op1=ALU.add))
            vdr()
            vop(lambda v: v.tensor_reduce(out=gsl(8, 9), in_=gsl(18, 26),
                                          axis=AX.X, op=ALU.max))
            vdr()
            vop(lambda v: v.tensor_scalar(out=gsl(18, 26), in0=gsl(0, 8),
                                          scalar1=gsl(8, 9), scalar2=None,
                                          op0=ALU.is_ge))
            vdr()
            vop(lambda v: v.tensor_tensor(out=gsl(18, 26), in0=gsl(18, 26),
                                          in1=idx_s[:, :], op=ALU.mult))
            vdr()
            vop(lambda v: v.tensor_reduce(out=gsl(8, 9), in_=gsl(18, 26),
                                          axis=AX.X, op=ALU.max))
            vdr()
            vop(lambda v: v.tensor_scalar(out=gsl(26, 34), in0=idx_s[:, :],
                                          scalar1=gsl(8, 9), scalar2=None,
                                          op0=ALU.is_equal))
            vdr()
            vop(lambda v: v.tensor_tensor(out=gsl(34, 42), in0=gsl(0, 8),
                                          in1=gsl(26, 34), op=ALU.mult))
            vdr()
            mt_vv = vop(lambda v: v.nop(), sem="vv")
            wait("tensor", "vv", mt_vv)
            tp_mm = mm_group([lambda t: t.transpose(
                ps[4][0:16, 0:128], gsl(26, 42), ident_s[:, :])])
            wait("vector", "mm", tp_mm)
            bank_vv = vop(lambda v, tc=tc: v.tensor_copy(
                msb[:, tc * 128:(tc + 1) * 128], ps[4][0:16, 0:128]), sem="vv")
            wait("tensor", "vv", bank_vv)
        gate_vv = C["vv"]
        gate_mm = C["mm"]
        wait("scalar", "vv", gate_vv)
        dma("scalar", d_m[:, :], msb[0:8, :], sem="smk")
        dma("scalar", d_pm[:, :], msb[8:16, :], sem="smk")
        mask_st = C["smk"]

        wait("gpsimd", "s0", ag1in_st)
        em("gpsimd", lambda g: g.collective_compute(
            "AllGather", ALU.bypass, replica_groups=[list(range(NCORES))],
            ins=[ag1_in.ap().opt()], outs=[ag1_out.ap().opt()]).then_inc(S["cc"], 1))
        bump("cc")

        wait("vector", "smk", mask_st)
        outacc_zero_vv = vop(lambda v: v.memset(A3[:, :], 0.0), sem="vv")

        wait("sync", "smk", mask_st)
        for dc in range(8):
            dma("sync", xe_sl[0][:, dc * 512:(dc + 1) * 512],
                d_xnbf[dc * 128:(dc + 1) * 128, 0:512], sem="xnb")
        dma("sync", pm_e, d_pm.ap()[0:1, 0:512].partition_broadcast(128)[:, 0, :],
            sem="pmm")
        dma("sync", m_e, d_m.ap()[0:1, 0:512].partition_broadcast(128)[:, 0, :],
            sem="pmm")

        w1_slot_reader = [0] * 6
        w1_slot_loaded = [0] * 6
        w1_seq = []
        w1_emit = [0]
        w1_next_chunk = [0]

        def ensure_w1_entries(upto_cyc):
            while w1_next_chunk[0] <= min(upto_cyc, 31):
                ee = w1_next_chunk[0] // 4
                w1_seq.extend((ee, hg) for hg in range(16))
                w1_next_chunk[0] += 1
        w2_slot_reader = [0, 0, 0, 0]
        w2_slot_loaded = [0, 0, 0, 0]
        w2_i = 0
        xe_done_gg = [0, 0]
        hpass_done_mm = {}
        ypass_bank_add_vv = [outacc_zero_vv] * 4
        relu_aq = {}
        ystg_free_gg = [0, 0]
        t1_free_vv = [outacc_zero_vv, outacc_zero_vv]
        xnbf_loaded = {0: C["xnb"]}
        pmm_loaded = {0: C["pmm"]}
        moe_state = {"mm_total": 0, "last_t1_gg": 0}

        def w1_emit_loads(upto):
            while w1_emit[0] <= upto and w1_emit[0] < len(w1_seq):
                i = w1_emit[0]
                e_, hg = w1_seq[i]
                sl = i % 6
                wait("sync", "mm", w1_slot_reader[sl])
                if hg == 0:
                    wait("sync", "gw1p", w1p_mark[e_])
                w1_slot_loaded[sl] = dma(
                    "sync", w1sl[sl], d_w1bf[e_, :, hg * 2048:(hg + 1) * 2048],
                    sem="w1r%d" % sl)
                w1_emit[0] = i + 1

        def emit_moe_chunk(cyc):
            e_, tt = divmod(cyc, 4)
            nonlocal w2_i
            ttp = cyc % 2
            w1_base = 16 * cyc
            ensure_w1_entries(cyc + 1)
            wait("gpsimd", "xnb", xnbf_loaded[cyc])
            wait("gpsimd", "pmm", pmm_loaded[cyc])
            for dc in range(8):
                xe_done_gg[ttp] = gop(lambda g, dc=dc, ttp=ttp: g.tensor_tensor(
                    out=xe_sl[ttp][:, dc * 512:(dc + 1) * 512],
                    in0=xe_sl[ttp][:, dc * 512:(dc + 1) * 512], in1=pm_e,
                    op=ALU.mult), sem="gg")
            if cyc + 1 < 32:
                e2_, tt2 = divmod(cyc + 1, 4)
                tsl2 = slice(tt2 * 512, (tt2 + 1) * 512)
                if cyc >= 1:
                    wait("sync", "mm", hpass_done_mm[cyc - 1])
                for dc in range(8):
                    dma("sync", xe_sl[(cyc + 1) % 2][:, dc * 512:(dc + 1) * 512],
                        d_xnbf[dc * 128:(dc + 1) * 128, tsl2], sem="xnb")
                xnbf_loaded[cyc + 1] = C["xnb"]
            for hc in range(32):
                ghg = w1_base + hc // 2
                sl = ghg % 6
                bank = hc % 2
                if hc == 0:
                    wait("tensor", "gg", xe_done_gg[ttp])
                if hc % 2 == 0:
                    w1_emit_loads(ghg + 4)
                    wait("tensor", "w1r%d" % sl, w1_slot_loaded[sl])
                if (cyc, hc - 2) in relu_aq:
                    wait("tensor", "aq", relu_aq[(cyc, hc - 2)])
                elif hc < 2 and (cyc - 1, 30 + hc) in relu_aq:
                    wait("tensor", "aq", relu_aq[(cyc - 1, 30 + hc)])
                mms = []
                for kd in range(8):
                    mms.append(lambda t, hc=hc, kd=kd, bank=bank, sl=sl, ttp=ttp:
                               t.matmul(
                                   ps[bank][:, :],
                                   w1sl[sl][:, (hc % 2) * 1024 + kd * 128:
                                            (hc % 2) * 1024 + (kd + 1) * 128],
                                   xe_sl[ttp][:, kd * 512:(kd + 1) * 512],
                                   start=(kd == 0), stop=(kd == 7)))
                mmv = mm_group(mms)
                if hc % 2 == 1:
                    w1_slot_reader[sl] = mmv
                wait("scalar", "mm", mmv)
                relu_aq[(cyc, hc)] = aop(
                    lambda s, hc=hc, e_=e_, bank=bank: s.activation(
                        out=h_sb[:, hc * 512:(hc + 1) * 512], in_=ps[bank][:, :],
                        func=AF.Relu, bias=b1_s[:, e_ * 32 + hc:e_ * 32 + hc + 1]),
                    sem="aq")
            hpass_done_mm[cyc] = C["mm"]
            h_done = relu_aq[(cyc, 31)]
            if dbg and cyc in (0, 3):
                wait("scalar", "gg", xe_done_gg[ttp])
                dma("scalar", dbg_m2[0 + (cyc == 3), :, :], xe_sl[ttp], sem="smk")
                wait("scalar", "aq", h_done)
                dma("scalar", dbg_m2[2 + (cyc == 3), :, :], h_sb[:, 0:4096], sem="smk")
            for dcg in range(2):
                for dc4 in range(4):
                    wait("tensor", "vv", ypass_bank_add_vv[dc4])
                for hc in range(32):
                    wi = w2_i % 4
                    wait("sync", "mm", w2_slot_reader[wi])
                    if hc == 0 and dcg == 0 and tt == 0:
                        wait("sync", "gw2p", w2p_mark[e_])
                    w2_slot_loaded[wi] = dma(
                        "sync", w2b[wi],
                        d_w2bf[e_, hc * 128:(hc + 1) * 128,
                               dcg * 512:(dcg + 1) * 512], sem="w2r%d" % wi)
                    w2_i += 1
                    wait("tensor", "w2r%d" % wi, w2_slot_loaded[wi])
                    if hc == 0:
                        wait("tensor", "aq", h_done)
                    mms = []
                    for dc4 in range(4):
                        mms.append(lambda t, wi=wi, dc4=dc4, hc=hc: t.matmul(
                            ps[2 + dc4][:, :],
                            w2b[wi][:, dc4 * 128:(dc4 + 1) * 128],
                            h_sb[:, hc * 512:(hc + 1) * 512],
                            start=(hc == 0), stop=(hc == 31)))
                    mmv = mm_group(mms)
                    w2_slot_reader[wi] = mmv
                for dc4 in range(4):
                    dc = dcg * 4 + dc4
                    pp = dc4 % 2
                    wait("scalar", "mm", mmv)
                    wait("scalar", "gg", ystg_free_gg[pp])
                    ev_aq = aop(lambda s, e_=e_, dc=dc, dc4=dc4, pp=pp:
                                s.activation(
                                    out=ystg[pp][:, :], in_=ps[2 + dc4][:, :],
                                    func=AF.Identity,
                                    bias=b2_s[:, e_ * 8 + dc:e_ * 8 + dc + 1]),
                                sem="aq")
                    wait("gpsimd", "aq", ev_aq)
                    wait("gpsimd", "vv", t1_free_vv[pp])
                    ystg_free_gg[pp] = gop(lambda g, pp=pp: g.tensor_tensor(
                        out=t1_b[pp][:, :], in0=ystg[pp][:, :], in1=m_e,
                        op=ALU.mult), sem="gg")
                    wait("vector", "gg", ystg_free_gg[pp])
                    osl = slice(dc * 2048 + tt * 512, dc * 2048 + (tt + 1) * 512)
                    t1_free_vv[pp] = vop(lambda v, osl=osl, pp=pp: v.tensor_tensor(
                        out=out_acc[:, osl], in0=out_acc[:, osl],
                        in1=t1_b[pp][:, :], op=ALU.add), sem="vv")
                    ypass_bank_add_vv[dc4] = t1_free_vv[pp]
            w1_emit_loads(w1_base + 19)
            moe_state["mm_total"] = C["mm"]
            moe_state["last_t1_gg"] = max(ystg_free_gg)
            if cyc + 1 < 32:
                e2_, tt2 = divmod(cyc + 1, 4)
                tsl2 = slice(tt2 * 512, (tt2 + 1) * 512)
                wait("sync", "gg", moe_state["last_t1_gg"])
                dma("sync", pm_e,
                    d_pm.ap()[e2_:e2_ + 1, tsl2].partition_broadcast(128)[:, 0, :],
                    sem="pmm")
                dma("sync", m_e,
                    d_m.ap()[e2_:e2_ + 1, tsl2].partition_broadcast(128)[:, 0, :],
                    sem="pmm")
                pmm_loaded[cyc + 1] = C["pmm"]
            if tt == 1 and e_ + 2 < E:
                emit_precast(e_ + 2)

        p1 = {
            "rhs_ld": [0, 0], "rhs_free": [0, 0], "bank_free": [0, 0],
            "xi_free": 0, "sz_free": 0, "first": True,
        }

        def emit_p1_gt(gt):
            bu = gt % 2
            rank = gt // 2
            lc0 = (gt % 2) * 1024
            if p1["first"]:
                p1["first"] = False
                wait("gpsimd", "mm", gate_mm)
                for dc in range(8):
                    dma("gpsimd", wxi_s[:, dc * DSL:(dc + 1) * DSL],
                        wxiT[dc * 128:(dc + 1) * 128, :], sem="gp1")
                    dma("gpsimd", wz_s[:, dc * DSL:(dc + 1) * DSL],
                        wzT[dc * 128:(dc + 1) * 128, :], sem="gp1")
                for kc in range(2):
                    dma("gpsimd", xpw_s[:, kc * 96:(kc + 1) * 96],
                        xpwT[kc * 128:(kc + 1) * 128, :], sem="gp1")
                p1["w_ld"] = C["gp1"]
                wait("sync", "cc", 1)
                wait("tensor", "gp1", p1["w_ld"])
                wait("vector", "gp1", p1["w_ld"])
                vop(lambda v: v.memset(xi_ext[0][:, 0:3], 0.0))
                vop(lambda v: v.memset(xi_ext[1][:, 0:3], 0.0))
                vdr()
            wait("sync", "mm", p1["rhs_free"][bu])
            for k in range(8):
                dma("sync", rhs_t[bu][k],
                    ag1_out[rank * D + k * 128:rank * D + (k + 1) * 128,
                            lc0:lc0 + 1024], sem="sr%d" % bu)
            p1["rhs_ld"][bu] = C["sr%d" % bu]
            wait("scalar", "vv", p1["xi_free"])
            for mg in range(4):
                w_s = wxi_s if mg < 2 else wz_s
                mc = mg % 2
                for nt in range(2):
                    bp = (mg * 2 + nt) % 2
                    bk = 6 + bp
                    wait("tensor", "aq", p1["bank_free"][bp])
                    if mg == 0 and nt == 0:
                        wait("tensor", "sr%d" % bu, p1["rhs_ld"][bu])
                    mms = []
                    for kd in range(8):
                        mms.append(lambda t, w_s=w_s, mc=mc, kd=kd, nt=nt, bk=bk,
                                   bu=bu: t.matmul(
                                       ps[bk][:, :],
                                       w_s[:, kd * DSL + mc * 128:
                                           kd * DSL + (mc + 1) * 128],
                                       rhs_t[bu][kd][:, nt * 512:(nt + 1) * 512],
                                       start=(kd == 0), stop=(kd == 7)))
                    mmv = mm_group(mms)
                    wait("scalar", "mm", mmv)
                    if mg < 2:
                        p1["bank_free"][bp] = aop(
                            lambda s, mc=mc, nt=nt, bk=bk: s.copy(
                                xi_ext[mc][:, 3 + nt * 512:3 + (nt + 1) * 512],
                                ps[bk][:, :]), sem="aq")
                    else:
                        if nt == 0:
                            wait("scalar", "ssz", C["ssz"])
                        p1["bank_free"][bp] = aop(
                            lambda s, nt=nt, bk=bk: s.activation(
                                out=sz_st[:, nt * 512:(nt + 1) * 512],
                                in_=ps[bk][:, :], func=AF.Silu), sem="aq")
                if mg >= 2:
                    wait("scalar", "aq", C["aq"])
                    dma("scalar",
                        d_silz[mc * 128:(mc + 1) * 128, gt * GTW:(gt + 1) * GTW],
                        sz_st[:, 0:1024], sem="ssz")
            xi_aq = C["aq"]
            p1["rhs_free"][bu] = C["mm"]
            for mc in range(2):
                wait("vector", "aq", xi_aq)
                src = xi_ext[mc]
                vop(lambda v, src=src, mc=mc: v.tensor_scalar(
                    out=cv_a, in0=src[:, 0:1024],
                    scalar1=convw_s[:, mc * 4:mc * 4 + 1],
                    scalar2=None, op0=ALU.mult))
                vdr()
                for k in range(1, 4):
                    a_buf, b_buf = (cv_a, cv_b) if k % 2 == 1 else (cv_b, cv_a)
                    vop(lambda v, src=src, mc=mc, k=k, a_buf=a_buf, b_buf=b_buf:
                        v.scalar_tensor_tensor(
                            out=b_buf, in0=src[:, k:k + 1024],
                            scalar=convw_s[:, mc * 4 + k:mc * 4 + k + 1],
                            in1=a_buf, op0=ALU.mult, op1=ALU.add))
                    vdr()
                if gt != NGT - 1:
                    if gt == 7:
                        vop(lambda v, src=src: v.memset(src[:, 0:3], 0.0))
                    else:
                        vop(lambda v, src=src: v.tensor_copy(src[:, 0:3],
                                                             src[:, 1024:1027]))
                vdr()
                cv_vv = vop(lambda v: v.nop(), sem="vv")
                wait("scalar", "vv", cv_vv)
                wait("scalar", "sxc", C["sxc"])
                xc_aq = aop(lambda s, mc=mc: s.activation(
                    out=xc_t[mc][:, :], in_=cv_b, func=AF.Silu,
                    bias=convb_s[:, mc:mc + 1]), sem="aq")
                wait("scalar", "aq", xc_aq)
                dma("scalar", d_xc[mc * 128:(mc + 1) * 128, gt * GTW:(gt + 1) * GTW],
                    xc_t[mc][:, :], sem="sxc")
                wait("vector", "aq", xc_aq)
            p1["xi_free"] = C["vv"]
            wait("tensor", "aq", C["aq"])
            for nt in range(2):
                bk = 6 + nt
                sarn = "sar" if nt == 0 else "sar1"
                wait("tensor", "aq", p1["bank_free"][nt])
                mms = []
                for mc in range(2):
                    mms.append(lambda t, mc=mc, nt=nt, bk=bk: t.matmul(
                        ps[bk][0:96, :], xpw_s[:, mc * 96:(mc + 1) * 96],
                        xc_t[mc][:, nt * 512:(nt + 1) * 512],
                        start=(mc == 0), stop=(mc == 1)))
                mmv = mm_group(mms)
                wait("scalar", "mm", mmv)
                wait("scalar", sarn, C[sarn])
                p1["bank_free"][nt] = aop(lambda s, bk=bk, nt=nt: s.copy(
                    po_st[nt][0:96, :], ps[bk][0:96, :]), sem="aq")
                wait("scalar", "aq", p1["bank_free"][nt])
                dma("scalar",
                    ar_in[:, gt * GTW + nt * 512:gt * GTW + (nt + 1) * 512],
                    po_st[nt][0:96, :], sem=sarn)

        sc = {
            "dt_mm": 0, "tl_ld": {}, "du_vv": {},
            "dA_aq": {}, "scan_vv": {}, "state_gg": {}, "hC_gg": {},
            "yacc_vv": {}, "fin_vv": {}, "bc_ld": {},
            "yop_rd_mm": {}, "b67": [0, 0], "opw_ld": 0,
        }

        def emit_scan_loads(j):
            if j >= 16:
                return
            par = j % 2
            cols = slice(j * GTW, (j + 1) * GTW)
            if j == 0:
                wait("scalar", "smk", C["smk"])
            if j >= 2:
                wait("scalar", "vv", sc["fin_vv"][j - 2])
            for mc in range(2):
                dma("scalar", xc2_t[par][mc], d_xc[mc * 128:(mc + 1) * 128, cols],
                    sem="stl%d" % par)
                dma("scalar", delta[par][mc],
                    d_delta[mc * 128:(mc + 1) * 128, cols], sem="stl%d" % par)
            sc["tl_ld"][j] = C["stl%d" % par]

        def emit_dt_burst():
            wait("scalar", "cc", 2)
            wait("scalar", "sxc", C["sxc"])
            wait("scalar", "ssz", C["ssz"])
            dma("gpsimd", opw_s[:, 0:D], opwT_sl[0:128, :], sem="gp1")
            dma("gpsimd", opw_s[:, D:2 * D], opwT_sl[128:256, :], sem="gp1")
            sc["opw_ld"] = C["gp1"]
            wait("tensor", "gp1", sc["opw_ld"])
            for jj in range(16):
                cols = slice(jj * GTW, (jj + 1) * GTW)
                wait("scalar", "mm", sc["dt_mm"])
                dma("scalar", dt_sb[:, :], ar_out[0:64, cols], sem="stl0")
                dtld = C["stl0"]
                wait("tensor", "stl0", dtld)
                for d2 in range(2):
                    for nt in range(2):
                        bp = nt
                        bk = 6 + bp
                        wait("tensor", "aq", sc["b67"][bp])
                        mmv = mm_group([lambda t, d2=d2, nt=nt, bk=bk: t.matmul(
                            ps[bk][:, :], dtw_s[:, d2 * 128:(d2 + 1) * 128],
                            dt_sb[:, nt * 512:(nt + 1) * 512],
                            start=True, stop=True)])
                        wait("scalar", "mm", mmv)
                        wait("scalar", "smk", C["smk"])
                        sc["b67"][bp] = aop(lambda s, d2=d2, bk=bk: s.activation(
                            out=es_t[:, :], in_=ps[bk][:, :], func=AF.Exp,
                            bias=dtb_s[:, d2:d2 + 1]), sem="aq")
                        adr()
                        aop(lambda s: s.activation(
                            out=dstage[:, :], in_=es_t[:, :], func=AF.Ln,
                            bias=cst_s[:, 1:2]))
                        adr()
                        dma("scalar",
                            d_delta[d2 * 128:(d2 + 1) * 128,
                                    jj * GTW + nt * 512:jj * GTW + (nt + 1) * 512],
                            dstage[:, :], sem="smk")
                sc["dt_mm"] = C["mm"]

        def emit_scan_tile(j):
            par = j % 2
            st0 = (j == 0 or j == 8)
            last_tile = (j == 7 or j == 15)
            if j >= 1:
                wait("scalar", "vv", sc["fin_vv"][j - 1])
            cols_j = slice(j * GTW, (j + 1) * GTW)
            for mc in range(2):
                dma("scalar", sz2_t[mc], d_silz[mc * 128:(mc + 1) * 128, cols_j],
                    sem="stl%d" % par)
            sc["sz_ld"] = sc.get("sz_ld", {})
            sc["sz_ld"][j] = C["stl%d" % par]
            wait("vector", "stl%d" % par, sc["tl_ld"][j])
            wait("scalar", "stl%d" % par, sc["tl_ld"][j])
            for d2 in range(2):
                vop(lambda v, d2=d2, par=par: v.tensor_tensor(
                    out=du_t[d2], in0=delta[par][d2], in1=xc2_t[par][d2],
                    op=ALU.mult))
            vdr()
            sc["du_vv"][j] = vop(lambda v: v.nop(), sem="vv")
            if dbg and j == 0:
                wait("scalar", "vv", sc["du_vv"][j])
                for d2 in range(2):
                    dma("scalar", dbg_sc[0 + d2, :, :], delta[0][d2], sem="smk")
                    dma("scalar", dbg_sc[2 + d2, :, :], du_t[d2], sem="smk")
            for n in range(NST):
                nb = n % 4
                for d2 in range(2):
                    ab = (2 * n + d2) % 4
                    if (j, n - 2, d2) in sc["scan_vv"]:
                        wait("scalar", "vv", sc["scan_vv"][(j, n - 2, d2)])
                    elif n < 2 and (j - 1, NST - 2 + n, d2) in sc["scan_vv"]:
                        wait("scalar", "vv", sc["scan_vv"][(j - 1, NST - 2 + n, d2)])
                    sc["dA_aq"][(j, n, d2)] = aop(
                        lambda s, n=n, d2=d2, ab=ab, par=par: s.activation(
                            out=dA_pp[ab], in_=delta[par][d2], func=AF.Exp,
                            scale=a_s[:, d2 * 16 + n:d2 * 16 + n + 1]), sem="aq")
                wait("vector", "sb%d" % nb, sc["bc_ld"][(j, n)])
                for d2 in range(2):
                    vop(lambda v, d2=d2, nb=nb: v.tensor_tensor(
                        out=dBu_t[d2], in0=du_t[d2], in1=Bnb[nb], op=ALU.mult))
                vdr()
                for d2 in range(2):
                    ab = (2 * n + d2) % 4
                    wait("vector", "aq", sc["dA_aq"][(j, n, d2)])
                    if (j, n - 1, d2) in sc["state_gg"]:
                        wait("vector", "gg", sc["state_gg"][(j, n - 1, d2)])
                    elif (j, n - 1, d2) in sc["hC_gg"]:
                        wait("vector", "gg", sc["hC_gg"][(j, n - 1, d2)])
                    elif n == 0 and (j - 1, NST - 1, d2) in sc["state_gg"]:
                        wait("vector", "gg", sc["state_gg"][(j - 1, NST - 1, d2)])
                    elif n == 0 and (j - 1, NST - 1, d2) in sc["hC_gg"]:
                        wait("vector", "gg", sc["hC_gg"][(j - 1, NST - 1, d2)])
                    if st0:
                        sc["scan_vv"][(j, n, d2)] = vop(
                            lambda v, d2=d2, ab=ab: v.tensor_tensor_scan(
                                out=hs_t[d2], data0=dA_pp[ab], data1=dBu_t[d2],
                                initial=0.0, op0=ALU.mult, op1=ALU.add), sem="vv")
                    else:
                        col = state_s[:, d2 * 16 + n:d2 * 16 + n + 1]
                        sc["scan_vv"][(j, n, d2)] = vop(
                            lambda v, d2=d2, col=col, ab=ab: v.tensor_tensor_scan(
                                out=hs_t[d2], data0=dA_pp[ab], data1=dBu_t[d2],
                                initial=col, op0=ALU.mult, op1=ALU.add), sem="vv")
                if n >= 1:
                    for d2 in range(2):
                        wait("vector", "gg", sc["hC_gg"][(j, n - 1, d2)])
                        if n == 1:
                            sc["yacc_vv"][(j, 0, d2)] = vop(
                                lambda v, d2=d2: v.tensor_copy(
                                    y_t[d2], hC_t[d2]), sem="vv")
                        else:
                            sc["yacc_vv"][(j, n - 1, d2)] = vop(
                                lambda v, d2=d2: v.tensor_tensor(
                                    out=y_t[d2], in0=y_t[d2], in1=hC_t[d2],
                                    op=ALU.add), sem="vv")
                for d2 in range(2):
                    wait("gpsimd", "vv", sc["scan_vv"][(j, n, d2)])
                    if (j, n - 1, d2) in sc["yacc_vv"]:
                        wait("gpsimd", "vv", sc["yacc_vv"][(j, n - 1, d2)])
                    sc["hC_gg"][(j, n, d2)] = gop(
                        lambda g, d2=d2, nb=nb: g.tensor_tensor(
                            out=hC_t[d2], in0=hs_t[d2], in1=Cnb[nb],
                            op=ALU.mult), sem="gg")
                    if not last_tile:
                        sc["state_gg"][(j, n, d2)] = gop(
                            lambda g, n=n, d2=d2: g.tensor_copy(
                                state_s[:, d2 * 16 + n:d2 * 16 + n + 1],
                                hs_t[d2][:, 1023:1024]), sem="gg")
                if n <= NST - 3:
                    if n >= 2:
                        emit_bc_load(j, n + 2, j, n - 2)
                    else:
                        emit_bc_load(j, n + 2, j - 1, n + 14)
            for d2 in range(2):
                wait("vector", "gg", sc["hC_gg"][(j, NST - 1, d2)])
                sc["yacc_vv"][(j, NST - 1, d2)] = vop(
                    lambda v, d2=d2: v.tensor_tensor(
                        out=y_t[d2], in0=y_t[d2], in1=hC_t[d2], op=ALU.add),
                    sem="vv")
            vdr()
            if j + 1 < 16:
                emit_bc_load(j + 1, 0, j, NST - 4)
                emit_bc_load(j + 1, 1, j, NST - 3)
            if j >= 1:
                wait("vector", "mm", sc["yop_rd_mm"].get(j - 1, 0))
            wait("vector", "stl%d" % par, sc["sz_ld"][j])
            for d2 in range(2):
                if not last_tile:
                    wait("vector", "gg", sc["state_gg"][(j, NST - 1, d2)])
                wait("vector", "gg", sc["hC_gg"][(j, NST - 1, d2)])
                vop(lambda v, d2=d2, par=par: v.scalar_tensor_tensor(
                    out=hs_t[d2], in0=xc2_t[par][d2], scalar=dp_s[:, d2:d2 + 1],
                    in1=y_t[d2], op0=ALU.mult, op1=ALU.add))
            vdr()
            for d2 in range(2):
                vop(lambda v, d2=d2: v.tensor_tensor(
                    out=yop_t[d2], in0=hs_t[d2], in1=sz2_t[d2], op=ALU.mult))
            vdr()
            sc["fin_vv"][j] = vop(lambda v: v.nop(), sem="vv")
            if dbg and j == 0:
                wait("scalar", "vv", sc["fin_vv"][j])
                for d2 in range(2):
                    dma("scalar", dbg_sc[4 + d2, :, :], yop_t[d2], sem="smk")
                    dma("scalar", dbg_sc[6 + d2, :, :], xc2_t[par][d2], sem="smk")

        def emit_bc_load(jn, n, gj, gn):
            nb = n % 4
            if (gj, gn, 1) in sc["scan_vv"]:
                wait("scalar", "vv", sc["scan_vv"][(gj, gn, 1)])
            if (gj, gn, 1) in sc["hC_gg"]:
                wait("scalar", "gg", sc["hC_gg"][(gj, gn, 1)])
            cols = slice(jn * GTW, (jn + 1) * GTW)
            dma("scalar", Bnb[nb],
                d_bc.ap()[n:n + 1, cols].partition_broadcast(128)[:, 0, :],
                sem="sb%d" % nb)
            dma("scalar", Cnb[nb],
                d_bc.ap()[16 + n:17 + n, cols].partition_broadcast(128)[:, 0, :],
                sem="sb%d" % nb)
            sc["bc_ld"][(jn, n)] = C["sb%d" % nb]

        def emit_bc_preload(j):
            wait("scalar", "gbc", C["gbc"])
            for n in range(2):
                emit_bc_load(j, n, -1, 0)

        def emit_out_proj(j):
            wait("tensor", "vv", sc["fin_vv"][j])
            shard = j // 2
            lc0 = (j % 2) * 1024
            for mc8 in range(8):
                for nt in range(2):
                    bp = (mc8 * 2 + nt) % 2
                    bk = 6 + bp
                    wait("tensor", "aq", sc["b67"][bp])
                    mms = []
                    for d2 in range(2):
                        mms.append(lambda t, d2=d2, mc8=mc8, nt=nt, bk=bk:
                                   t.matmul(
                                       ps[bk][:, :],
                                       opw_s[:, d2 * D + mc8 * 128:
                                             d2 * D + (mc8 + 1) * 128],
                                       yop_t[d2][:, nt * 512:(nt + 1) * 512],
                                       start=(d2 == 0), stop=(d2 == 1)))
                    mmv = mm_group(mms)
                    wait("scalar", "mm", mmv)
                    wait("scalar", "srs%d" % bp, C["srs%d" % bp])
                    ev = aop(lambda s, bk=bk, bp=bp: s.copy(
                        po2[bp][:, :], ps[bk][:, :]), sem="aq")
                    sc["b67"][bp] = ev
                    wait("scalar", "aq", ev)
                    dma("scalar",
                        rs_in[shard, mc8 * 128:(mc8 + 1) * 128,
                              lc0 + nt * 512:lc0 + (nt + 1) * 512],
                        po2[bp][:, :], sem="srs%d" % bp)
            sc["yop_rd_mm"][j] = C["mm"]

        scan_at = {}
        for jj in range(16):
            scan_at.setdefault(9 + jj * 11 // 8, []).append(jj)

        for cyc in range(32):
            emit_moe_chunk(cyc)
            if 4 <= cyc <= 7:
                for k in range(4):
                    emit_p1_gt((cyc - 4) * 4 + k)
                if cyc == 7:
                    wait("gpsimd", "sar", C["sar"])
                    wait("gpsimd", "sar1", C["sar1"])
                    em("gpsimd", lambda g: g.collective_compute(
                        "AllReduce", ALU.add, replica_groups=[list(range(NCORES))],
                        ins=[ar_in.ap().opt()],
                        outs=[ar_out.ap().opt()]).then_inc(S["cc"], 1))
                    bump("cc")
                    wait("gpsimd", "cc", 2)
                    dma("gpsimd", d_bc[:, :], ar_out[64:96, :], sem="gbc")
                    if dbg:
                        dma("gpsimd", dbg_ar[:, :], ar_out[:, :], sem="gbc")
            if cyc == 8:
                sc["b67"] = list(p1["bank_free"])
                emit_dt_burst()
                emit_scan_loads(0)
                emit_bc_preload(0)
                emit_scan_loads(1)
            for jj in scan_at.get(cyc, []):
                if jj >= 1:
                    emit_out_proj(jj - 1)
                emit_scan_tile(jj)
                emit_scan_loads(jj + 2)
        emit_out_proj(15)
        rs_total0 = C["srs0"]
        rs_total1 = C["srs1"]

        wait("gpsimd", "srs0", rs_total0)
        wait("gpsimd", "srs1", rs_total1)
        em("gpsimd", lambda g: g.collective_compute(
            "ReduceScatter", ALU.add, replica_groups=[list(range(NCORES))],
            ins=[rs_in.ap().opt()], outs=[rs_out.ap().opt()]).then_inc(S["cc"], 1))
        bump("cc")

        wait("sync", "cc", 3)
        wait("sync", "mm", moe_state["mm_total"])
        if dbg:
            dma("sync", dbg_rs[:, :], rs_out[:, :], sem="sf0")
            dma("sync", dbg_rsp[:, :], rs_in[0, :, :], sem="sf0")
            dma("sync", dbg_bc[:, :], d_bc[:, :], sem="sf0")
            wait("sync", "vv", C["vv"])
            dma("sync", dbg_moe[:, :], out_acc[:, :], sem="sf0")
        dma("sync", rstd_b, d_rows.ap()[0:1, :].partition_broadcast(128)[:, 0, :],
            sem="sf0")
        dma("sync", nmr_b, d_rows.ap()[1:2, :].partition_broadcast(128)[:, 0, :],
            sem="sf0")
        p3b_ld = C["sf0"]
        last_acc_vv = C["vv"]
        f_ld = [p3b_ld, 0]
        f_free = [0, 0]
        f_st = [0, 0]
        i_t = 0
        for tt in range(4):
            for dc in range(8):
                bu = i_t % 2
                rsl = slice(dc * 128, (dc + 1) * 128)
                csl = slice(tt * 512, (tt + 1) * 512)
                wait("sync", "vv", f_free[bu])
                dma("sync", f_rs[bu], rs_out[rsl, csl], sem="sf%d" % bu)
                dma("sync", f_x[bu], xT[rsl, csl], sem="sf%d" % bu)
                f_ld[bu] = C["sf%d" % bu]
                asl = slice(dc * 2048 + tt * 512, dc * 2048 + (tt + 1) * 512)
                wait("scalar", "vv", max(f_free[bu], last_acc_vv))
                g_aq = aop(lambda s, asl=asl, bu=bu: s.activation(
                    out=f_g[bu], in_=out_acc[:, asl], func=AF.Silu), sem="aq")
                wait("vector", "sf%d" % bu, max(f_ld[bu], f_st[bu]))
                wait("vector", "aq", g_aq)
                vop(lambda v, bu=bu, csl=csl: v.tensor_tensor(
                    out=f_s[bu], in0=f_x[bu], in1=rstd_b[:, csl], op=ALU.mult))
                vdr()
                vop(lambda v, bu=bu, csl=csl: v.tensor_tensor(
                    out=f_s[bu], in0=f_s[bu], in1=nmr_b[:, csl], op=ALU.add))
                vdr()
                vop(lambda v, bu=bu: v.tensor_tensor(
                    out=f_s[bu], in0=f_s[bu], in1=f_rs[bu], op=ALU.add))
                vdr()
                vop(lambda v, bu=bu: v.tensor_tensor(
                    out=f_s[bu], in0=f_s[bu], in1=f_g[bu], op=ALU.mult))
                vdr()
                o_vv = vop(lambda v, bu=bu: v.tensor_tensor(
                    out=f_s[bu], in0=f_s[bu], in1=f_x[bu], op=ALU.add), sem="vv")
                f_free[bu] = o_vv
                wait("sync", "vv", o_vv)
                dma("sync", out_d[rsl, csl], f_s[bu], sem="sf%d" % bu)
                f_st[bu] = C["sf%d" % bu]
                i_t += 1
        wait("sync", "sf0", C["sf0"])
        wait("sync", "sf1", C["sf1"])

        @block.tensor
        def _(t):
            for fn in prog["tensor"]:
                fn(t)

        @block.vector
        def _(v):
            for fn in prog["vector"]:
                fn(v)

        @block.scalar
        def _(s):
            for fn in prog["scalar"]:
                fn(s)

        @block.gpsimd
        def _(g):
            for fn in prog["gpsimd"]:
                fn(g)

        @block.sync
        def _(sy):
            for fn in prog["sync"]:
                fn(sy)

    return nc


_NC_CACHE = None


def _get_nc():
    global _NC_CACHE
    if _NC_CACHE is None:
        _NC_CACHE = build_nc()
    return _NC_CACHE


def make_in_maps(L0_attn_x, L0_attn_norm_w, L0_attn_in_proj_w, L0_attn_conv_w,
                 L0_attn_conv_b, L0_attn_x_proj_w, L0_attn_dt_proj_w,
                 L0_attn_dt_proj_b, L0_attn_A_log, L0_attn_Dp, L0_attn_out_proj_w,
                 L0_ffn_gate_w, L0_ffn_gate_b, L0_ffn_w1, L0_ffn_b1, L0_ffn_w2,
                 L0_ffn_b2):
    f = np.float32
    x = np.asarray(L0_attn_x, f)
    w1 = np.asarray(L0_ffn_w1, f)
    w1r = np.ascontiguousarray(
        w1.reshape(E, 32, 128, 8, 128).transpose(0, 4, 1, 3, 2)
        .reshape(E, 128, 32768))
    shared = {
        "ident": np.eye(128, dtype=f),
        "idx128": np.tile(np.arange(E, dtype=f), (128, 1)),
        "gwT": np.ascontiguousarray(np.asarray(L0_ffn_gate_w, f).T),
        "gb": np.asarray(L0_ffn_gate_b, f).reshape(1, E),
        "normw_r": np.ascontiguousarray(np.asarray(L0_attn_norm_w, f).reshape(8, 128).T),
        "b1T": np.ascontiguousarray(
            np.asarray(L0_ffn_b1, f).reshape(E, 32, 128).transpose(2, 0, 1).reshape(128, E * 32)),
        "b2T": np.ascontiguousarray(
            np.asarray(L0_ffn_b2, f).reshape(E, 8, 128).transpose(2, 0, 1).reshape(128, E * 8)),
        "w1r": w1r,
        "w2T": np.ascontiguousarray(np.asarray(L0_ffn_w2, f).transpose(0, 2, 1)),
    }
    ipw = np.asarray(L0_attn_in_proj_w, f)
    cw = np.asarray(L0_attn_conv_w, f)
    cb = np.asarray(L0_attn_conv_b, f)
    xpw = np.asarray(L0_attn_x_proj_w, f)
    dtp = np.asarray(L0_attn_dt_proj_w, f)
    dtb = np.asarray(L0_attn_dt_proj_b, f)
    alog = np.asarray(L0_attn_A_log, f)
    dpv = np.asarray(L0_attn_Dp, f)
    opw = np.asarray(L0_attn_out_proj_w, f)

    in_maps = []
    for c in range(NCORES):
        b_, q = c // 4, c % 4
        sl = slice(DSL * c, DSL * (c + 1))
        m = dict(shared)
        m["xT"] = np.ascontiguousarray(x[b_, q * TOK:(q + 1) * TOK, :].T)
        m["wxiT"] = np.ascontiguousarray(ipw[sl, :].T)
        m["wzT"] = np.ascontiguousarray(ipw[DI + DSL * c:DI + DSL * (c + 1), :].T)
        m["convw_r"] = np.ascontiguousarray(
            cw[sl].reshape(2, 128, 4).transpose(1, 0, 2).reshape(128, 8))
        m["convb_r"] = np.ascontiguousarray(cb[sl].reshape(2, 128).T)
        m["xpwT"] = np.ascontiguousarray(xpw[:, sl].T)
        m["dtw"] = np.ascontiguousarray(dtp[sl, :].T)
        m["dtb_r"] = np.ascontiguousarray(dtb[sl].reshape(2, 128).T)
        m["alog_r"] = np.ascontiguousarray(
            alog[sl].reshape(2, 128, NST).transpose(1, 0, 2).reshape(128, 32))
        m["dp_r"] = np.ascontiguousarray(dpv[sl].reshape(2, 128).T)
        m["opwT_sl"] = np.ascontiguousarray(opw[:, sl].T)
        in_maps.append(m)
    return in_maps


def kernel(**inputs):
    in_maps = make_in_maps(**inputs)
    nc = _get_nc()
    res = run_bass_kernel_spmd(nc, in_maps, core_ids=list(range(NCORES)))
    out = np.empty((B, L, D), dtype=np.float32)
    for c in range(NCORES):
        b_, q = c // 4, c % 4
        out[b_, q * TOK:(q + 1) * TOK, :] = res.results[c]["out"].T
    return out


# version 08
# speedup vs baseline: 1.0213x; 1.0213x over previous
import numpy as np
from contextlib import ExitStack

import concourse.bass as bass
import concourse.mybir as mybir
from concourse.bass_utils import run_bass_kernel_spmd

F32 = mybir.dt.float32
BF16 = mybir.dt.bfloat16
AF = mybir.ActivationFunctionType
ALU = mybir.AluOpType
AX = mybir.AxisListType

NCORES = 8
B, L, D = 2, 8192, 1024
DI, NST, DTR = 2048, 16, 64
E, HID = 8, 4096
TOK = 2048
GT = B * L
DSL = DI // NCORES
NGT = 16
GTW = GT // NGT
EPS = 1e-5
BIGNEG = 100.0


def build_nc(dbg=False):
    nc = bass.Bass("TRN2", target_bir_lowering=False, debug=False)

    def din(name, shape, dt=F32):
        return nc.dram_tensor(name, shape, dt, kind="ExternalInput")

    dkind = "ExternalOutput" if dbg else "Internal"

    xT = din("xT", [D, TOK])
    ident = din("ident", [128, 128])
    idx128 = din("idx128", [128, E])
    gwT = din("gwT", [D, E])
    gb = din("gb", [1, E])
    normw_r = din("normw_r", [128, 8])
    wxiT = din("wxiT", [D, DSL])
    wzT = din("wzT", [D, DSL])
    convw_r = din("convw_r", [128, 8])
    convb_r = din("convb_r", [128, 2])
    xpwT = din("xpwT", [DSL, 96])
    dtw = din("dtw", [DTR, DSL])
    dtb_r = din("dtb_r", [128, 2])
    alog_r = din("alog_r", [128, 32])
    dp_r = din("dp_r", [128, 2])
    opwT_sl = din("opwT_sl", [DSL, D])
    b1T = din("b1T", [128, E * 32])
    b2T = din("b2T", [128, E * 8])
    w1r = din("w1r", [E, 128, 32768])
    w2T = din("w2T", [E, HID, D])

    out_d = nc.dram_tensor("out", [D, TOK], F32, kind="ExternalOutput")

    d_xnbf = nc.dram_tensor("d_xnbf", [D, TOK], BF16, kind=dkind)
    d_m = nc.dram_tensor("d_m", [E, TOK], BF16, kind=dkind)
    d_pm = nc.dram_tensor("d_pm", [E, TOK], BF16, kind=dkind)
    d_rows = nc.dram_tensor("d_rows", [4, TOK], F32)
    ag1_in = nc.dram_tensor("ag1_in", [D, TOK], BF16)
    ag1_out = nc.dram_tensor("ag1_out", [NCORES * D, TOK], BF16)
    ar_in = nc.dram_tensor("ar_in", [96, GT], F32)
    ar_out = nc.dram_tensor("ar_out", [96, GT], F32)
    d_bc = nc.dram_tensor("d_bc", [32, GT], BF16)
    d_xc = nc.dram_tensor("d_xc", [DSL, GT], BF16, kind=dkind)
    d_delta = nc.dram_tensor("d_delta", [DSL, GT], BF16)
    d_silz = nc.dram_tensor("d_silz", [DSL, GT], BF16, kind=dkind)
    d_w1bf = nc.dram_tensor("d_w1bf", [E, 128, 32768], BF16)
    d_w2bf = nc.dram_tensor("d_w2bf", [E, HID, D], BF16)
    rs_in = nc.dram_tensor("rs_in", [NCORES, D, TOK], F32)
    rs_out = nc.dram_tensor("rs_out", [D, TOK], F32)
    if dbg:
        dbg_ar = nc.dram_tensor("dbg_ar", [96, GT], F32, kind="ExternalOutput")
        dbg_rs = nc.dram_tensor("dbg_rs", [D, TOK], F32, kind="ExternalOutput")
        dbg_moe = nc.dram_tensor("dbg_moe", [128, 16384], BF16, kind="ExternalOutput")
        dbg_rsp = nc.dram_tensor("dbg_rsp", [D, TOK], F32, kind="ExternalOutput")
        dbg_bc = nc.dram_tensor("dbg_bc", [32, GT], BF16, kind="ExternalOutput")
        dbg_sc = nc.dram_tensor("dbg_sc", [8, 128, 1024], BF16, kind="ExternalOutput")
        dbg_m2 = nc.dram_tensor("dbg_m2", [4, 128, 4096], BF16, kind="ExternalOutput")

    with ExitStack() as stk:
        EE = stk.enter_context

        A1 = EE(nc.sbuf_tensor("A1", [128, 12288], BF16))
        A2 = EE(nc.sbuf_tensor("A2", [128, 16384], BF16))
        A3 = EE(nc.sbuf_tensor("A3", [128, 16384], BF16))
        A4 = EE(nc.sbuf_tensor("A4", [128, 43008], BF16))
        AM = EE(nc.sbuf_tensor("AM", [128, 11264], BF16))
        AW2 = EE(nc.sbuf_tensor("AW2", [128, 2048], BF16))
        opw_s = EE(nc.sbuf_tensor("opw_s", [128, 2048], BF16))

        ones_s = EE(nc.sbuf_tensor("ones_s", [128, 1], F32))
        ident_s = EE(nc.sbuf_tensor("ident_s", [128, 128], F32))
        idx_s = EE(nc.sbuf_tensor("idx_s", [128, E], F32))
        gw_s = EE(nc.sbuf_tensor("gw_s", [128, 64], F32))
        gbb_s = EE(nc.sbuf_tensor("gbb_s", [128, E], F32))
        normw_s = EE(nc.sbuf_tensor("normw_s", [128, 8], F32))
        convw_s = EE(nc.sbuf_tensor("convw_s", [128, 8], F32))
        convb_s = EE(nc.sbuf_tensor("convb_s", [128, 2], F32))
        dtw_s = EE(nc.sbuf_tensor("dtw_s", [64, DSL], F32))
        dtb_s = EE(nc.sbuf_tensor("dtb_s", [128, 2], F32))
        alog_s = EE(nc.sbuf_tensor("alog_s", [128, 32], F32))
        a_s = EE(nc.sbuf_tensor("a_s", [128, 32], F32))
        dp_s = EE(nc.sbuf_tensor("dp_s", [128, 2], F32))
        b1_s = EE(nc.sbuf_tensor("b1_s", [128, E * 32], F32))
        b2_s = EE(nc.sbuf_tensor("b2_s", [128, E * 8], F32))
        state_s = EE(nc.sbuf_tensor("state_s", [128, 32], BF16))
        cst_s = EE(nc.sbuf_tensor("cst_s", [128, 2], F32))

        ps = [EE(nc.psum_tensor(f"psb{i}", [128, 512], F32)) for i in range(8)]

        sem_names = ["s0", "mm", "vv", "aq", "gg", "cc", "gw1p", "gw2p",
                     "gbc", "smk", "w1r0", "w1r1", "w1r2", "w1r3", "w1r4", "w1r5",
                     "w2r0", "w2r1", "w2r2", "w2r3", "xnb", "pmm",
                     "sr0", "sr1", "sar", "stl0", "stl1",
                     "sb0", "sb1", "sb2", "sb3", "srs0", "srs1",
                     "sf0", "sf1", "sxc", "gp1", "sar1", "ssz"]
        S = {}
        for name in sem_names:
            S[name] = EE(nc.semaphore(name))
        C = {k: 0 for k in S}

        block = EE(nc.Block())
        prog = {"tensor": [], "vector": [], "scalar": [], "gpsimd": [], "sync": []}

        def em(eng, fn):
            prog[eng].append(fn)

        def bump(name, by=1):
            C[name] += by
            return C[name]

        def dma(eng_name, out_ap, in_ap, sem="s0"):
            h = S[sem]
            t = bump(sem, 16)

            def fn(e, out_ap=out_ap, in_ap=in_ap, h=h):
                e.dma_start(out=out_ap, in_=in_ap).then_inc(h, 16)
            em(eng_name, fn)
            return t

        def wait(eng_name, sem, tgt):
            if tgt <= 0:
                return
            h = S[sem]

            def fn(e, h=h, tgt=tgt):
                e.wait_ge(h, tgt)
            em(eng_name, fn)

        def vop(fb, sem=None):
            if sem is None:
                em("vector", lambda v, f=fb: f(v))
                return None
            h = S[sem]
            t = bump(sem)

            def fn(v, f=fb, h=h):
                f(v).then_inc(h, 1)
            em("vector", fn)
            return t

        def aop(fb, sem=None):
            if sem is None:
                em("scalar", lambda s, f=fb: f(s))
                return None
            h = S[sem]
            t = bump(sem)

            def fn(s, f=fb, h=h):
                f(s).then_inc(h, 1)
            em("scalar", fn)
            return t

        def gop(fb, sem=None):
            if sem is None:
                em("gpsimd", lambda g, f=fb: f(g))
                return None
            h = S[sem]
            t = bump(sem)

            def fn(g, f=fb, h=h):
                f(g).then_inc(h, 1)
            em("gpsimd", fn)
            return t

        def vdr():
            em("vector", lambda v: v.drain())

        def adr():
            em("scalar", lambda s: s.drain())

        def mm_group(mms, sem="mm"):
            t = None
            for i, mb in enumerate(mms):
                if i == len(mms) - 1:
                    h = S[sem]
                    t = bump(sem)

                    def fn(tt_, f=mb, h=h):
                        f(tt_).then_inc(h, 1)
                    em("tensor", fn)
                else:
                    em("tensor", lambda tt_, f=mb: f(tt_))
            return t

        A1b, A2b, A3b, A4b = A1.ap(), A2.ap(), A3.ap(), A4.ap()
        AMb, AW2b = AM.ap(), AW2.ap()
        A1f = A1.ap().bitcast(F32)
        A2f = A2.ap().bitcast(F32)
        A3f = A3.ap().bitcast(F32)
        A4f = A4.ap().bitcast(F32)
        AMf = AM.ap().bitcast(F32)
        opwf = opw_s.ap().bitcast(F32)

        h_sb = A2b
        out_acc = A3b
        xe_sl = [AMb[:, 0:4096], AMb[:, 4096:8192]]
        pm_e = AMb[:, 8192:8704]
        m_e = AMb[:, 8704:9216]
        ystg = [AMb[:, 9216 + i * 512:9216 + (i + 1) * 512] for i in range(2)]
        t1_b = [AMb[:, 10240 + i * 512:10240 + (i + 1) * 512] for i in range(2)]
        w1sl = [A1b[:, i * 2048:(i + 1) * 2048] for i in range(6)]
        w2b = [AW2b[:, i * 512:(i + 1) * 512] for i in range(4)]

        delta = [A4b[:, d2 * 1024:(d2 + 1) * 1024] for d2 in range(2)]
        dA_pp = [A4f[:, 1024 + i * 1024:1024 + (i + 1) * 1024] for i in range(4)]
        y_t = [A4f[:, 5120 + d2 * 1024:5120 + (d2 + 1) * 1024] for d2 in range(2)]
        du_t = [A4b[:, 14336 + d2 * 1024:14336 + (d2 + 1) * 1024] for d2 in range(2)]
        dBu_t = [A4b[:, 16384 + d2 * 1024:16384 + (d2 + 1) * 1024] for d2 in range(2)]
        hs_t = [[A4b[:, 18432 + (np_ * 2 + d2) * 1024:
                     18432 + (np_ * 2 + d2 + 1) * 1024] for d2 in range(2)]
                for np_ in range(2)]
        hC_t = [A4b[:, 22528 + d2 * 1024:22528 + (d2 + 1) * 1024] for d2 in range(2)]
        Bnb = [A4b[:, 24576 + i * 1024:24576 + (i + 1) * 1024] for i in range(4)]
        Cnb = [A4b[:, 28672 + i * 1024:28672 + (i + 1) * 1024] for i in range(4)]
        xc2_t = [[A4b[:, 32768 + (par * 2 + d2) * 1024:
                      32768 + (par * 2 + d2 + 1) * 1024] for d2 in range(2)]
                 for par in range(2)]
        sz2_t = [A4b[:, 36864 + d2 * 1024:36864 + (d2 + 1) * 1024] for d2 in range(2)]
        yop_t = [A4b[:, 38912 + d2 * 1024:38912 + (d2 + 1) * 1024] for d2 in range(2)]
        po2 = [A4f[:, 20480 + i * 512:20480 + (i + 1) * 512] for i in range(2)]
        es_t = po2[0]
        dstage = A4b[:, 41984:42496]
        dt_sb = A4f[0:64, 16384:17408]

        def x_chunk(dc):
            if dc < 3:
                return A1f[:, dc * 2048:(dc + 1) * 2048]
            if dc < 7:
                return A2f[:, (dc - 3) * 2048:(dc - 2) * 2048]
            return A4f[:, 16384:18432]

        def xn_chunk(dc):
            return A4f[:, dc * 2048:(dc + 1) * 2048]

        rstd_b = AMf[:, 0:2048]
        nmr_b = AMf[:, 2048:4096]

        def rw(i):
            return A3f[0:1, i * 2048:(i + 1) * 2048]
        gscr = A3f[:, 0:1024]
        msb = A3b[0:16, 2048:4096]
        xnstg = [A3b[:, 4096 + i * 2048:4096 + (i + 1) * 2048] for i in range(2)]
        ag1stg = [A3b[:, 8192 + i * 2048:8192 + (i + 1) * 2048] for i in range(2)]

        rhs_t = [[A4b[:, (bu * 8 + k) * 1024:(bu * 8 + k + 1) * 1024]
                  for k in range(8)] for bu in range(2)]
        wxi_s = A4b[:, 16384:18432]
        wz_s = A4b[:, 18432:20480]
        xc_t = [A4b[:, 20480 + i * 1024:20480 + (i + 1) * 1024] for i in range(2)]
        sz_st = A4b[:, 22528:23552]
        xpw_s = A4b[:, 23552:23744]
        xi_ext = [A4f[:, 11872:12899], A4f[:, 12899:13926]]
        cv_a = A4f[:, 13926:14950]
        cv_b = A4f[:, 14950:15974]
        po_st = [opwf[:, 0:512], opwf[:, 512:1024]]

        f_rs = [A1f[:, i * 512:(i + 1) * 512] for i in range(2)]
        f_x = [A1f[:, 1024 + i * 512:1024 + (i + 1) * 512] for i in range(2)]
        f_g = [A1f[:, 2048 + i * 512:2048 + (i + 1) * 512] for i in range(2)]
        f_s = [A1f[:, 3072 + i * 512:3072 + (i + 1) * 512] for i in range(2)]

        vop(lambda v: v.memset(ones_s[:, :], 1.0))
        vop(lambda v: v.memset(state_s[:, :], 0.0))
        vop(lambda v: v.memset(cst_s[:, 0:1], EPS))
        cst_vv = vop(lambda v: v.memset(cst_s[:, 1:2], 1.0), sem="vv")
        wait("scalar", "vv", cst_vv)
        dma("sync", ident_s[:, :], ident[:, :])
        dma("sync", idx_s[:, :], idx128[:, :])
        for dc in range(8):
            dma("sync", gw_s[:, dc * 8:(dc + 1) * 8], gwT[dc * 128:(dc + 1) * 128, :])
        dma("sync", gbb_s[:, :], gb.ap().partition_broadcast(128)[:, 0, :])
        dma("sync", normw_s[:, :], normw_r[:, :])
        dma("sync", convw_s[:, :], convw_r[:, :])
        dma("sync", convb_s[:, :], convb_r[:, :])
        dma("sync", dtw_s[:, :], dtw[:, :])
        dma("sync", dtb_s[:, :], dtb_r[:, :])
        dma("sync", alog_s[:, :], alog_r[:, :])
        dma("sync", dp_s[:, :], dp_r[:, :])
        dma("sync", b1_s[:, :], b1T[:, :])
        dma("sync", b2_s[:, :], b2T[:, :])
        consts_ld = C["s0"]

        wait("scalar", "s0", consts_ld)
        a_aq = aop(lambda s: s.activation(out=a_s[:, :], in_=alog_s[:, :],
                                          func=AF.Exp), sem="aq")
        wait("vector", "aq", a_aq)
        vop(lambda v: v.tensor_scalar(out=a_s[:, :], in0=a_s[:, :], scalar1=-1.0,
                                      scalar2=None, op0=ALU.mult))
        vdr()

        w1p_mark = {}
        w2p_mark = {}

        def emit_precast(e_):
            for hf in range(2):
                dma("gpsimd", d_w1bf[e_, :, hf * 16384:(hf + 1) * 16384],
                    w1r[e_, :, hf * 16384:(hf + 1) * 16384], sem="gw1p")
            w1p_mark[e_] = C["gw1p"]
            for hc4 in range(0, 32, 8):
                dma("gpsimd", d_w2bf[e_, hc4 * 128:(hc4 + 8) * 128, :],
                    w2T[e_, hc4 * 128:(hc4 + 8) * 128, :], sem="gw2p")
            w2p_mark[e_] = C["gw2p"]

        emit_precast(0)
        emit_precast(1)

        for dc in range(8):
            dma("sync", x_chunk(dc), xT[dc * 128:(dc + 1) * 128, :])
        x_ld = C["s0"]

        wait("scalar", "s0", x_ld)
        for dc in range(8):
            aop(lambda s, dc=dc: s.activation(out=xn_chunk(dc), in_=x_chunk(dc),
                                              func=AF.Square))
        adr()
        sq_aq = aop(lambda s: s.nop(), sem="aq")

        wait("tensor", "s0", x_ld)
        wait("tensor", "aq", sq_aq)
        for nt in range(4):
            mms = []
            for dc in range(8):
                mms.append(lambda t, nt=nt, dc=dc: t.matmul(
                    ps[6][0:1, :], ones_s[:, :],
                    x_chunk(dc)[:, nt * 512:(nt + 1) * 512],
                    start=(dc == 0), stop=(dc == 7)))
            for dc in range(8):
                mms.append(lambda t, nt=nt, dc=dc: t.matmul(
                    ps[7][0:1, :], ones_s[:, :],
                    xn_chunk(dc)[:, nt * 512:(nt + 1) * 512],
                    start=(dc == 0), stop=(dc == 7)))
            mmv = mm_group(mms)
            wait("vector", "mm", mmv)
            vop(lambda v, nt=nt: v.tensor_copy(
                rw(0)[:, nt * 512:(nt + 1) * 512], ps[6][0:1, :]))
            bank_vv = vop(lambda v, nt=nt: v.tensor_copy(
                rw(1)[:, nt * 512:(nt + 1) * 512], ps[7][0:1, :]), sem="vv")
            wait("tensor", "vv", bank_vv)
        stats_mm = C["mm"]

        vop(lambda v: v.tensor_scalar(out=rw(0), in0=rw(0), scalar1=1.0 / D,
                                      scalar2=None, op0=ALU.mult))
        vdr()
        vop(lambda v: v.tensor_scalar(out=rw(1), in0=rw(1), scalar1=1.0 / D,
                                      scalar2=None, op0=ALU.mult))
        vdr()
        vop(lambda v: v.tensor_tensor(out=rw(2), in0=rw(0), in1=rw(0), op=ALU.mult))
        vdr()
        var_vv = vop(lambda v: v.tensor_tensor(out=rw(1), in0=rw(1), in1=rw(2),
                                               op=ALU.subtract), sem="vv")
        wait("scalar", "vv", var_vv)
        aop(lambda s: s.activation(out=rw(2), in_=rw(1), func=AF.Ln,
                                   bias=cst_s[0:1, 0:1]))
        adr()
        rstd_aq = aop(lambda s: s.activation(out=rw(2), in_=rw(2), func=AF.Exp,
                                             scale=-0.5), sem="aq")
        wait("vector", "aq", rstd_aq)
        vop(lambda v: v.tensor_tensor(out=rw(3), in0=rw(2), in1=rw(2), op=ALU.mult))
        vdr()
        t5_vv = vop(lambda v: v.tensor_tensor(out=rw(3), in0=rw(3), in1=rw(1),
                                              op=ALU.mult), sem="vv")
        wait("scalar", "vv", t5_vv)
        aop(lambda s: s.activation(out=rw(3), in_=rw(3), func=AF.Ln,
                                   bias=cst_s[0:1, 0:1]))
        adr()
        rrms_aq = aop(lambda s: s.activation(out=rw(3), in_=rw(3), func=AF.Exp,
                                             scale=-0.5), sem="aq")
        wait("vector", "aq", rrms_aq)
        nmr_vv = vop(lambda v: v.scalar_tensor_tensor(
            out=rw(0), in0=rw(0), scalar=-1.0, in1=rw(2), op0=ALU.mult,
            op1=ALU.mult), sem="vv")
        wait("sync", "vv", nmr_vv)
        dma("sync", d_rows[0:1, :], rw(2))
        dma("sync", d_rows[1:2, :], rw(0))
        dma("sync", d_rows[2:3, :], rw(3))
        rows_st = C["s0"]
        wait("sync", "s0", rows_st)
        dma("sync", rstd_b, d_rows.ap()[0:1, :].partition_broadcast(128)[:, 0, :])
        dma("sync", nmr_b, d_rows.ap()[1:2, :].partition_broadcast(128)[:, 0, :])
        bco_ld = C["s0"]

        wait("vector", "s0", bco_ld)
        wait("vector", "mm", stats_mm)
        for dc in range(8):
            vop(lambda v, dc=dc: v.tensor_tensor(
                out=xn_chunk(dc), in0=x_chunk(dc), in1=rstd_b, op=ALU.mult))
            vdr()
            vop(lambda v, dc=dc: v.tensor_tensor(
                out=xn_chunk(dc), in0=xn_chunk(dc), in1=nmr_b, op=ALU.add))
            vdr()
        xn_vv = vop(lambda v: v.nop(), sem="vv")

        wait("sync", "vv", xn_vv)
        dma("sync", rstd_b, d_rows.ap()[2:3, :].partition_broadcast(128)[:, 0, :])
        rrms_ld2 = C["s0"]
        wait("vector", "s0", rrms_ld2)
        rrms_b = rstd_b

        wait("vector", "s0", rows_st)
        stg_stq = [0, 0]
        for dc in range(8):
            bu = dc % 2
            if dc >= 2:
                wait("vector", "s0", C["s0"])
            vop(lambda v, dc=dc, bu=bu: v.tensor_copy(
                xnstg[bu][:, :], xn_chunk(dc)))
            st_vv = vop(lambda v, dc=dc, bu=bu: v.scalar_tensor_tensor(
                out=ag1stg[bu][:, :], in0=xn_chunk(dc),
                scalar=normw_s[:, dc:dc + 1], in1=rrms_b, op0=ALU.mult,
                op1=ALU.mult), sem="vv")
            wait("sync", "vv", st_vv)
            dma("sync", d_xnbf[dc * 128:(dc + 1) * 128, :], xnstg[bu][:, :])
            dma("sync", ag1_in[dc * 128:(dc + 1) * 128, :], ag1stg[bu][:, :])
            stg_stq[bu] = C["s0"]
        ag1in_st = C["s0"]

        wait("vector", "s0", rows_st)
        wait("tensor", "vv", xn_vv)
        for tc in range(16):
            mms = []
            for dc in range(8):
                mms.append(lambda t, tc=tc, dc=dc: t.matmul(
                    ps[5][:, 0:8],
                    xn_chunk(dc)[:, tc * 128:(tc + 1) * 128],
                    gw_s[:, dc * 8:(dc + 1) * 8], start=(dc == 0), stop=(dc == 7)))
            mmv = mm_group(mms)
            wait("vector", "mm", mmv)

            def gsl(a, b_):
                return gscr[:, a:b_]
            vop(lambda v: v.tensor_tensor(out=gsl(0, 8), in0=ps[5][:, 0:8],
                                          in1=gbb_s[:, :], op=ALU.add))
            vdr()
            bank_vv = vop(lambda v: v.nop(), sem="vv")
            wait("tensor", "vv", bank_vv)
            vop(lambda v: v.tensor_reduce(out=gsl(8, 9), in_=gsl(0, 8),
                                          axis=AX.X, op=ALU.max, negate=True))
            vdr()
            sm_vv = vop(lambda v: v.nop(), sem="vv")
            wait("scalar", "vv", sm_vv)
            e_aq = aop(lambda s: s.activation(out=gsl(0, 8), in_=gsl(0, 8),
                                              func=AF.Exp, bias=gsl(8, 9)),
                       sem="aq")
            wait("vector", "aq", e_aq)
            vop(lambda v: v.tensor_reduce(out=gsl(8, 9), in_=gsl(0, 8),
                                          axis=AX.X, op=ALU.add))
            vdr()
            vop(lambda v: v.reciprocal(gsl(9, 10), gsl(8, 9)))
            vdr()
            vop(lambda v: v.tensor_scalar(out=gsl(0, 8), in0=gsl(0, 8),
                                          scalar1=gsl(9, 10), scalar2=None,
                                          op0=ALU.mult))
            vdr()
            vop(lambda v: v.tensor_reduce(out=gsl(8, 9), in_=gsl(0, 8),
                                          axis=AX.X, op=ALU.max))
            vdr()
            vop(lambda v: v.tensor_scalar(out=gsl(10, 18), in0=gsl(0, 8),
                                          scalar1=gsl(8, 9), scalar2=None,
                                          op0=ALU.is_ge))
            vdr()
            vop(lambda v: v.scalar_tensor_tensor(
                out=gsl(18, 26), in0=gsl(10, 18), scalar=-BIGNEG, in1=gsl(0, 8),
                op0=ALU.mult, op1=ALU.add))
            vdr()
            vop(lambda v: v.tensor_reduce(out=gsl(8, 9), in_=gsl(18, 26),
                                          axis=AX.X, op=ALU.max))
            vdr()
            vop(lambda v: v.tensor_scalar(out=gsl(18, 26), in0=gsl(0, 8),
                                          scalar1=gsl(8, 9), scalar2=None,
                                          op0=ALU.is_ge))
            vdr()
            vop(lambda v: v.tensor_tensor(out=gsl(18, 26), in0=gsl(18, 26),
                                          in1=idx_s[:, :], op=ALU.mult))
            vdr()
            vop(lambda v: v.tensor_reduce(out=gsl(8, 9), in_=gsl(18, 26),
                                          axis=AX.X, op=ALU.max))
            vdr()
            vop(lambda v: v.tensor_scalar(out=gsl(26, 34), in0=idx_s[:, :],
                                          scalar1=gsl(8, 9), scalar2=None,
                                          op0=ALU.is_equal))
            vdr()
            vop(lambda v: v.tensor_tensor(out=gsl(34, 42), in0=gsl(0, 8),
                                          in1=gsl(26, 34), op=ALU.mult))
            vdr()
            mt_vv = vop(lambda v: v.nop(), sem="vv")
            wait("tensor", "vv", mt_vv)
            tp_mm = mm_group([lambda t: t.transpose(
                ps[4][0:16, 0:128], gsl(26, 42), ident_s[:, :])])
            wait("vector", "mm", tp_mm)
            bank_vv = vop(lambda v, tc=tc: v.tensor_copy(
                msb[:, tc * 128:(tc + 1) * 128], ps[4][0:16, 0:128]), sem="vv")
            wait("tensor", "vv", bank_vv)
        gate_vv = C["vv"]
        gate_mm = C["mm"]
        wait("scalar", "vv", gate_vv)
        dma("scalar", d_m[:, :], msb[0:8, :], sem="smk")
        dma("scalar", d_pm[:, :], msb[8:16, :], sem="smk")
        mask_st = C["smk"]

        wait("gpsimd", "s0", ag1in_st)
        em("gpsimd", lambda g: g.collective_compute(
            "AllGather", ALU.bypass, replica_groups=[list(range(NCORES))],
            ins=[ag1_in.ap().opt()], outs=[ag1_out.ap().opt()]).then_inc(S["cc"], 1))
        bump("cc")

        wait("vector", "smk", mask_st)
        outacc_zero_vv = vop(lambda v: v.memset(A3[:, :], 0.0), sem="vv")

        wait("sync", "smk", mask_st)
        for dc in range(8):
            dma("sync", xe_sl[0][:, dc * 512:(dc + 1) * 512],
                d_xnbf[dc * 128:(dc + 1) * 128, 0:512], sem="xnb")
        dma("sync", pm_e, d_pm.ap()[0:1, 0:512].partition_broadcast(128)[:, 0, :],
            sem="pmm")
        dma("sync", m_e, d_m.ap()[0:1, 0:512].partition_broadcast(128)[:, 0, :],
            sem="pmm")

        w1_slot_reader = [0] * 6
        w1_slot_loaded = [0] * 6
        w1_seq = []
        w1_emit = [0]
        w1_next_chunk = [0]

        def ensure_w1_entries(upto_cyc):
            while w1_next_chunk[0] <= min(upto_cyc, 31):
                ee = w1_next_chunk[0] // 4
                w1_seq.extend((ee, hg) for hg in range(16))
                w1_next_chunk[0] += 1
        w2_slot_reader = [0, 0, 0, 0]
        w2_slot_loaded = [0, 0, 0, 0]
        w2_i = 0
        xe_done_gg = [0, 0]
        hpass_done_mm = {}
        ypass_bank_add_vv = [outacc_zero_vv] * 4
        relu_aq = {}
        ystg_free_gg = [0, 0]
        t1_free_vv = [outacc_zero_vv, outacc_zero_vv]
        xnbf_loaded = {0: C["xnb"]}
        pmm_loaded = {0: C["pmm"]}
        moe_state = {"mm_total": 0, "last_t1_gg": 0}

        def w1_emit_loads(upto):
            while w1_emit[0] <= upto and w1_emit[0] < len(w1_seq):
                i = w1_emit[0]
                e_, hg = w1_seq[i]
                sl = i % 6
                wait("sync", "mm", w1_slot_reader[sl])
                if hg == 0:
                    wait("sync", "gw1p", w1p_mark[e_])
                w1_slot_loaded[sl] = dma(
                    "sync", w1sl[sl], d_w1bf[e_, :, hg * 2048:(hg + 1) * 2048],
                    sem="w1r%d" % sl)
                w1_emit[0] = i + 1

        def emit_moe_chunk(cyc):
            e_, tt = divmod(cyc, 4)
            nonlocal w2_i
            ttp = cyc % 2
            w1_base = 16 * cyc
            ensure_w1_entries(cyc + 1)
            wait("gpsimd", "xnb", xnbf_loaded[cyc])
            wait("gpsimd", "pmm", pmm_loaded[cyc])
            for dc in range(8):
                xe_done_gg[ttp] = gop(lambda g, dc=dc, ttp=ttp: g.tensor_tensor(
                    out=xe_sl[ttp][:, dc * 512:(dc + 1) * 512],
                    in0=xe_sl[ttp][:, dc * 512:(dc + 1) * 512], in1=pm_e,
                    op=ALU.mult), sem="gg")
            if cyc + 1 < 32:
                e2_, tt2 = divmod(cyc + 1, 4)
                tsl2 = slice(tt2 * 512, (tt2 + 1) * 512)
                if cyc >= 1:
                    wait("sync", "mm", hpass_done_mm[cyc - 1])
                for dc in range(8):
                    dma("sync", xe_sl[(cyc + 1) % 2][:, dc * 512:(dc + 1) * 512],
                        d_xnbf[dc * 128:(dc + 1) * 128, tsl2], sem="xnb")
                xnbf_loaded[cyc + 1] = C["xnb"]
            for hc in range(32):
                ghg = w1_base + hc // 2
                sl = ghg % 6
                bank = hc % 2
                if hc == 0:
                    wait("tensor", "gg", xe_done_gg[ttp])
                if hc % 2 == 0:
                    w1_emit_loads(ghg + 4)
                    wait("tensor", "w1r%d" % sl, w1_slot_loaded[sl])
                if (cyc, hc - 2) in relu_aq:
                    wait("tensor", "aq", relu_aq[(cyc, hc - 2)])
                elif hc < 2 and (cyc - 1, 30 + hc) in relu_aq:
                    wait("tensor", "aq", relu_aq[(cyc - 1, 30 + hc)])
                mms = []
                for kd in range(8):
                    mms.append(lambda t, hc=hc, kd=kd, bank=bank, sl=sl, ttp=ttp:
                               t.matmul(
                                   ps[bank][:, :],
                                   w1sl[sl][:, (hc % 2) * 1024 + kd * 128:
                                            (hc % 2) * 1024 + (kd + 1) * 128],
                                   xe_sl[ttp][:, kd * 512:(kd + 1) * 512],
                                   start=(kd == 0), stop=(kd == 7)))
                mmv = mm_group(mms)
                if hc % 2 == 1:
                    w1_slot_reader[sl] = mmv
                wait("scalar", "mm", mmv)
                relu_aq[(cyc, hc)] = aop(
                    lambda s, hc=hc, e_=e_, bank=bank: s.activation(
                        out=h_sb[:, hc * 512:(hc + 1) * 512], in_=ps[bank][:, :],
                        func=AF.Relu, bias=b1_s[:, e_ * 32 + hc:e_ * 32 + hc + 1]),
                    sem="aq")
            hpass_done_mm[cyc] = C["mm"]
            h_done = relu_aq[(cyc, 31)]
            if dbg and cyc in (0, 3):
                wait("scalar", "gg", xe_done_gg[ttp])
                dma("scalar", dbg_m2[0 + (cyc == 3), :, :], xe_sl[ttp], sem="smk")
                wait("scalar", "aq", h_done)
                dma("scalar", dbg_m2[2 + (cyc == 3), :, :], h_sb[:, 0:4096], sem="smk")
            for dcg in range(2):
                for dc4 in range(4):
                    wait("tensor", "vv", ypass_bank_add_vv[dc4])
                for hc in range(32):
                    wi = w2_i % 4
                    wait("sync", "mm", w2_slot_reader[wi])
                    if hc == 0 and dcg == 0 and tt == 0:
                        wait("sync", "gw2p", w2p_mark[e_])
                    w2_slot_loaded[wi] = dma(
                        "sync", w2b[wi],
                        d_w2bf[e_, hc * 128:(hc + 1) * 128,
                               dcg * 512:(dcg + 1) * 512], sem="w2r%d" % wi)
                    w2_i += 1
                    wait("tensor", "w2r%d" % wi, w2_slot_loaded[wi])
                    if hc == 0:
                        wait("tensor", "aq", h_done)
                    mms = []
                    for dc4 in range(4):
                        mms.append(lambda t, wi=wi, dc4=dc4, hc=hc: t.matmul(
                            ps[2 + dc4][:, :],
                            w2b[wi][:, dc4 * 128:(dc4 + 1) * 128],
                            h_sb[:, hc * 512:(hc + 1) * 512],
                            start=(hc == 0), stop=(hc == 31)))
                    mmv = mm_group(mms)
                    w2_slot_reader[wi] = mmv
                for dc4 in range(4):
                    dc = dcg * 4 + dc4
                    pp = dc4 % 2
                    wait("scalar", "mm", mmv)
                    wait("scalar", "gg", ystg_free_gg[pp])
                    ev_aq = aop(lambda s, e_=e_, dc=dc, dc4=dc4, pp=pp:
                                s.activation(
                                    out=ystg[pp][:, :], in_=ps[2 + dc4][:, :],
                                    func=AF.Identity,
                                    bias=b2_s[:, e_ * 8 + dc:e_ * 8 + dc + 1]),
                                sem="aq")
                    wait("gpsimd", "aq", ev_aq)
                    wait("gpsimd", "vv", t1_free_vv[pp])
                    ystg_free_gg[pp] = gop(lambda g, pp=pp: g.tensor_tensor(
                        out=t1_b[pp][:, :], in0=ystg[pp][:, :], in1=m_e,
                        op=ALU.mult), sem="gg")
                    wait("vector", "gg", ystg_free_gg[pp])
                    osl = slice(dc * 2048 + tt * 512, dc * 2048 + (tt + 1) * 512)
                    t1_free_vv[pp] = vop(lambda v, osl=osl, pp=pp: v.tensor_tensor(
                        out=out_acc[:, osl], in0=out_acc[:, osl],
                        in1=t1_b[pp][:, :], op=ALU.add), sem="vv")
                    ypass_bank_add_vv[dc4] = t1_free_vv[pp]
            w1_emit_loads(w1_base + 19)
            moe_state["mm_total"] = C["mm"]
            moe_state["last_t1_gg"] = max(ystg_free_gg)
            if cyc + 1 < 32:
                e2_, tt2 = divmod(cyc + 1, 4)
                tsl2 = slice(tt2 * 512, (tt2 + 1) * 512)
                wait("sync", "gg", moe_state["last_t1_gg"])
                dma("sync", pm_e,
                    d_pm.ap()[e2_:e2_ + 1, tsl2].partition_broadcast(128)[:, 0, :],
                    sem="pmm")
                dma("sync", m_e,
                    d_m.ap()[e2_:e2_ + 1, tsl2].partition_broadcast(128)[:, 0, :],
                    sem="pmm")
                pmm_loaded[cyc + 1] = C["pmm"]
            if tt == 1 and e_ + 2 < E:
                emit_precast(e_ + 2)

        p1 = {
            "rhs_ld": [0, 0], "rhs_free": [0, 0], "bank_free": [0, 0],
            "xi_free": 0, "sz_free": 0, "first": True,
        }

        def emit_p1_gt(gt):
            bu = gt % 2
            rank = gt // 2
            lc0 = (gt % 2) * 1024
            if p1["first"]:
                p1["first"] = False
                wait("gpsimd", "mm", gate_mm)
                for dc in range(8):
                    dma("gpsimd", wxi_s[:, dc * DSL:(dc + 1) * DSL],
                        wxiT[dc * 128:(dc + 1) * 128, :], sem="gp1")
                    dma("gpsimd", wz_s[:, dc * DSL:(dc + 1) * DSL],
                        wzT[dc * 128:(dc + 1) * 128, :], sem="gp1")
                for kc in range(2):
                    dma("gpsimd", xpw_s[:, kc * 96:(kc + 1) * 96],
                        xpwT[kc * 128:(kc + 1) * 128, :], sem="gp1")
                p1["w_ld"] = C["gp1"]
                wait("sync", "cc", 1)
                wait("tensor", "gp1", p1["w_ld"])
                wait("vector", "gp1", p1["w_ld"])
                vop(lambda v: v.memset(xi_ext[0][:, 0:3], 0.0))
                vop(lambda v: v.memset(xi_ext[1][:, 0:3], 0.0))
                vdr()
            wait("sync", "mm", p1["rhs_free"][bu])
            for k in range(8):
                dma("sync", rhs_t[bu][k],
                    ag1_out[rank * D + k * 128:rank * D + (k + 1) * 128,
                            lc0:lc0 + 1024], sem="sr%d" % bu)
            p1["rhs_ld"][bu] = C["sr%d" % bu]
            wait("scalar", "vv", p1["xi_free"])
            for mg in range(4):
                w_s = wxi_s if mg < 2 else wz_s
                mc = mg % 2
                for nt in range(2):
                    bp = (mg * 2 + nt) % 2
                    bk = 6 + bp
                    wait("tensor", "aq", p1["bank_free"][bp])
                    if mg == 0 and nt == 0:
                        wait("tensor", "sr%d" % bu, p1["rhs_ld"][bu])
                    mms = []
                    for kd in range(8):
                        mms.append(lambda t, w_s=w_s, mc=mc, kd=kd, nt=nt, bk=bk,
                                   bu=bu: t.matmul(
                                       ps[bk][:, :],
                                       w_s[:, kd * DSL + mc * 128:
                                           kd * DSL + (mc + 1) * 128],
                                       rhs_t[bu][kd][:, nt * 512:(nt + 1) * 512],
                                       start=(kd == 0), stop=(kd == 7)))
                    mmv = mm_group(mms)
                    wait("scalar", "mm", mmv)
                    if mg < 2:
                        p1["bank_free"][bp] = aop(
                            lambda s, mc=mc, nt=nt, bk=bk: s.copy(
                                xi_ext[mc][:, 3 + nt * 512:3 + (nt + 1) * 512],
                                ps[bk][:, :]), sem="aq")
                    else:
                        if nt == 0:
                            wait("scalar", "ssz", C["ssz"])
                        p1["bank_free"][bp] = aop(
                            lambda s, nt=nt, bk=bk: s.activation(
                                out=sz_st[:, nt * 512:(nt + 1) * 512],
                                in_=ps[bk][:, :], func=AF.Silu), sem="aq")
                if mg >= 2:
                    wait("scalar", "aq", C["aq"])
                    dma("scalar",
                        d_silz[mc * 128:(mc + 1) * 128, gt * GTW:(gt + 1) * GTW],
                        sz_st[:, 0:1024], sem="ssz")
            xi_aq = C["aq"]
            p1["rhs_free"][bu] = C["mm"]
            for mc in range(2):
                wait("vector", "aq", xi_aq)
                src = xi_ext[mc]
                vop(lambda v, src=src, mc=mc: v.tensor_scalar(
                    out=cv_a, in0=src[:, 0:1024],
                    scalar1=convw_s[:, mc * 4:mc * 4 + 1],
                    scalar2=None, op0=ALU.mult))
                vdr()
                for k in range(1, 4):
                    a_buf, b_buf = (cv_a, cv_b) if k % 2 == 1 else (cv_b, cv_a)
                    vop(lambda v, src=src, mc=mc, k=k, a_buf=a_buf, b_buf=b_buf:
                        v.scalar_tensor_tensor(
                            out=b_buf, in0=src[:, k:k + 1024],
                            scalar=convw_s[:, mc * 4 + k:mc * 4 + k + 1],
                            in1=a_buf, op0=ALU.mult, op1=ALU.add))
                    vdr()
                if gt != NGT - 1:
                    if gt == 7:
                        vop(lambda v, src=src: v.memset(src[:, 0:3], 0.0))
                    else:
                        vop(lambda v, src=src: v.tensor_copy(src[:, 0:3],
                                                             src[:, 1024:1027]))
                vdr()
                cv_vv = vop(lambda v: v.nop(), sem="vv")
                wait("scalar", "vv", cv_vv)
                wait("scalar", "sxc", C["sxc"])
                xc_aq = aop(lambda s, mc=mc: s.activation(
                    out=xc_t[mc][:, :], in_=cv_b, func=AF.Silu,
                    bias=convb_s[:, mc:mc + 1]), sem="aq")
                wait("scalar", "aq", xc_aq)
                dma("scalar", d_xc[mc * 128:(mc + 1) * 128, gt * GTW:(gt + 1) * GTW],
                    xc_t[mc][:, :], sem="sxc")
                wait("vector", "aq", xc_aq)
            p1["xi_free"] = C["vv"]
            wait("tensor", "aq", C["aq"])
            for nt in range(2):
                bk = 6 + nt
                sarn = "sar" if nt == 0 else "sar1"
                wait("tensor", "aq", p1["bank_free"][nt])
                mms = []
                for mc in range(2):
                    mms.append(lambda t, mc=mc, nt=nt, bk=bk: t.matmul(
                        ps[bk][0:96, :], xpw_s[:, mc * 96:(mc + 1) * 96],
                        xc_t[mc][:, nt * 512:(nt + 1) * 512],
                        start=(mc == 0), stop=(mc == 1)))
                mmv = mm_group(mms)
                wait("scalar", "mm", mmv)
                wait("scalar", sarn, C[sarn])
                p1["bank_free"][nt] = aop(lambda s, bk=bk, nt=nt: s.copy(
                    po_st[nt][0:96, :], ps[bk][0:96, :]), sem="aq")
                wait("scalar", "aq", p1["bank_free"][nt])
                dma("scalar",
                    ar_in[:, gt * GTW + nt * 512:gt * GTW + (nt + 1) * 512],
                    po_st[nt][0:96, :], sem=sarn)

        sc = {
            "dt_mm": 0, "tl_ld": {}, "du_vv": {}, "dl_ld": {},
            "dA_aq": {}, "scan_vv": {}, "state_gg": {}, "hC_gg": {},
            "yacc_vv": {}, "fin_vv": {}, "bc_ld": {},
            "yop_rd_mm": {}, "b67": [0, 0], "opw_ld": 0,
        }

        def emit_scan_loads(j):
            if j >= 16:
                return
            par = j % 2
            cols = slice(j * GTW, (j + 1) * GTW)
            if j == 0:
                wait("scalar", "smk", C["smk"])
            if j >= 2:
                wait("scalar", "vv", sc["fin_vv"][j - 2])
            for mc in range(2):
                dma("scalar", xc2_t[par][mc], d_xc[mc * 128:(mc + 1) * 128, cols],
                    sem="stl%d" % par)
            sc["tl_ld"][j] = C["stl%d" % par]

        def emit_dt_burst():
            wait("scalar", "cc", 2)
            wait("scalar", "sxc", C["sxc"])
            wait("scalar", "ssz", C["ssz"])
            dma("gpsimd", opw_s[:, 0:D], opwT_sl[0:128, :], sem="gp1")
            dma("gpsimd", opw_s[:, D:2 * D], opwT_sl[128:256, :], sem="gp1")
            sc["opw_ld"] = C["gp1"]
            wait("tensor", "gp1", sc["opw_ld"])
            for jj in range(16):
                cols = slice(jj * GTW, (jj + 1) * GTW)
                wait("scalar", "mm", sc["dt_mm"])
                dma("scalar", dt_sb[:, :], ar_out[0:64, cols], sem="stl0")
                dtld = C["stl0"]
                wait("tensor", "stl0", dtld)
                for d2 in range(2):
                    for nt in range(2):
                        bp = nt
                        bk = 6 + bp
                        wait("tensor", "aq", sc["b67"][bp])
                        mmv = mm_group([lambda t, d2=d2, nt=nt, bk=bk: t.matmul(
                            ps[bk][:, :], dtw_s[:, d2 * 128:(d2 + 1) * 128],
                            dt_sb[:, nt * 512:(nt + 1) * 512],
                            start=True, stop=True)])
                        wait("scalar", "mm", mmv)
                        wait("scalar", "smk", C["smk"])
                        sc["b67"][bp] = aop(lambda s, d2=d2, bk=bk: s.activation(
                            out=es_t[:, :], in_=ps[bk][:, :], func=AF.Exp,
                            bias=dtb_s[:, d2:d2 + 1]), sem="aq")
                        adr()
                        aop(lambda s: s.activation(
                            out=dstage[:, :], in_=es_t[:, :], func=AF.Ln,
                            bias=cst_s[:, 1:2]))
                        adr()
                        dma("scalar",
                            d_delta[d2 * 128:(d2 + 1) * 128,
                                    jj * GTW + nt * 512:jj * GTW + (nt + 1) * 512],
                            dstage[:, :], sem="smk")
                sc["dt_mm"] = C["mm"]

        def emit_scan_tile(j):
            par = j % 2
            st0 = (j == 0 or j == 8)
            last_tile = (j == 7 or j == 15)
            if j >= 1:
                wait("scalar", "vv", sc["fin_vv"][j - 1])
            cols_j = slice(j * GTW, (j + 1) * GTW)
            for mc in range(2):
                dma("scalar", sz2_t[mc], d_silz[mc * 128:(mc + 1) * 128, cols_j],
                    sem="stl%d" % par)
            sc["sz_ld"] = sc.get("sz_ld", {})
            sc["sz_ld"][j] = C["stl%d" % par]
            tl = max(sc["tl_ld"][j], sc["dl_ld"][j])
            wait("vector", "stl%d" % par, tl)
            wait("scalar", "stl%d" % par, tl)
            for d2 in range(2):
                vop(lambda v, d2=d2, par=par: v.tensor_tensor(
                    out=du_t[d2], in0=delta[d2], in1=xc2_t[par][d2],
                    op=ALU.mult))
            vdr()
            sc["du_vv"][j] = vop(lambda v: v.nop(), sem="vv")
            if dbg and j == 0:
                wait("scalar", "vv", sc["du_vv"][j])
                for d2 in range(2):
                    dma("scalar", dbg_sc[0 + d2, :, :], delta[d2], sem="smk")
                    dma("scalar", dbg_sc[2 + d2, :, :], du_t[d2], sem="smk")
            for n in range(NST):
                nb = n % 4
                for d2 in range(2):
                    ab = (2 * n + d2) % 4
                    if (j, n - 2, d2) in sc["scan_vv"]:
                        wait("scalar", "vv", sc["scan_vv"][(j, n - 2, d2)])
                    elif n < 2 and (j - 1, NST - 2 + n, d2) in sc["scan_vv"]:
                        wait("scalar", "vv", sc["scan_vv"][(j - 1, NST - 2 + n, d2)])
                    sc["dA_aq"][(j, n, d2)] = aop(
                        lambda s, n=n, d2=d2, ab=ab: s.activation(
                            out=dA_pp[ab], in_=delta[d2], func=AF.Exp,
                            scale=a_s[:, d2 * 16 + n:d2 * 16 + n + 1]), sem="aq")
                wait("vector", "sb%d" % nb, sc["bc_ld"][(j, n)])
                for d2 in range(2):
                    vop(lambda v, d2=d2, nb=nb: v.tensor_tensor(
                        out=dBu_t[d2], in0=du_t[d2], in1=Bnb[nb], op=ALU.mult))
                vdr()
                hp = n % 2
                for d2 in range(2):
                    ab = (2 * n + d2) % 4
                    wait("vector", "aq", sc["dA_aq"][(j, n, d2)])
                    for key in [(j, n - 2, d2), (j - 1, NST - 2 + n, d2)]:
                        if key in sc["state_gg"]:
                            wait("vector", "gg", sc["state_gg"][key])
                            break
                        if key in sc["hC_gg"]:
                            wait("vector", "gg", sc["hC_gg"][key])
                            break
                    if st0:
                        sc["scan_vv"][(j, n, d2)] = vop(
                            lambda v, d2=d2, ab=ab, hp=hp: v.tensor_tensor_scan(
                                out=hs_t[hp][d2], data0=dA_pp[ab], data1=dBu_t[d2],
                                initial=0.0, op0=ALU.mult, op1=ALU.add), sem="vv")
                    else:
                        col = state_s[:, d2 * 16 + n:d2 * 16 + n + 1]
                        sc["scan_vv"][(j, n, d2)] = vop(
                            lambda v, d2=d2, col=col, ab=ab, hp=hp:
                            v.tensor_tensor_scan(
                                out=hs_t[hp][d2], data0=dA_pp[ab], data1=dBu_t[d2],
                                initial=col, op0=ALU.mult, op1=ALU.add), sem="vv")
                if n >= 1:
                    for d2 in range(2):
                        wait("vector", "gg", sc["hC_gg"][(j, n - 1, d2)])
                        if n == 1:
                            sc["yacc_vv"][(j, 0, d2)] = vop(
                                lambda v, d2=d2: v.tensor_copy(
                                    y_t[d2], hC_t[d2]), sem="vv")
                        else:
                            sc["yacc_vv"][(j, n - 1, d2)] = vop(
                                lambda v, d2=d2: v.tensor_tensor(
                                    out=y_t[d2], in0=y_t[d2], in1=hC_t[d2],
                                    op=ALU.add), sem="vv")
                for d2 in range(2):
                    wait("gpsimd", "vv", sc["scan_vv"][(j, n, d2)])
                    if (j, n - 1, d2) in sc["yacc_vv"]:
                        wait("gpsimd", "vv", sc["yacc_vv"][(j, n - 1, d2)])
                    sc["hC_gg"][(j, n, d2)] = gop(
                        lambda g, d2=d2, nb=nb, hp=hp: g.tensor_tensor(
                            out=hC_t[d2], in0=hs_t[hp][d2], in1=Cnb[nb],
                            op=ALU.mult), sem="gg")
                    if not last_tile:
                        sc["state_gg"][(j, n, d2)] = gop(
                            lambda g, n=n, d2=d2, hp=hp: g.tensor_copy(
                                state_s[:, d2 * 16 + n:d2 * 16 + n + 1],
                                hs_t[hp][d2][:, 1023:1024]), sem="gg")
                if n <= NST - 3:
                    if n >= 2:
                        emit_bc_load(j, n + 2, j, n - 2)
                    else:
                        emit_bc_load(j, n + 2, j - 1, n + 14)
            for d2 in range(2):
                wait("vector", "gg", sc["hC_gg"][(j, NST - 1, d2)])
                sc["yacc_vv"][(j, NST - 1, d2)] = vop(
                    lambda v, d2=d2: v.tensor_tensor(
                        out=y_t[d2], in0=y_t[d2], in1=hC_t[d2], op=ALU.add),
                    sem="vv")
            vdr()
            if j + 1 < 16:
                emit_bc_load(j + 1, 0, j, NST - 4)
                emit_bc_load(j + 1, 1, j, NST - 3)
                wait("scalar", "vv", sc["du_vv"][j])
                colsn = slice((j + 1) * GTW, (j + 2) * GTW)
                for mc in range(2):
                    dma("scalar", delta[mc],
                        d_delta[mc * 128:(mc + 1) * 128, colsn],
                        sem="stl%d" % ((j + 1) % 2))
                sc["dl_ld"][j + 1] = C["stl%d" % ((j + 1) % 2)]
            if j >= 1:
                wait("vector", "mm", sc["yop_rd_mm"].get(j - 1, 0))
            wait("vector", "stl%d" % par, sc["sz_ld"][j])
            for d2 in range(2):
                if not last_tile:
                    wait("vector", "gg", sc["state_gg"][(j, NST - 1, d2)])
                wait("vector", "gg", sc["hC_gg"][(j, NST - 1, d2)])
                vop(lambda v, d2=d2, par=par: v.scalar_tensor_tensor(
                    out=hs_t[0][d2], in0=xc2_t[par][d2],
                    scalar=dp_s[:, d2:d2 + 1],
                    in1=y_t[d2], op0=ALU.mult, op1=ALU.add))
            vdr()
            for d2 in range(2):
                vop(lambda v, d2=d2: v.tensor_tensor(
                    out=yop_t[d2], in0=hs_t[0][d2], in1=sz2_t[d2], op=ALU.mult))
            vdr()
            sc["fin_vv"][j] = vop(lambda v: v.nop(), sem="vv")
            if dbg and j == 0:
                wait("scalar", "vv", sc["fin_vv"][j])
                for d2 in range(2):
                    dma("scalar", dbg_sc[4 + d2, :, :], yop_t[d2], sem="smk")
                    dma("scalar", dbg_sc[6 + d2, :, :], xc2_t[par][d2], sem="smk")

        def emit_bc_load(jn, n, gj, gn):
            nb = n % 4
            if (gj, gn, 1) in sc["scan_vv"]:
                wait("scalar", "vv", sc["scan_vv"][(gj, gn, 1)])
            if (gj, gn, 1) in sc["hC_gg"]:
                wait("scalar", "gg", sc["hC_gg"][(gj, gn, 1)])
            cols = slice(jn * GTW, (jn + 1) * GTW)
            dma("scalar", Bnb[nb],
                d_bc.ap()[n:n + 1, cols].partition_broadcast(128)[:, 0, :],
                sem="sb%d" % nb)
            dma("scalar", Cnb[nb],
                d_bc.ap()[16 + n:17 + n, cols].partition_broadcast(128)[:, 0, :],
                sem="sb%d" % nb)
            sc["bc_ld"][(jn, n)] = C["sb%d" % nb]

        def emit_bc_preload(j):
            wait("scalar", "gbc", C["gbc"])
            for n in range(2):
                emit_bc_load(j, n, -1, 0)

        def emit_out_proj(j):
            wait("tensor", "vv", sc["fin_vv"][j])
            shard = j // 2
            lc0 = (j % 2) * 1024
            for mc8 in range(8):
                for nt in range(2):
                    bp = (mc8 * 2 + nt) % 2
                    bk = 6 + bp
                    wait("tensor", "aq", sc["b67"][bp])
                    mms = []
                    for d2 in range(2):
                        mms.append(lambda t, d2=d2, mc8=mc8, nt=nt, bk=bk:
                                   t.matmul(
                                       ps[bk][:, :],
                                       opw_s[:, d2 * D + mc8 * 128:
                                             d2 * D + (mc8 + 1) * 128],
                                       yop_t[d2][:, nt * 512:(nt + 1) * 512],
                                       start=(d2 == 0), stop=(d2 == 1)))
                    mmv = mm_group(mms)
                    wait("scalar", "mm", mmv)
                    wait("scalar", "srs%d" % bp, C["srs%d" % bp])
                    ev = aop(lambda s, bk=bk, bp=bp: s.copy(
                        po2[bp][:, :], ps[bk][:, :]), sem="aq")
                    sc["b67"][bp] = ev
                    wait("scalar", "aq", ev)
                    dma("scalar",
                        rs_in[shard, mc8 * 128:(mc8 + 1) * 128,
                              lc0 + nt * 512:lc0 + (nt + 1) * 512],
                        po2[bp][:, :], sem="srs%d" % bp)
            sc["yop_rd_mm"][j] = C["mm"]

        scan_at = {}
        for jj in range(16):
            scan_at.setdefault(9 + jj * 11 // 8, []).append(jj)

        for cyc in range(32):
            emit_moe_chunk(cyc)
            if 4 <= cyc <= 7:
                for k in range(4):
                    emit_p1_gt((cyc - 4) * 4 + k)
                if cyc == 7:
                    wait("gpsimd", "sar", C["sar"])
                    wait("gpsimd", "sar1", C["sar1"])
                    em("gpsimd", lambda g: g.collective_compute(
                        "AllReduce", ALU.add, replica_groups=[list(range(NCORES))],
                        ins=[ar_in.ap().opt()],
                        outs=[ar_out.ap().opt()]).then_inc(S["cc"], 1))
                    bump("cc")
                    wait("gpsimd", "cc", 2)
                    dma("gpsimd", d_bc[:, :], ar_out[64:96, :], sem="gbc")
                    if dbg:
                        dma("gpsimd", dbg_ar[:, :], ar_out[:, :], sem="gbc")
            if cyc == 8:
                sc["b67"] = list(p1["bank_free"])
                emit_dt_burst()
                wait("scalar", "smk", C["smk"])
                for mc in range(2):
                    dma("scalar", delta[mc], d_delta[mc * 128:(mc + 1) * 128, 0:GTW],
                        sem="stl0")
                sc["dl_ld"][0] = C["stl0"]
                emit_scan_loads(0)
                emit_bc_preload(0)
                emit_scan_loads(1)
            for jj in scan_at.get(cyc, []):
                if jj >= 1:
                    emit_out_proj(jj - 1)
                emit_scan_tile(jj)
                emit_scan_loads(jj + 2)
        emit_out_proj(15)
        rs_total0 = C["srs0"]
        rs_total1 = C["srs1"]

        wait("gpsimd", "srs0", rs_total0)
        wait("gpsimd", "srs1", rs_total1)
        em("gpsimd", lambda g: g.collective_compute(
            "ReduceScatter", ALU.add, replica_groups=[list(range(NCORES))],
            ins=[rs_in.ap().opt()], outs=[rs_out.ap().opt()]).then_inc(S["cc"], 1))
        bump("cc")

        wait("sync", "cc", 3)
        wait("sync", "mm", moe_state["mm_total"])
        if dbg:
            dma("sync", dbg_rs[:, :], rs_out[:, :], sem="sf0")
            dma("sync", dbg_rsp[:, :], rs_in[0, :, :], sem="sf0")
            dma("sync", dbg_bc[:, :], d_bc[:, :], sem="sf0")
            wait("sync", "vv", C["vv"])
            dma("sync", dbg_moe[:, :], out_acc[:, :], sem="sf0")
        dma("sync", rstd_b, d_rows.ap()[0:1, :].partition_broadcast(128)[:, 0, :],
            sem="sf0")
        dma("sync", nmr_b, d_rows.ap()[1:2, :].partition_broadcast(128)[:, 0, :],
            sem="sf0")
        p3b_ld = C["sf0"]
        last_acc_vv = C["vv"]
        f_ld = [p3b_ld, 0]
        f_free = [0, 0]
        f_st = [0, 0]
        i_t = 0
        for tt in range(4):
            for dc in range(8):
                bu = i_t % 2
                rsl = slice(dc * 128, (dc + 1) * 128)
                csl = slice(tt * 512, (tt + 1) * 512)
                wait("sync", "vv", f_free[bu])
                dma("sync", f_rs[bu], rs_out[rsl, csl], sem="sf%d" % bu)
                dma("sync", f_x[bu], xT[rsl, csl], sem="sf%d" % bu)
                f_ld[bu] = C["sf%d" % bu]
                asl = slice(dc * 2048 + tt * 512, dc * 2048 + (tt + 1) * 512)
                wait("scalar", "vv", max(f_free[bu], last_acc_vv))
                g_aq = aop(lambda s, asl=asl, bu=bu: s.activation(
                    out=f_g[bu], in_=out_acc[:, asl], func=AF.Silu), sem="aq")
                wait("vector", "sf%d" % bu, max(f_ld[bu], f_st[bu]))
                wait("vector", "aq", g_aq)
                vop(lambda v, bu=bu, csl=csl: v.tensor_tensor(
                    out=f_s[bu], in0=f_x[bu], in1=rstd_b[:, csl], op=ALU.mult))
                vdr()
                vop(lambda v, bu=bu, csl=csl: v.tensor_tensor(
                    out=f_s[bu], in0=f_s[bu], in1=nmr_b[:, csl], op=ALU.add))
                vdr()
                vop(lambda v, bu=bu: v.tensor_tensor(
                    out=f_s[bu], in0=f_s[bu], in1=f_rs[bu], op=ALU.add))
                vdr()
                vop(lambda v, bu=bu: v.tensor_tensor(
                    out=f_s[bu], in0=f_s[bu], in1=f_g[bu], op=ALU.mult))
                vdr()
                o_vv = vop(lambda v, bu=bu: v.tensor_tensor(
                    out=f_s[bu], in0=f_s[bu], in1=f_x[bu], op=ALU.add), sem="vv")
                f_free[bu] = o_vv
                wait("sync", "vv", o_vv)
                dma("sync", out_d[rsl, csl], f_s[bu], sem="sf%d" % bu)
                f_st[bu] = C["sf%d" % bu]
                i_t += 1
        wait("sync", "sf0", C["sf0"])
        wait("sync", "sf1", C["sf1"])

        @block.tensor
        def _(t):
            for fn in prog["tensor"]:
                fn(t)

        @block.vector
        def _(v):
            for fn in prog["vector"]:
                fn(v)

        @block.scalar
        def _(s):
            for fn in prog["scalar"]:
                fn(s)

        @block.gpsimd
        def _(g):
            for fn in prog["gpsimd"]:
                fn(g)

        @block.sync
        def _(sy):
            for fn in prog["sync"]:
                fn(sy)

    return nc


_NC_CACHE = None


def _get_nc():
    global _NC_CACHE
    if _NC_CACHE is None:
        _NC_CACHE = build_nc()
    return _NC_CACHE


def make_in_maps(L0_attn_x, L0_attn_norm_w, L0_attn_in_proj_w, L0_attn_conv_w,
                 L0_attn_conv_b, L0_attn_x_proj_w, L0_attn_dt_proj_w,
                 L0_attn_dt_proj_b, L0_attn_A_log, L0_attn_Dp, L0_attn_out_proj_w,
                 L0_ffn_gate_w, L0_ffn_gate_b, L0_ffn_w1, L0_ffn_b1, L0_ffn_w2,
                 L0_ffn_b2):
    f = np.float32
    x = np.asarray(L0_attn_x, f)
    w1 = np.asarray(L0_ffn_w1, f)
    w1r = np.ascontiguousarray(
        w1.reshape(E, 32, 128, 8, 128).transpose(0, 4, 1, 3, 2)
        .reshape(E, 128, 32768))
    shared = {
        "ident": np.eye(128, dtype=f),
        "idx128": np.tile(np.arange(E, dtype=f), (128, 1)),
        "gwT": np.ascontiguousarray(np.asarray(L0_ffn_gate_w, f).T),
        "gb": np.asarray(L0_ffn_gate_b, f).reshape(1, E),
        "normw_r": np.ascontiguousarray(np.asarray(L0_attn_norm_w, f).reshape(8, 128).T),
        "b1T": np.ascontiguousarray(
            np.asarray(L0_ffn_b1, f).reshape(E, 32, 128).transpose(2, 0, 1).reshape(128, E * 32)),
        "b2T": np.ascontiguousarray(
            np.asarray(L0_ffn_b2, f).reshape(E, 8, 128).transpose(2, 0, 1).reshape(128, E * 8)),
        "w1r": w1r,
        "w2T": np.ascontiguousarray(np.asarray(L0_ffn_w2, f).transpose(0, 2, 1)),
    }
    ipw = np.asarray(L0_attn_in_proj_w, f)
    cw = np.asarray(L0_attn_conv_w, f)
    cb = np.asarray(L0_attn_conv_b, f)
    xpw = np.asarray(L0_attn_x_proj_w, f)
    dtp = np.asarray(L0_attn_dt_proj_w, f)
    dtb = np.asarray(L0_attn_dt_proj_b, f)
    alog = np.asarray(L0_attn_A_log, f)
    dpv = np.asarray(L0_attn_Dp, f)
    opw = np.asarray(L0_attn_out_proj_w, f)

    in_maps = []
    for c in range(NCORES):
        b_, q = c // 4, c % 4
        sl = slice(DSL * c, DSL * (c + 1))
        m = dict(shared)
        m["xT"] = np.ascontiguousarray(x[b_, q * TOK:(q + 1) * TOK, :].T)
        m["wxiT"] = np.ascontiguousarray(ipw[sl, :].T)
        m["wzT"] = np.ascontiguousarray(ipw[DI + DSL * c:DI + DSL * (c + 1), :].T)
        m["convw_r"] = np.ascontiguousarray(
            cw[sl].reshape(2, 128, 4).transpose(1, 0, 2).reshape(128, 8))
        m["convb_r"] = np.ascontiguousarray(cb[sl].reshape(2, 128).T)
        m["xpwT"] = np.ascontiguousarray(xpw[:, sl].T)
        m["dtw"] = np.ascontiguousarray(dtp[sl, :].T)
        m["dtb_r"] = np.ascontiguousarray(dtb[sl].reshape(2, 128).T)
        m["alog_r"] = np.ascontiguousarray(
            alog[sl].reshape(2, 128, NST).transpose(1, 0, 2).reshape(128, 32))
        m["dp_r"] = np.ascontiguousarray(dpv[sl].reshape(2, 128).T)
        m["opwT_sl"] = np.ascontiguousarray(opw[:, sl].T)
        in_maps.append(m)
    return in_maps


def kernel(**inputs):
    in_maps = make_in_maps(**inputs)
    nc = _get_nc()
    res = run_bass_kernel_spmd(nc, in_maps, core_ids=list(range(NCORES)))
    out = np.empty((B, L, D), dtype=np.float32)
    for c in range(NCORES):
        b_, q = c // 4, c % 4
        out[b_, q * TOK:(q + 1) * TOK, :] = res.results[c]["out"].T
    return out


# version 09
# speedup vs baseline: 1.0657x; 1.0435x over previous
import numpy as np
from contextlib import ExitStack

import concourse.bass as bass
import concourse.mybir as mybir
from concourse.bass_utils import run_bass_kernel_spmd

F32 = mybir.dt.float32
BF16 = mybir.dt.bfloat16
AF = mybir.ActivationFunctionType
ALU = mybir.AluOpType
AX = mybir.AxisListType

NCORES = 8
B, L, D = 2, 8192, 1024
DI, NST, DTR = 2048, 16, 64
E, HID = 8, 4096
TOK = 2048
GT = B * L
DSL = DI // NCORES
NGT = 16
GTW = GT // NGT
EPS = 1e-5
BIGNEG = 100.0


def build_nc(dbg=False):
    nc = bass.Bass("TRN2", target_bir_lowering=False, debug=False)

    def din(name, shape, dt=F32):
        return nc.dram_tensor(name, shape, dt, kind="ExternalInput")

    dkind = "ExternalOutput" if dbg else "Internal"

    xT = din("xT", [D, TOK])
    ident = din("ident", [128, 128])
    idx128 = din("idx128", [128, E])
    gwT = din("gwT", [D, E])
    gb = din("gb", [1, E])
    normw_r = din("normw_r", [128, 8])
    wxiT = din("wxiT", [D, DSL])
    wzT = din("wzT", [D, DSL])
    convw_r = din("convw_r", [128, 8])
    convb_r = din("convb_r", [128, 2])
    xpwT = din("xpwT", [DSL, 96])
    dtw = din("dtw", [DTR, DSL])
    dtb_r = din("dtb_r", [128, 2])
    alog_r = din("alog_r", [128, 32])
    dp_r = din("dp_r", [128, 2])
    opwT_sl = din("opwT_sl", [DSL, D])
    b1T = din("b1T", [128, E * 32])
    b2T = din("b2T", [128, E * 8])
    w1r = din("w1r", [E, 128, 32768])
    w2T = din("w2T", [E, HID, D])

    out_d = nc.dram_tensor("out", [D, TOK], F32, kind="ExternalOutput")

    d_xnbf = nc.dram_tensor("d_xnbf", [D, TOK], BF16, kind=dkind)
    d_m = nc.dram_tensor("d_m", [E, TOK], BF16, kind=dkind)
    d_pm = nc.dram_tensor("d_pm", [E, TOK], BF16, kind=dkind)
    d_rows = nc.dram_tensor("d_rows", [4, TOK], F32)
    ag1_in = nc.dram_tensor("ag1_in", [D, TOK], BF16)
    ag1_out = nc.dram_tensor("ag1_out", [NCORES * D, TOK], BF16)
    ar_in = nc.dram_tensor("ar_in", [96, GT], F32)
    ar_out = nc.dram_tensor("ar_out", [96, GT], F32)
    d_bc = nc.dram_tensor("d_bc", [32, GT], BF16)
    d_xc = nc.dram_tensor("d_xc", [DSL, GT], BF16, kind=dkind)
    d_delta = nc.dram_tensor("d_delta", [DSL, GT], BF16)
    d_silz = nc.dram_tensor("d_silz", [DSL, GT], BF16, kind=dkind)
    d_w1bf = nc.dram_tensor("d_w1bf", [E, 128, 32768], BF16)
    d_w2bf = nc.dram_tensor("d_w2bf", [E, HID, D], BF16)
    rs_in = nc.dram_tensor("rs_in", [NCORES, D, TOK], F32)
    rs_out = nc.dram_tensor("rs_out", [D, TOK], F32)
    if dbg:
        dbg_ar = nc.dram_tensor("dbg_ar", [96, GT], F32, kind="ExternalOutput")
        dbg_rs = nc.dram_tensor("dbg_rs", [D, TOK], F32, kind="ExternalOutput")
        dbg_moe = nc.dram_tensor("dbg_moe", [128, 16384], BF16, kind="ExternalOutput")
        dbg_rsp = nc.dram_tensor("dbg_rsp", [D, TOK], F32, kind="ExternalOutput")
        dbg_bc = nc.dram_tensor("dbg_bc", [32, GT], BF16, kind="ExternalOutput")
        dbg_sc = nc.dram_tensor("dbg_sc", [8, 128, 1024], BF16, kind="ExternalOutput")
        dbg_m2 = nc.dram_tensor("dbg_m2", [4, 128, 4096], BF16, kind="ExternalOutput")

    with ExitStack() as stk:
        EE = stk.enter_context

        A1 = EE(nc.sbuf_tensor("A1", [128, 12288], BF16))
        A2 = EE(nc.sbuf_tensor("A2", [128, 16384], BF16))
        A3 = EE(nc.sbuf_tensor("A3", [128, 16384], BF16))
        A4 = EE(nc.sbuf_tensor("A4", [128, 43008], BF16))
        AM = EE(nc.sbuf_tensor("AM", [128, 11264], BF16))
        AW2 = EE(nc.sbuf_tensor("AW2", [128, 2048], BF16))
        opw_s = EE(nc.sbuf_tensor("opw_s", [128, 2048], BF16))

        ones_s = EE(nc.sbuf_tensor("ones_s", [128, 1], F32))
        ident_s = EE(nc.sbuf_tensor("ident_s", [128, 128], F32))
        idx_s = EE(nc.sbuf_tensor("idx_s", [128, E], F32))
        gw_s = EE(nc.sbuf_tensor("gw_s", [128, 64], F32))
        gbb_s = EE(nc.sbuf_tensor("gbb_s", [128, E], F32))
        normw_s = EE(nc.sbuf_tensor("normw_s", [128, 8], F32))
        convw_s = EE(nc.sbuf_tensor("convw_s", [128, 8], F32))
        convb_s = EE(nc.sbuf_tensor("convb_s", [128, 2], F32))
        dtw_s = EE(nc.sbuf_tensor("dtw_s", [64, DSL], F32))
        dtb_s = EE(nc.sbuf_tensor("dtb_s", [128, 2], F32))
        alog_s = EE(nc.sbuf_tensor("alog_s", [128, 32], F32))
        a_s = EE(nc.sbuf_tensor("a_s", [128, 32], F32))
        dp_s = EE(nc.sbuf_tensor("dp_s", [128, 2], F32))
        b1_s = EE(nc.sbuf_tensor("b1_s", [128, E * 32], F32))
        b2_s = EE(nc.sbuf_tensor("b2_s", [128, E * 8], F32))
        state_s = EE(nc.sbuf_tensor("state_s", [128, 32], BF16))
        cst_s = EE(nc.sbuf_tensor("cst_s", [128, 2], F32))

        ps = [EE(nc.psum_tensor(f"psb{i}", [128, 512], F32)) for i in range(8)]

        sem_names = ["s0", "mm", "vv", "aq", "gg", "cc", "gw1p", "gw2p",
                     "gbc", "smk", "w1r0", "w1r1", "w1r2", "w1r3", "w1r4", "w1r5",
                     "w2r0", "w2r1", "w2r2", "w2r3", "xnb", "pmm",
                     "sr0", "sr1", "sar", "stl0", "stl1",
                     "sb0", "sb1", "sb2", "sb3", "srs0", "srs1",
                     "sf0", "sf1", "sxc", "gp1", "sar1", "ssz"]
        S = {}
        for name in sem_names:
            S[name] = EE(nc.semaphore(name))
        C = {k: 0 for k in S}

        block = EE(nc.Block())
        prog = {"tensor": [], "vector": [], "scalar": [], "gpsimd": [], "sync": []}

        def em(eng, fn):
            prog[eng].append(fn)

        def bump(name, by=1):
            C[name] += by
            return C[name]

        def dma(eng_name, out_ap, in_ap, sem="s0"):
            h = S[sem]
            t = bump(sem, 16)

            def fn(e, out_ap=out_ap, in_ap=in_ap, h=h):
                e.dma_start(out=out_ap, in_=in_ap).then_inc(h, 16)
            em(eng_name, fn)
            return t

        def wait(eng_name, sem, tgt):
            if tgt <= 0:
                return
            h = S[sem]

            def fn(e, h=h, tgt=tgt):
                e.wait_ge(h, tgt)
            em(eng_name, fn)

        def wait_lazy(eng_name, sem, tgt_fn):
            h = S[sem]

            def fn(e, h=h, tgt_fn=tgt_fn):
                t = tgt_fn()
                if t > 0:
                    e.wait_ge(h, t)
            em(eng_name, fn)

        def vop(fb, sem=None):
            if sem is None:
                em("vector", lambda v, f=fb: f(v))
                return None
            h = S[sem]
            t = bump(sem)

            def fn(v, f=fb, h=h):
                f(v).then_inc(h, 1)
            em("vector", fn)
            return t

        def aop(fb, sem=None):
            if sem is None:
                em("scalar", lambda s, f=fb: f(s))
                return None
            h = S[sem]
            t = bump(sem)

            def fn(s, f=fb, h=h):
                f(s).then_inc(h, 1)
            em("scalar", fn)
            return t

        def gop(fb, sem=None):
            if sem is None:
                em("gpsimd", lambda g, f=fb: f(g))
                return None
            h = S[sem]
            t = bump(sem)

            def fn(g, f=fb, h=h):
                f(g).then_inc(h, 1)
            em("gpsimd", fn)
            return t

        def vdr():
            em("vector", lambda v: v.drain())

        def adr():
            em("scalar", lambda s: s.drain())

        def mm_group(mms, sem="mm"):
            t = None
            for i, mb in enumerate(mms):
                if i == len(mms) - 1:
                    h = S[sem]
                    t = bump(sem)

                    def fn(tt_, f=mb, h=h):
                        f(tt_).then_inc(h, 1)
                    em("tensor", fn)
                else:
                    em("tensor", lambda tt_, f=mb: f(tt_))
            return t

        A1b, A2b, A3b, A4b = A1.ap(), A2.ap(), A3.ap(), A4.ap()
        AMb, AW2b = AM.ap(), AW2.ap()
        A1f = A1.ap().bitcast(F32)
        A2f = A2.ap().bitcast(F32)
        A3f = A3.ap().bitcast(F32)
        A4f = A4.ap().bitcast(F32)
        AMf = AM.ap().bitcast(F32)
        opwf = opw_s.ap().bitcast(F32)

        h_sb = A2b
        out_acc = A3b
        xe_sl = [AMb[:, 0:4096], AMb[:, 4096:8192]]
        pm_e = AMb[:, 8192:8704]
        m_e = AMb[:, 8704:9216]
        ystg = [AMb[:, 9216 + i * 512:9216 + (i + 1) * 512] for i in range(2)]
        t1_b = [AMb[:, 10240 + i * 512:10240 + (i + 1) * 512] for i in range(2)]
        w1sl = [A1b[:, i * 2048:(i + 1) * 2048] for i in range(6)]
        w2b = [AW2b[:, i * 512:(i + 1) * 512] for i in range(4)]

        delta = [A4b[:, d2 * 1024:(d2 + 1) * 1024] for d2 in range(2)]
        dA_pp = [A4f[:, 1024 + i * 1024:1024 + (i + 1) * 1024] for i in range(4)]
        y_t = [A4f[:, 5120 + d2 * 1024:5120 + (d2 + 1) * 1024] for d2 in range(2)]
        du_t = [A4b[:, 14336 + d2 * 1024:14336 + (d2 + 1) * 1024] for d2 in range(2)]
        dBu_t = [A4b[:, 16384 + d2 * 1024:16384 + (d2 + 1) * 1024] for d2 in range(2)]
        hs_t = [[A4b[:, 18432 + (np_ * 2 + d2) * 1024:
                     18432 + (np_ * 2 + d2 + 1) * 1024] for d2 in range(2)]
                for np_ in range(2)]
        hC_t = [A4b[:, 22528 + d2 * 1024:22528 + (d2 + 1) * 1024] for d2 in range(2)]
        Bnb = [A4b[:, 24576 + i * 1024:24576 + (i + 1) * 1024] for i in range(4)]
        Cnb = [A4b[:, 28672 + i * 1024:28672 + (i + 1) * 1024] for i in range(4)]
        xc2_t = [[A4b[:, 32768 + (par * 2 + d2) * 1024:
                      32768 + (par * 2 + d2 + 1) * 1024] for d2 in range(2)]
                 for par in range(2)]
        sz2_t = [A4b[:, 36864 + d2 * 1024:36864 + (d2 + 1) * 1024] for d2 in range(2)]
        yop_t = [A4b[:, 38912 + d2 * 1024:38912 + (d2 + 1) * 1024] for d2 in range(2)]
        po2 = [A4f[:, 20480 + i * 512:20480 + (i + 1) * 512] for i in range(2)]
        es_t = po2[0]
        dstage = A4b[:, 41984:42496]
        dt_sb = A4f[0:64, 16384:17408]

        def x_chunk(dc):
            if dc < 3:
                return A1f[:, dc * 2048:(dc + 1) * 2048]
            if dc < 7:
                return A2f[:, (dc - 3) * 2048:(dc - 2) * 2048]
            return A4f[:, 16384:18432]

        def xn_chunk(dc):
            return A4f[:, dc * 2048:(dc + 1) * 2048]

        rstd_b = AMf[:, 0:2048]
        nmr_b = AMf[:, 2048:4096]

        def rw(i):
            return A3f[0:1, i * 2048:(i + 1) * 2048]
        gscr = A3f[:, 0:1024]
        msb = A3b[0:16, 2048:4096]
        xnstg = [A3b[:, 4096 + i * 2048:4096 + (i + 1) * 2048] for i in range(2)]
        ag1stg = [A3b[:, 8192 + i * 2048:8192 + (i + 1) * 2048] for i in range(2)]

        rhs_t = [[A4b[:, (bu * 8 + k) * 1024:(bu * 8 + k + 1) * 1024]
                  for k in range(8)] for bu in range(2)]
        wxi_s = A4b[:, 16384:18432]
        wz_s = A4b[:, 18432:20480]
        xc_t = [A4b[:, 20480 + i * 1024:20480 + (i + 1) * 1024] for i in range(2)]
        sz_st = A4b[:, 22528:23552]
        xpw_s = A4b[:, 23552:23744]
        xi_ext = [A4f[:, 11872:12899], A4f[:, 12899:13926]]
        cv_a = A4f[:, 13926:14950]
        cv_b = A4f[:, 14950:15974]
        po_st = [opwf[:, 0:512], opwf[:, 512:1024]]

        f_rs = [A1f[:, i * 512:(i + 1) * 512] for i in range(2)]
        f_x = [A1f[:, 1024 + i * 512:1024 + (i + 1) * 512] for i in range(2)]
        f_g = [A1f[:, 2048 + i * 512:2048 + (i + 1) * 512] for i in range(2)]
        f_s = [A1f[:, 3072 + i * 512:3072 + (i + 1) * 512] for i in range(2)]

        vop(lambda v: v.memset(ones_s[:, :], 1.0))
        vop(lambda v: v.memset(state_s[:, :], 0.0))
        vop(lambda v: v.memset(cst_s[:, 0:1], EPS))
        cst_vv = vop(lambda v: v.memset(cst_s[:, 1:2], 1.0), sem="vv")
        wait("scalar", "vv", cst_vv)
        dma("sync", ident_s[:, :], ident[:, :])
        dma("sync", idx_s[:, :], idx128[:, :])
        for dc in range(8):
            dma("sync", gw_s[:, dc * 8:(dc + 1) * 8], gwT[dc * 128:(dc + 1) * 128, :])
        dma("sync", gbb_s[:, :], gb.ap().partition_broadcast(128)[:, 0, :])
        dma("sync", normw_s[:, :], normw_r[:, :])
        dma("sync", convw_s[:, :], convw_r[:, :])
        dma("sync", convb_s[:, :], convb_r[:, :])
        dma("sync", dtw_s[:, :], dtw[:, :])
        dma("sync", dtb_s[:, :], dtb_r[:, :])
        dma("sync", alog_s[:, :], alog_r[:, :])
        dma("sync", dp_s[:, :], dp_r[:, :])
        dma("sync", b1_s[:, :], b1T[:, :])
        dma("sync", b2_s[:, :], b2T[:, :])
        consts_ld = C["s0"]

        wait("scalar", "s0", consts_ld)
        a_aq = aop(lambda s: s.activation(out=a_s[:, :], in_=alog_s[:, :],
                                          func=AF.Exp), sem="aq")
        wait("vector", "aq", a_aq)
        vop(lambda v: v.tensor_scalar(out=a_s[:, :], in0=a_s[:, :], scalar1=-1.0,
                                      scalar2=None, op0=ALU.mult))
        vdr()

        w1p_mark = {}
        w2p_mark = {}

        def emit_precast(e_):
            for hf in range(2):
                dma("gpsimd", d_w1bf[e_, :, hf * 16384:(hf + 1) * 16384],
                    w1r[e_, :, hf * 16384:(hf + 1) * 16384], sem="gw1p")
            w1p_mark[e_] = C["gw1p"]
            for hc4 in range(0, 32, 8):
                dma("gpsimd", d_w2bf[e_, hc4 * 128:(hc4 + 8) * 128, :],
                    w2T[e_, hc4 * 128:(hc4 + 8) * 128, :], sem="gw2p")
            w2p_mark[e_] = C["gw2p"]

        emit_precast(0)
        emit_precast(1)

        for dc in range(8):
            dma("sync", x_chunk(dc), xT[dc * 128:(dc + 1) * 128, :])
        x_ld = C["s0"]

        wait("scalar", "s0", x_ld)
        for dc in range(8):
            aop(lambda s, dc=dc: s.activation(out=xn_chunk(dc), in_=x_chunk(dc),
                                              func=AF.Square))
        adr()
        sq_aq = aop(lambda s: s.nop(), sem="aq")

        wait("tensor", "s0", x_ld)
        wait("tensor", "aq", sq_aq)
        for nt in range(4):
            mms = []
            for dc in range(8):
                mms.append(lambda t, nt=nt, dc=dc: t.matmul(
                    ps[6][0:1, :], ones_s[:, :],
                    x_chunk(dc)[:, nt * 512:(nt + 1) * 512],
                    start=(dc == 0), stop=(dc == 7)))
            for dc in range(8):
                mms.append(lambda t, nt=nt, dc=dc: t.matmul(
                    ps[7][0:1, :], ones_s[:, :],
                    xn_chunk(dc)[:, nt * 512:(nt + 1) * 512],
                    start=(dc == 0), stop=(dc == 7)))
            mmv = mm_group(mms)
            wait("vector", "mm", mmv)
            vop(lambda v, nt=nt: v.tensor_copy(
                rw(0)[:, nt * 512:(nt + 1) * 512], ps[6][0:1, :]))
            bank_vv = vop(lambda v, nt=nt: v.tensor_copy(
                rw(1)[:, nt * 512:(nt + 1) * 512], ps[7][0:1, :]), sem="vv")
            wait("tensor", "vv", bank_vv)
        stats_mm = C["mm"]

        vop(lambda v: v.tensor_scalar(out=rw(0), in0=rw(0), scalar1=1.0 / D,
                                      scalar2=None, op0=ALU.mult))
        vdr()
        vop(lambda v: v.tensor_scalar(out=rw(1), in0=rw(1), scalar1=1.0 / D,
                                      scalar2=None, op0=ALU.mult))
        vdr()
        vop(lambda v: v.tensor_tensor(out=rw(2), in0=rw(0), in1=rw(0), op=ALU.mult))
        vdr()
        var_vv = vop(lambda v: v.tensor_tensor(out=rw(1), in0=rw(1), in1=rw(2),
                                               op=ALU.subtract), sem="vv")
        wait("scalar", "vv", var_vv)
        aop(lambda s: s.activation(out=rw(2), in_=rw(1), func=AF.Ln,
                                   bias=cst_s[0:1, 0:1]))
        adr()
        rstd_aq = aop(lambda s: s.activation(out=rw(2), in_=rw(2), func=AF.Exp,
                                             scale=-0.5), sem="aq")
        wait("vector", "aq", rstd_aq)
        vop(lambda v: v.tensor_tensor(out=rw(3), in0=rw(2), in1=rw(2), op=ALU.mult))
        vdr()
        t5_vv = vop(lambda v: v.tensor_tensor(out=rw(3), in0=rw(3), in1=rw(1),
                                              op=ALU.mult), sem="vv")
        wait("scalar", "vv", t5_vv)
        aop(lambda s: s.activation(out=rw(3), in_=rw(3), func=AF.Ln,
                                   bias=cst_s[0:1, 0:1]))
        adr()
        rrms_aq = aop(lambda s: s.activation(out=rw(3), in_=rw(3), func=AF.Exp,
                                             scale=-0.5), sem="aq")
        wait("vector", "aq", rrms_aq)
        nmr_vv = vop(lambda v: v.scalar_tensor_tensor(
            out=rw(0), in0=rw(0), scalar=-1.0, in1=rw(2), op0=ALU.mult,
            op1=ALU.mult), sem="vv")
        wait("sync", "vv", nmr_vv)
        dma("sync", d_rows[0:1, :], rw(2))
        dma("sync", d_rows[1:2, :], rw(0))
        dma("sync", d_rows[2:3, :], rw(3))
        rows_st = C["s0"]
        wait("sync", "s0", rows_st)
        dma("sync", rstd_b, d_rows.ap()[0:1, :].partition_broadcast(128)[:, 0, :])
        dma("sync", nmr_b, d_rows.ap()[1:2, :].partition_broadcast(128)[:, 0, :])
        bco_ld = C["s0"]

        wait("vector", "s0", bco_ld)
        wait("vector", "mm", stats_mm)
        for dc in range(8):
            vop(lambda v, dc=dc: v.tensor_tensor(
                out=xn_chunk(dc), in0=x_chunk(dc), in1=rstd_b, op=ALU.mult))
            vdr()
            vop(lambda v, dc=dc: v.tensor_tensor(
                out=xn_chunk(dc), in0=xn_chunk(dc), in1=nmr_b, op=ALU.add))
            vdr()
        xn_vv = vop(lambda v: v.nop(), sem="vv")

        wait("sync", "vv", xn_vv)
        dma("sync", rstd_b, d_rows.ap()[2:3, :].partition_broadcast(128)[:, 0, :])
        rrms_ld2 = C["s0"]
        wait("vector", "s0", rrms_ld2)
        rrms_b = rstd_b

        wait("vector", "s0", rows_st)
        stg_stq = [0, 0]
        for dc in range(8):
            bu = dc % 2
            if dc >= 2:
                wait("vector", "s0", C["s0"])
            vop(lambda v, dc=dc, bu=bu: v.tensor_copy(
                xnstg[bu][:, :], xn_chunk(dc)))
            st_vv = vop(lambda v, dc=dc, bu=bu: v.scalar_tensor_tensor(
                out=ag1stg[bu][:, :], in0=xn_chunk(dc),
                scalar=normw_s[:, dc:dc + 1], in1=rrms_b, op0=ALU.mult,
                op1=ALU.mult), sem="vv")
            wait("sync", "vv", st_vv)
            dma("sync", d_xnbf[dc * 128:(dc + 1) * 128, :], xnstg[bu][:, :])
            dma("sync", ag1_in[dc * 128:(dc + 1) * 128, :], ag1stg[bu][:, :])
            stg_stq[bu] = C["s0"]
        ag1in_st = C["s0"]

        wait("vector", "s0", rows_st)
        wait("tensor", "vv", xn_vv)
        for tc in range(16):
            mms = []
            for dc in range(8):
                mms.append(lambda t, tc=tc, dc=dc: t.matmul(
                    ps[5][:, 0:8],
                    xn_chunk(dc)[:, tc * 128:(tc + 1) * 128],
                    gw_s[:, dc * 8:(dc + 1) * 8], start=(dc == 0), stop=(dc == 7)))
            mmv = mm_group(mms)
            wait("vector", "mm", mmv)

            def gsl(a, b_):
                return gscr[:, a:b_]
            vop(lambda v: v.tensor_tensor(out=gsl(0, 8), in0=ps[5][:, 0:8],
                                          in1=gbb_s[:, :], op=ALU.add))
            vdr()
            bank_vv = vop(lambda v: v.nop(), sem="vv")
            wait("tensor", "vv", bank_vv)
            vop(lambda v: v.tensor_reduce(out=gsl(8, 9), in_=gsl(0, 8),
                                          axis=AX.X, op=ALU.max, negate=True))
            vdr()
            sm_vv = vop(lambda v: v.nop(), sem="vv")
            wait("scalar", "vv", sm_vv)
            e_aq = aop(lambda s: s.activation(out=gsl(0, 8), in_=gsl(0, 8),
                                              func=AF.Exp, bias=gsl(8, 9)),
                       sem="aq")
            wait("vector", "aq", e_aq)
            vop(lambda v: v.tensor_reduce(out=gsl(8, 9), in_=gsl(0, 8),
                                          axis=AX.X, op=ALU.add))
            vdr()
            vop(lambda v: v.reciprocal(gsl(9, 10), gsl(8, 9)))
            vdr()
            vop(lambda v: v.tensor_scalar(out=gsl(0, 8), in0=gsl(0, 8),
                                          scalar1=gsl(9, 10), scalar2=None,
                                          op0=ALU.mult))
            vdr()
            vop(lambda v: v.tensor_reduce(out=gsl(8, 9), in_=gsl(0, 8),
                                          axis=AX.X, op=ALU.max))
            vdr()
            vop(lambda v: v.tensor_scalar(out=gsl(10, 18), in0=gsl(0, 8),
                                          scalar1=gsl(8, 9), scalar2=None,
                                          op0=ALU.is_ge))
            vdr()
            vop(lambda v: v.scalar_tensor_tensor(
                out=gsl(18, 26), in0=gsl(10, 18), scalar=-BIGNEG, in1=gsl(0, 8),
                op0=ALU.mult, op1=ALU.add))
            vdr()
            vop(lambda v: v.tensor_reduce(out=gsl(8, 9), in_=gsl(18, 26),
                                          axis=AX.X, op=ALU.max))
            vdr()
            vop(lambda v: v.tensor_scalar(out=gsl(18, 26), in0=gsl(0, 8),
                                          scalar1=gsl(8, 9), scalar2=None,
                                          op0=ALU.is_ge))
            vdr()
            vop(lambda v: v.tensor_tensor(out=gsl(18, 26), in0=gsl(18, 26),
                                          in1=idx_s[:, :], op=ALU.mult))
            vdr()
            vop(lambda v: v.tensor_reduce(out=gsl(8, 9), in_=gsl(18, 26),
                                          axis=AX.X, op=ALU.max))
            vdr()
            vop(lambda v: v.tensor_scalar(out=gsl(26, 34), in0=idx_s[:, :],
                                          scalar1=gsl(8, 9), scalar2=None,
                                          op0=ALU.is_equal))
            vdr()
            vop(lambda v: v.tensor_tensor(out=gsl(34, 42), in0=gsl(0, 8),
                                          in1=gsl(26, 34), op=ALU.mult))
            vdr()
            mt_vv = vop(lambda v: v.nop(), sem="vv")
            wait("tensor", "vv", mt_vv)
            tp_mm = mm_group([lambda t: t.transpose(
                ps[4][0:16, 0:128], gsl(26, 42), ident_s[:, :])])
            wait("vector", "mm", tp_mm)
            bank_vv = vop(lambda v, tc=tc: v.tensor_copy(
                msb[:, tc * 128:(tc + 1) * 128], ps[4][0:16, 0:128]), sem="vv")
            wait("tensor", "vv", bank_vv)
        gate_vv = C["vv"]
        gate_mm = C["mm"]
        wait("scalar", "vv", gate_vv)
        dma("scalar", d_m[:, :], msb[0:8, :], sem="smk")
        dma("scalar", d_pm[:, :], msb[8:16, :], sem="smk")
        mask_st = C["smk"]

        wait("gpsimd", "s0", ag1in_st)
        em("gpsimd", lambda g: g.collective_compute(
            "AllGather", ALU.bypass, replica_groups=[list(range(NCORES))],
            ins=[ag1_in.ap().opt()], outs=[ag1_out.ap().opt()]).then_inc(S["cc"], 1))
        bump("cc")

        wait("vector", "smk", mask_st)
        outacc_zero_vv = vop(lambda v: v.memset(A3[:, :], 0.0), sem="vv")

        wait("sync", "smk", mask_st)
        for dc in range(8):
            dma("sync", xe_sl[0][:, dc * 512:(dc + 1) * 512],
                d_xnbf[dc * 128:(dc + 1) * 128, 0:512], sem="xnb")
        dma("sync", pm_e, d_pm.ap()[0:1, 0:512].partition_broadcast(128)[:, 0, :],
            sem="pmm")
        dma("sync", m_e, d_m.ap()[0:1, 0:512].partition_broadcast(128)[:, 0, :],
            sem="pmm")

        w1_slot_reader = [0] * 6
        w1_slot_loaded = [0] * 6
        w1_seq = []
        w1_emit = [0]
        w1_next_chunk = [0]

        def ensure_w1_entries(upto_cyc):
            while w1_next_chunk[0] <= min(upto_cyc, 31):
                ee = w1_next_chunk[0] // 4
                w1_seq.extend((ee, hg) for hg in range(16))
                w1_next_chunk[0] += 1
        w2_slot_reader = [0, 0, 0, 0]
        w2_slot_loaded = [0, 0, 0, 0]
        w2_i = 0
        xe_done_gg = [0, 0]
        hpass_done_mm = {}
        ypass_bank_add_vv = [outacc_zero_vv] * 4
        relu_aq = {}
        ystg_free_gg = [0, 0]
        t1_free_vv = [outacc_zero_vv, outacc_zero_vv]
        xnbf_loaded = {0: C["xnb"]}
        pmm_loaded = {0: C["pmm"]}
        moe_state = {"mm_total": 0, "last_t1_gg": 0}

        def w1_emit_loads(upto):
            while w1_emit[0] <= upto and w1_emit[0] < len(w1_seq):
                i = w1_emit[0]
                e_, hg = w1_seq[i]
                sl = i % 6
                wait("sync", "mm", w1_slot_reader[sl])
                if hg == 0:
                    wait("sync", "gw1p", w1p_mark[e_])
                w1_slot_loaded[sl] = dma(
                    "sync", w1sl[sl], d_w1bf[e_, :, hg * 2048:(hg + 1) * 2048],
                    sem="w1r%d" % sl)
                w1_emit[0] = i + 1

        def emit_moe_chunk(cyc):
            e_, tt = divmod(cyc, 4)
            nonlocal w2_i
            ttp = cyc % 2
            w1_base = 16 * cyc
            ensure_w1_entries(cyc + 1)
            wait("gpsimd", "xnb", xnbf_loaded[cyc])
            wait("gpsimd", "pmm", pmm_loaded[cyc])
            for dc in range(8):
                xe_done_gg[ttp] = gop(lambda g, dc=dc, ttp=ttp: g.tensor_tensor(
                    out=xe_sl[ttp][:, dc * 512:(dc + 1) * 512],
                    in0=xe_sl[ttp][:, dc * 512:(dc + 1) * 512], in1=pm_e,
                    op=ALU.mult), sem="gg")
            if cyc + 1 < 32:
                e2_, tt2 = divmod(cyc + 1, 4)
                tsl2 = slice(tt2 * 512, (tt2 + 1) * 512)
                if cyc >= 1:
                    wait("sync", "mm", hpass_done_mm[cyc - 1])
                for dc in range(8):
                    dma("sync", xe_sl[(cyc + 1) % 2][:, dc * 512:(dc + 1) * 512],
                        d_xnbf[dc * 128:(dc + 1) * 128, tsl2], sem="xnb")
                xnbf_loaded[cyc + 1] = C["xnb"]
            for hc in range(32):
                ghg = w1_base + hc // 2
                sl = ghg % 6
                bank = hc % 2
                if hc == 0:
                    wait("tensor", "gg", xe_done_gg[ttp])
                if hc % 2 == 0:
                    w1_emit_loads(ghg + 4)
                    wait("tensor", "w1r%d" % sl, w1_slot_loaded[sl])
                if (cyc, hc - 2) in relu_aq:
                    wait("tensor", "aq", relu_aq[(cyc, hc - 2)])
                elif hc < 2 and (cyc - 1, 30 + hc) in relu_aq:
                    wait("tensor", "aq", relu_aq[(cyc - 1, 30 + hc)])
                mms = []
                for kd in range(8):
                    mms.append(lambda t, hc=hc, kd=kd, bank=bank, sl=sl, ttp=ttp:
                               t.matmul(
                                   ps[bank][:, :],
                                   w1sl[sl][:, (hc % 2) * 1024 + kd * 128:
                                            (hc % 2) * 1024 + (kd + 1) * 128],
                                   xe_sl[ttp][:, kd * 512:(kd + 1) * 512],
                                   start=(kd == 0), stop=(kd == 7)))
                mmv = mm_group(mms)
                if hc % 2 == 1:
                    w1_slot_reader[sl] = mmv
                wait("scalar", "mm", mmv)
                relu_aq[(cyc, hc)] = aop(
                    lambda s, hc=hc, e_=e_, bank=bank: s.activation(
                        out=h_sb[:, hc * 512:(hc + 1) * 512], in_=ps[bank][:, :],
                        func=AF.Relu, bias=b1_s[:, e_ * 32 + hc:e_ * 32 + hc + 1]),
                    sem="aq")
            hpass_done_mm[cyc] = C["mm"]
            h_done = relu_aq[(cyc, 31)]
            if dbg and cyc in (0, 3):
                wait("scalar", "gg", xe_done_gg[ttp])
                dma("scalar", dbg_m2[0 + (cyc == 3), :, :], xe_sl[ttp], sem="smk")
                wait("scalar", "aq", h_done)
                dma("scalar", dbg_m2[2 + (cyc == 3), :, :], h_sb[:, 0:4096], sem="smk")
            for dcg in range(2):
                for dc4 in range(4):
                    wait("tensor", "vv", ypass_bank_add_vv[dc4])
                for hc in range(32):
                    wi = w2_i % 4
                    wait("sync", "mm", w2_slot_reader[wi])
                    if hc == 0 and dcg == 0 and tt == 0:
                        wait("sync", "gw2p", w2p_mark[e_])
                    w2_slot_loaded[wi] = dma(
                        "sync", w2b[wi],
                        d_w2bf[e_, hc * 128:(hc + 1) * 128,
                               dcg * 512:(dcg + 1) * 512], sem="w2r%d" % wi)
                    w2_i += 1
                    wait("tensor", "w2r%d" % wi, w2_slot_loaded[wi])
                    if hc == 0:
                        wait("tensor", "aq", h_done)
                    mms = []
                    for dc4 in range(4):
                        mms.append(lambda t, wi=wi, dc4=dc4, hc=hc: t.matmul(
                            ps[2 + dc4][:, :],
                            w2b[wi][:, dc4 * 128:(dc4 + 1) * 128],
                            h_sb[:, hc * 512:(hc + 1) * 512],
                            start=(hc == 0), stop=(hc == 31)))
                    mmv = mm_group(mms)
                    w2_slot_reader[wi] = mmv
                for dc4 in range(4):
                    dc = dcg * 4 + dc4
                    pp = dc4 % 2
                    wait("scalar", "mm", mmv)
                    wait("scalar", "gg", ystg_free_gg[pp])
                    ev_aq = aop(lambda s, e_=e_, dc=dc, dc4=dc4, pp=pp:
                                s.activation(
                                    out=ystg[pp][:, :], in_=ps[2 + dc4][:, :],
                                    func=AF.Identity,
                                    bias=b2_s[:, e_ * 8 + dc:e_ * 8 + dc + 1]),
                                sem="aq")
                    wait("gpsimd", "aq", ev_aq)
                    wait("gpsimd", "vv", t1_free_vv[pp])
                    ystg_free_gg[pp] = gop(lambda g, pp=pp: g.tensor_tensor(
                        out=t1_b[pp][:, :], in0=ystg[pp][:, :], in1=m_e,
                        op=ALU.mult), sem="gg")
                    wait("vector", "gg", ystg_free_gg[pp])
                    osl = slice(dc * 2048 + tt * 512, dc * 2048 + (tt + 1) * 512)
                    t1_free_vv[pp] = vop(lambda v, osl=osl, pp=pp: v.tensor_tensor(
                        out=out_acc[:, osl], in0=out_acc[:, osl],
                        in1=t1_b[pp][:, :], op=ALU.add), sem="vv")
                    ypass_bank_add_vv[dc4] = t1_free_vv[pp]
            w1_emit_loads(w1_base + 19)
            moe_state["mm_total"] = C["mm"]
            moe_state["last_t1_gg"] = max(ystg_free_gg)
            if cyc + 1 < 32:
                e2_, tt2 = divmod(cyc + 1, 4)
                tsl2 = slice(tt2 * 512, (tt2 + 1) * 512)
                wait("sync", "gg", moe_state["last_t1_gg"])
                dma("sync", pm_e,
                    d_pm.ap()[e2_:e2_ + 1, tsl2].partition_broadcast(128)[:, 0, :],
                    sem="pmm")
                dma("sync", m_e,
                    d_m.ap()[e2_:e2_ + 1, tsl2].partition_broadcast(128)[:, 0, :],
                    sem="pmm")
                pmm_loaded[cyc + 1] = C["pmm"]
            if tt == 1 and e_ + 2 < E:
                emit_precast(e_ + 2)

        p1 = {
            "rhs_ld": [0, 0], "rhs_free": [0, 0], "bank_free": [0, 0],
            "xi_free": 0, "sz_free": 0, "first": True,
        }

        def emit_p1_gt(gt):
            bu = gt % 2
            rank = gt // 2
            lc0 = (gt % 2) * 1024
            if p1["first"]:
                p1["first"] = False
                wait("gpsimd", "mm", gate_mm)
                for dc in range(8):
                    dma("gpsimd", wxi_s[:, dc * DSL:(dc + 1) * DSL],
                        wxiT[dc * 128:(dc + 1) * 128, :], sem="gp1")
                    dma("gpsimd", wz_s[:, dc * DSL:(dc + 1) * DSL],
                        wzT[dc * 128:(dc + 1) * 128, :], sem="gp1")
                for kc in range(2):
                    dma("gpsimd", xpw_s[:, kc * 96:(kc + 1) * 96],
                        xpwT[kc * 128:(kc + 1) * 128, :], sem="gp1")
                p1["w_ld"] = C["gp1"]
                wait("sync", "cc", 1)
                wait("tensor", "gp1", p1["w_ld"])
                wait("vector", "gp1", p1["w_ld"])
                vop(lambda v: v.memset(xi_ext[0][:, 0:3], 0.0))
                vop(lambda v: v.memset(xi_ext[1][:, 0:3], 0.0))
                vdr()
            wait("sync", "mm", p1["rhs_free"][bu])
            for k in range(8):
                dma("sync", rhs_t[bu][k],
                    ag1_out[rank * D + k * 128:rank * D + (k + 1) * 128,
                            lc0:lc0 + 1024], sem="sr%d" % bu)
            p1["rhs_ld"][bu] = C["sr%d" % bu]
            wait("scalar", "vv", p1["xi_free"])
            for mg in range(4):
                w_s = wxi_s if mg < 2 else wz_s
                mc = mg % 2
                for nt in range(2):
                    bp = (mg * 2 + nt) % 2
                    bk = 6 + bp
                    wait("tensor", "aq", p1["bank_free"][bp])
                    if mg == 0 and nt == 0:
                        wait("tensor", "sr%d" % bu, p1["rhs_ld"][bu])
                    mms = []
                    for kd in range(8):
                        mms.append(lambda t, w_s=w_s, mc=mc, kd=kd, nt=nt, bk=bk,
                                   bu=bu: t.matmul(
                                       ps[bk][:, :],
                                       w_s[:, kd * DSL + mc * 128:
                                           kd * DSL + (mc + 1) * 128],
                                       rhs_t[bu][kd][:, nt * 512:(nt + 1) * 512],
                                       start=(kd == 0), stop=(kd == 7)))
                    mmv = mm_group(mms)
                    wait("scalar", "mm", mmv)
                    if mg < 2:
                        p1["bank_free"][bp] = aop(
                            lambda s, mc=mc, nt=nt, bk=bk: s.copy(
                                xi_ext[mc][:, 3 + nt * 512:3 + (nt + 1) * 512],
                                ps[bk][:, :]), sem="aq")
                    else:
                        if nt == 0:
                            wait("scalar", "ssz", C["ssz"])
                        p1["bank_free"][bp] = aop(
                            lambda s, nt=nt, bk=bk: s.activation(
                                out=sz_st[:, nt * 512:(nt + 1) * 512],
                                in_=ps[bk][:, :], func=AF.Silu), sem="aq")
                if mg >= 2:
                    wait("scalar", "aq", C["aq"])
                    dma("scalar",
                        d_silz[mc * 128:(mc + 1) * 128, gt * GTW:(gt + 1) * GTW],
                        sz_st[:, 0:1024], sem="ssz")
            xi_aq = C["aq"]
            p1["rhs_free"][bu] = C["mm"]
            for mc in range(2):
                wait("vector", "aq", xi_aq)
                src = xi_ext[mc]
                vop(lambda v, src=src, mc=mc: v.tensor_scalar(
                    out=cv_a, in0=src[:, 0:1024],
                    scalar1=convw_s[:, mc * 4:mc * 4 + 1],
                    scalar2=None, op0=ALU.mult))
                vdr()
                for k in range(1, 4):
                    a_buf, b_buf = (cv_a, cv_b) if k % 2 == 1 else (cv_b, cv_a)
                    vop(lambda v, src=src, mc=mc, k=k, a_buf=a_buf, b_buf=b_buf:
                        v.scalar_tensor_tensor(
                            out=b_buf, in0=src[:, k:k + 1024],
                            scalar=convw_s[:, mc * 4 + k:mc * 4 + k + 1],
                            in1=a_buf, op0=ALU.mult, op1=ALU.add))
                    vdr()
                if gt != NGT - 1:
                    if gt == 7:
                        vop(lambda v, src=src: v.memset(src[:, 0:3], 0.0))
                    else:
                        vop(lambda v, src=src: v.tensor_copy(src[:, 0:3],
                                                             src[:, 1024:1027]))
                vdr()
                cv_vv = vop(lambda v: v.nop(), sem="vv")
                wait("scalar", "vv", cv_vv)
                wait("scalar", "sxc", C["sxc"])
                xc_aq = aop(lambda s, mc=mc: s.activation(
                    out=xc_t[mc][:, :], in_=cv_b, func=AF.Silu,
                    bias=convb_s[:, mc:mc + 1]), sem="aq")
                wait("scalar", "aq", xc_aq)
                dma("scalar", d_xc[mc * 128:(mc + 1) * 128, gt * GTW:(gt + 1) * GTW],
                    xc_t[mc][:, :], sem="sxc")
                wait("vector", "aq", xc_aq)
            p1["xi_free"] = C["vv"]
            wait("tensor", "aq", C["aq"])
            for nt in range(2):
                bk = 6 + nt
                sarn = "sar" if nt == 0 else "sar1"
                wait("tensor", "aq", p1["bank_free"][nt])
                mms = []
                for mc in range(2):
                    mms.append(lambda t, mc=mc, nt=nt, bk=bk: t.matmul(
                        ps[bk][0:96, :], xpw_s[:, mc * 96:(mc + 1) * 96],
                        xc_t[mc][:, nt * 512:(nt + 1) * 512],
                        start=(mc == 0), stop=(mc == 1)))
                mmv = mm_group(mms)
                wait("scalar", "mm", mmv)
                wait("scalar", sarn, C[sarn])
                p1["bank_free"][nt] = aop(lambda s, bk=bk, nt=nt: s.copy(
                    po_st[nt][0:96, :], ps[bk][0:96, :]), sem="aq")
                wait("scalar", "aq", p1["bank_free"][nt])
                dma("scalar",
                    ar_in[:, gt * GTW + nt * 512:gt * GTW + (nt + 1) * 512],
                    po_st[nt][0:96, :], sem=sarn)

        sc = {
            "dt_mm": 0, "tl_ld": {}, "du_vv": {}, "dl_ld": {},
            "dA_aq": {}, "scan_vv": {}, "state_gg": {}, "hC_gg": {},
            "yacc_vv": {}, "fin_vv": {}, "bc_ld": {},
            "yop_rd_mm": {}, "b67": [0, 0], "opw_ld": 0,
        }

        def emit_scan_loads(j):
            if j >= 16:
                return
            par = j % 2
            cols = slice(j * GTW, (j + 1) * GTW)
            if j == 0:
                wait("scalar", "smk", C["smk"])
            if j >= 2:
                wait("scalar", "vv", sc["fin_vv"][j - 2])
            for mc in range(2):
                dma("scalar", xc2_t[par][mc], d_xc[mc * 128:(mc + 1) * 128, cols],
                    sem="stl%d" % par)
            sc["tl_ld"][j] = C["stl%d" % par]

        def emit_dt_burst():
            wait("scalar", "cc", 2)
            wait("scalar", "sxc", C["sxc"])
            wait("scalar", "ssz", C["ssz"])
            dma("gpsimd", opw_s[:, 0:D], opwT_sl[0:128, :], sem="gp1")
            dma("gpsimd", opw_s[:, D:2 * D], opwT_sl[128:256, :], sem="gp1")
            sc["opw_ld"] = C["gp1"]
            wait("tensor", "gp1", sc["opw_ld"])
            for jj in range(16):
                cols = slice(jj * GTW, (jj + 1) * GTW)
                wait("scalar", "mm", sc["dt_mm"])
                dma("scalar", dt_sb[:, :], ar_out[0:64, cols], sem="stl0")
                dtld = C["stl0"]
                wait("tensor", "stl0", dtld)
                for d2 in range(2):
                    for nt in range(2):
                        bp = nt
                        bk = 6 + bp
                        wait("tensor", "aq", sc["b67"][bp])
                        mmv = mm_group([lambda t, d2=d2, nt=nt, bk=bk: t.matmul(
                            ps[bk][:, :], dtw_s[:, d2 * 128:(d2 + 1) * 128],
                            dt_sb[:, nt * 512:(nt + 1) * 512],
                            start=True, stop=True)])
                        wait("scalar", "mm", mmv)
                        wait("scalar", "smk", C["smk"])
                        sc["b67"][bp] = aop(lambda s, d2=d2, bk=bk: s.activation(
                            out=es_t[:, :], in_=ps[bk][:, :], func=AF.Exp,
                            bias=dtb_s[:, d2:d2 + 1]), sem="aq")
                        adr()
                        aop(lambda s: s.activation(
                            out=dstage[:, :], in_=es_t[:, :], func=AF.Ln,
                            bias=cst_s[:, 1:2]))
                        adr()
                        dma("scalar",
                            d_delta[d2 * 128:(d2 + 1) * 128,
                                    jj * GTW + nt * 512:jj * GTW + (nt + 1) * 512],
                            dstage[:, :], sem="smk")
                sc["dt_mm"] = C["mm"]

        def emit_scan_tile(j):
            par = j % 2
            st0 = (j == 0 or j == 8)
            last_tile = (j == 7 or j == 15)
            if j >= 1:
                wait("scalar", "vv", sc["fin_vv"][j - 1])
            cols_j = slice(j * GTW, (j + 1) * GTW)
            for mc in range(2):
                dma("scalar", sz2_t[mc], d_silz[mc * 128:(mc + 1) * 128, cols_j],
                    sem="stl%d" % par)
            sc["sz_ld"] = sc.get("sz_ld", {})
            sc["sz_ld"][j] = C["stl%d" % par]
            tl = max(sc["tl_ld"][j], sc["dl_ld"][j])
            wait("vector", "stl%d" % par, tl)
            wait("scalar", "stl%d" % par, tl)
            for d2 in range(2):
                vop(lambda v, d2=d2, par=par: v.tensor_tensor(
                    out=du_t[d2], in0=delta[d2], in1=xc2_t[par][d2],
                    op=ALU.mult))
            vdr()
            sc["du_vv"][j] = vop(lambda v: v.nop(), sem="vv")
            if dbg and j == 0:
                wait("scalar", "vv", sc["du_vv"][j])
                for d2 in range(2):
                    dma("scalar", dbg_sc[0 + d2, :, :], delta[d2], sem="smk")
                    dma("scalar", dbg_sc[2 + d2, :, :], du_t[d2], sem="smk")
            for n in range(NST):
                nb = n % 4
                for d2 in range(2):
                    ab = (2 * n + d2) % 4
                    if (j, n - 2, d2) in sc["scan_vv"]:
                        wait("scalar", "vv", sc["scan_vv"][(j, n - 2, d2)])
                    elif n < 2 and (j - 1, NST - 2 + n, d2) in sc["scan_vv"]:
                        wait("scalar", "vv", sc["scan_vv"][(j - 1, NST - 2 + n, d2)])
                    sc["dA_aq"][(j, n, d2)] = aop(
                        lambda s, n=n, d2=d2, ab=ab: s.activation(
                            out=dA_pp[ab], in_=delta[d2], func=AF.Exp,
                            scale=a_s[:, d2 * 16 + n:d2 * 16 + n + 1]), sem="aq")
                wait("vector", "sb%d" % nb, sc["bc_ld"][(j, n)])
                for d2 in range(2):
                    vop(lambda v, d2=d2, nb=nb: v.tensor_tensor(
                        out=dBu_t[d2], in0=du_t[d2], in1=Bnb[nb], op=ALU.mult))
                vdr()
                hp = n % 2
                for d2 in range(2):
                    ab = (2 * n + d2) % 4
                    wait("vector", "aq", sc["dA_aq"][(j, n, d2)])
                    for key in [(j, n - 2, d2), (j - 1, NST - 2 + n, d2)]:
                        if key in sc["state_gg"]:
                            wait("vector", "gg", sc["state_gg"][key])
                            break
                        if key in sc["hC_gg"]:
                            wait("vector", "gg", sc["hC_gg"][key])
                            break
                    if st0:
                        sc["scan_vv"][(j, n, d2)] = vop(
                            lambda v, d2=d2, ab=ab, hp=hp: v.tensor_tensor_scan(
                                out=hs_t[hp][d2], data0=dA_pp[ab], data1=dBu_t[d2],
                                initial=0.0, op0=ALU.mult, op1=ALU.add), sem="vv")
                    else:
                        col = state_s[:, d2 * 16 + n:d2 * 16 + n + 1]
                        sc["scan_vv"][(j, n, d2)] = vop(
                            lambda v, d2=d2, col=col, ab=ab, hp=hp:
                            v.tensor_tensor_scan(
                                out=hs_t[hp][d2], data0=dA_pp[ab], data1=dBu_t[d2],
                                initial=col, op0=ALU.mult, op1=ALU.add), sem="vv")
                if n >= 1:
                    for d2 in range(2):
                        wait("vector", "gg", sc["hC_gg"][(j, n - 1, d2)])
                        if n == 1:
                            sc["yacc_vv"][(j, 0, d2)] = vop(
                                lambda v, d2=d2: v.tensor_copy(
                                    y_t[d2], hC_t[d2]), sem="vv")
                        else:
                            sc["yacc_vv"][(j, n - 1, d2)] = vop(
                                lambda v, d2=d2: v.tensor_tensor(
                                    out=y_t[d2], in0=y_t[d2], in1=hC_t[d2],
                                    op=ALU.add), sem="vv")
                for d2 in range(2):
                    wait("gpsimd", "vv", sc["scan_vv"][(j, n, d2)])
                    if (j, n - 1, d2) in sc["yacc_vv"]:
                        wait("gpsimd", "vv", sc["yacc_vv"][(j, n - 1, d2)])
                    sc["hC_gg"][(j, n, d2)] = gop(
                        lambda g, d2=d2, nb=nb, hp=hp: g.tensor_tensor(
                            out=hC_t[d2], in0=hs_t[hp][d2], in1=Cnb[nb],
                            op=ALU.mult), sem="gg")
                    if not last_tile:
                        sc["state_gg"][(j, n, d2)] = gop(
                            lambda g, n=n, d2=d2, hp=hp: g.tensor_copy(
                                state_s[:, d2 * 16 + n:d2 * 16 + n + 1],
                                hs_t[hp][d2][:, 1023:1024]), sem="gg")
                if n <= NST - 3:
                    if n >= 2:
                        emit_bc_load(j, n + 2, j, n - 2)
                    else:
                        emit_bc_load(j, n + 2, j - 1, n + 14)
            for d2 in range(2):
                wait("vector", "gg", sc["hC_gg"][(j, NST - 1, d2)])
                sc["yacc_vv"][(j, NST - 1, d2)] = vop(
                    lambda v, d2=d2: v.tensor_tensor(
                        out=y_t[d2], in0=y_t[d2], in1=hC_t[d2], op=ALU.add),
                    sem="vv")
            vdr()
            if j + 1 < 16:
                emit_bc_load(j + 1, 0, j, NST - 4)
                emit_bc_load(j + 1, 1, j, NST - 3)
                wait("scalar", "vv", sc["du_vv"][j])
                colsn = slice((j + 1) * GTW, (j + 2) * GTW)
                for mc in range(2):
                    dma("scalar", delta[mc],
                        d_delta[mc * 128:(mc + 1) * 128, colsn],
                        sem="stl%d" % ((j + 1) % 2))
                sc["dl_ld"][j + 1] = C["stl%d" % ((j + 1) % 2)]
            if j >= 1:
                wait_lazy("vector", "mm",
                          lambda jj=j: sc["yop_rd_mm"].get(jj - 1, 0))
            wait("vector", "stl%d" % par, sc["sz_ld"][j])
            for d2 in range(2):
                if not last_tile:
                    wait("vector", "gg", sc["state_gg"][(j, NST - 1, d2)])
                wait("vector", "gg", sc["hC_gg"][(j, NST - 1, d2)])
                vop(lambda v, d2=d2, par=par: v.scalar_tensor_tensor(
                    out=hs_t[0][d2], in0=xc2_t[par][d2],
                    scalar=dp_s[:, d2:d2 + 1],
                    in1=y_t[d2], op0=ALU.mult, op1=ALU.add))
            vdr()
            for d2 in range(2):
                vop(lambda v, d2=d2: v.tensor_tensor(
                    out=yop_t[d2], in0=hs_t[0][d2], in1=sz2_t[d2], op=ALU.mult))
            vdr()
            sc["fin_vv"][j] = vop(lambda v: v.nop(), sem="vv")
            if dbg and j == 0:
                wait("scalar", "vv", sc["fin_vv"][j])
                for d2 in range(2):
                    dma("scalar", dbg_sc[4 + d2, :, :], yop_t[d2], sem="smk")
                    dma("scalar", dbg_sc[6 + d2, :, :], xc2_t[par][d2], sem="smk")

        def emit_bc_load(jn, n, gj, gn):
            nb = n % 4
            if (gj, gn, 1) in sc["scan_vv"]:
                wait("scalar", "vv", sc["scan_vv"][(gj, gn, 1)])
            if (gj, gn, 1) in sc["hC_gg"]:
                wait("scalar", "gg", sc["hC_gg"][(gj, gn, 1)])
            cols = slice(jn * GTW, (jn + 1) * GTW)
            dma("scalar", Bnb[nb],
                d_bc.ap()[n:n + 1, cols].partition_broadcast(128)[:, 0, :],
                sem="sb%d" % nb)
            dma("scalar", Cnb[nb],
                d_bc.ap()[16 + n:17 + n, cols].partition_broadcast(128)[:, 0, :],
                sem="sb%d" % nb)
            sc["bc_ld"][(jn, n)] = C["sb%d" % nb]

        def emit_bc_preload(j):
            wait("scalar", "gbc", C["gbc"])
            for n in range(2):
                emit_bc_load(j, n, -1, 0)

        def emit_out_proj(j):
            wait("tensor", "vv", sc["fin_vv"][j])
            shard = j // 2
            lc0 = (j % 2) * 1024
            for mc8 in range(8):
                for nt in range(2):
                    bp = (mc8 * 2 + nt) % 2
                    bk = 6 + bp
                    wait("tensor", "aq", sc["b67"][bp])
                    mms = []
                    for d2 in range(2):
                        mms.append(lambda t, d2=d2, mc8=mc8, nt=nt, bk=bk:
                                   t.matmul(
                                       ps[bk][:, :],
                                       opw_s[:, d2 * D + mc8 * 128:
                                             d2 * D + (mc8 + 1) * 128],
                                       yop_t[d2][:, nt * 512:(nt + 1) * 512],
                                       start=(d2 == 0), stop=(d2 == 1)))
                    mmv = mm_group(mms)
                    wait("scalar", "mm", mmv)
                    wait("scalar", "srs%d" % bp, C["srs%d" % bp])
                    ev = aop(lambda s, bk=bk, bp=bp: s.copy(
                        po2[bp][:, :], ps[bk][:, :]), sem="aq")
                    sc["b67"][bp] = ev
                    wait("scalar", "aq", ev)
                    dma("scalar",
                        rs_in[shard, mc8 * 128:(mc8 + 1) * 128,
                              lc0 + nt * 512:lc0 + (nt + 1) * 512],
                        po2[bp][:, :], sem="srs%d" % bp)
            sc["yop_rd_mm"][j] = C["mm"]

        scan_at = {}
        for jj in range(16):
            scan_at.setdefault(9 + jj * 11 // 8, []).append(jj)

        for cyc in range(32):
            emit_moe_chunk(cyc)
            if 4 <= cyc <= 7:
                for k in range(4):
                    emit_p1_gt((cyc - 4) * 4 + k)
                if cyc == 7:
                    wait("gpsimd", "sar", C["sar"])
                    wait("gpsimd", "sar1", C["sar1"])
                    em("gpsimd", lambda g: g.collective_compute(
                        "AllReduce", ALU.add, replica_groups=[list(range(NCORES))],
                        ins=[ar_in.ap().opt()],
                        outs=[ar_out.ap().opt()]).then_inc(S["cc"], 1))
                    bump("cc")
                    wait("gpsimd", "cc", 2)
                    dma("gpsimd", d_bc[:, :], ar_out[64:96, :], sem="gbc")
                    if dbg:
                        dma("gpsimd", dbg_ar[:, :], ar_out[:, :], sem="gbc")
            if cyc == 8:
                sc["b67"] = list(p1["bank_free"])
                emit_dt_burst()
                wait("scalar", "smk", C["smk"])
                for mc in range(2):
                    dma("scalar", delta[mc], d_delta[mc * 128:(mc + 1) * 128, 0:GTW],
                        sem="stl0")
                sc["dl_ld"][0] = C["stl0"]
                emit_scan_loads(0)
                emit_bc_preload(0)
                emit_scan_loads(1)
            for jj in scan_at.get(cyc, []):
                emit_scan_tile(jj)
                if jj >= 1:
                    emit_out_proj(jj - 1)
                emit_scan_loads(jj + 2)
        emit_out_proj(15)
        rs_total0 = C["srs0"]
        rs_total1 = C["srs1"]

        wait("gpsimd", "srs0", rs_total0)
        wait("gpsimd", "srs1", rs_total1)
        em("gpsimd", lambda g: g.collective_compute(
            "ReduceScatter", ALU.add, replica_groups=[list(range(NCORES))],
            ins=[rs_in.ap().opt()], outs=[rs_out.ap().opt()]).then_inc(S["cc"], 1))
        bump("cc")

        wait("sync", "cc", 3)
        wait("sync", "mm", moe_state["mm_total"])
        if dbg:
            dma("sync", dbg_rs[:, :], rs_out[:, :], sem="sf0")
            dma("sync", dbg_rsp[:, :], rs_in[0, :, :], sem="sf0")
            dma("sync", dbg_bc[:, :], d_bc[:, :], sem="sf0")
            wait("sync", "vv", C["vv"])
            dma("sync", dbg_moe[:, :], out_acc[:, :], sem="sf0")
        dma("sync", rstd_b, d_rows.ap()[0:1, :].partition_broadcast(128)[:, 0, :],
            sem="sf0")
        dma("sync", nmr_b, d_rows.ap()[1:2, :].partition_broadcast(128)[:, 0, :],
            sem="sf0")
        p3b_ld = C["sf0"]
        last_acc_vv = C["vv"]
        f_ld = [p3b_ld, 0]
        f_free = [0, 0]
        f_st = [0, 0]
        i_t = 0
        for tt in range(4):
            for dc in range(8):
                bu = i_t % 2
                rsl = slice(dc * 128, (dc + 1) * 128)
                csl = slice(tt * 512, (tt + 1) * 512)
                wait("sync", "vv", f_free[bu])
                dma("sync", f_rs[bu], rs_out[rsl, csl], sem="sf%d" % bu)
                dma("sync", f_x[bu], xT[rsl, csl], sem="sf%d" % bu)
                f_ld[bu] = C["sf%d" % bu]
                asl = slice(dc * 2048 + tt * 512, dc * 2048 + (tt + 1) * 512)
                wait("scalar", "vv", max(f_free[bu], last_acc_vv))
                g_aq = aop(lambda s, asl=asl, bu=bu: s.activation(
                    out=f_g[bu], in_=out_acc[:, asl], func=AF.Silu), sem="aq")
                wait("vector", "sf%d" % bu, max(f_ld[bu], f_st[bu]))
                wait("vector", "aq", g_aq)
                vop(lambda v, bu=bu, csl=csl: v.tensor_tensor(
                    out=f_s[bu], in0=f_x[bu], in1=rstd_b[:, csl], op=ALU.mult))
                vdr()
                vop(lambda v, bu=bu, csl=csl: v.tensor_tensor(
                    out=f_s[bu], in0=f_s[bu], in1=nmr_b[:, csl], op=ALU.add))
                vdr()
                vop(lambda v, bu=bu: v.tensor_tensor(
                    out=f_s[bu], in0=f_s[bu], in1=f_rs[bu], op=ALU.add))
                vdr()
                vop(lambda v, bu=bu: v.tensor_tensor(
                    out=f_s[bu], in0=f_s[bu], in1=f_g[bu], op=ALU.mult))
                vdr()
                o_vv = vop(lambda v, bu=bu: v.tensor_tensor(
                    out=f_s[bu], in0=f_s[bu], in1=f_x[bu], op=ALU.add), sem="vv")
                f_free[bu] = o_vv
                wait("sync", "vv", o_vv)
                dma("sync", out_d[rsl, csl], f_s[bu], sem="sf%d" % bu)
                f_st[bu] = C["sf%d" % bu]
                i_t += 1
        wait("sync", "sf0", C["sf0"])
        wait("sync", "sf1", C["sf1"])

        @block.tensor
        def _(t):
            for fn in prog["tensor"]:
                fn(t)

        @block.vector
        def _(v):
            for fn in prog["vector"]:
                fn(v)

        @block.scalar
        def _(s):
            for fn in prog["scalar"]:
                fn(s)

        @block.gpsimd
        def _(g):
            for fn in prog["gpsimd"]:
                fn(g)

        @block.sync
        def _(sy):
            for fn in prog["sync"]:
                fn(sy)

    return nc


_NC_CACHE = None


def _get_nc():
    global _NC_CACHE
    if _NC_CACHE is None:
        _NC_CACHE = build_nc()
    return _NC_CACHE


def make_in_maps(L0_attn_x, L0_attn_norm_w, L0_attn_in_proj_w, L0_attn_conv_w,
                 L0_attn_conv_b, L0_attn_x_proj_w, L0_attn_dt_proj_w,
                 L0_attn_dt_proj_b, L0_attn_A_log, L0_attn_Dp, L0_attn_out_proj_w,
                 L0_ffn_gate_w, L0_ffn_gate_b, L0_ffn_w1, L0_ffn_b1, L0_ffn_w2,
                 L0_ffn_b2):
    f = np.float32
    x = np.asarray(L0_attn_x, f)
    w1 = np.asarray(L0_ffn_w1, f)
    w1r = np.ascontiguousarray(
        w1.reshape(E, 32, 128, 8, 128).transpose(0, 4, 1, 3, 2)
        .reshape(E, 128, 32768))
    shared = {
        "ident": np.eye(128, dtype=f),
        "idx128": np.tile(np.arange(E, dtype=f), (128, 1)),
        "gwT": np.ascontiguousarray(np.asarray(L0_ffn_gate_w, f).T),
        "gb": np.asarray(L0_ffn_gate_b, f).reshape(1, E),
        "normw_r": np.ascontiguousarray(np.asarray(L0_attn_norm_w, f).reshape(8, 128).T),
        "b1T": np.ascontiguousarray(
            np.asarray(L0_ffn_b1, f).reshape(E, 32, 128).transpose(2, 0, 1).reshape(128, E * 32)),
        "b2T": np.ascontiguousarray(
            np.asarray(L0_ffn_b2, f).reshape(E, 8, 128).transpose(2, 0, 1).reshape(128, E * 8)),
        "w1r": w1r,
        "w2T": np.ascontiguousarray(np.asarray(L0_ffn_w2, f).transpose(0, 2, 1)),
    }
    ipw = np.asarray(L0_attn_in_proj_w, f)
    cw = np.asarray(L0_attn_conv_w, f)
    cb = np.asarray(L0_attn_conv_b, f)
    xpw = np.asarray(L0_attn_x_proj_w, f)
    dtp = np.asarray(L0_attn_dt_proj_w, f)
    dtb = np.asarray(L0_attn_dt_proj_b, f)
    alog = np.asarray(L0_attn_A_log, f)
    dpv = np.asarray(L0_attn_Dp, f)
    opw = np.asarray(L0_attn_out_proj_w, f)

    in_maps = []
    for c in range(NCORES):
        b_, q = c // 4, c % 4
        sl = slice(DSL * c, DSL * (c + 1))
        m = dict(shared)
        m["xT"] = np.ascontiguousarray(x[b_, q * TOK:(q + 1) * TOK, :].T)
        m["wxiT"] = np.ascontiguousarray(ipw[sl, :].T)
        m["wzT"] = np.ascontiguousarray(ipw[DI + DSL * c:DI + DSL * (c + 1), :].T)
        m["convw_r"] = np.ascontiguousarray(
            cw[sl].reshape(2, 128, 4).transpose(1, 0, 2).reshape(128, 8))
        m["convb_r"] = np.ascontiguousarray(cb[sl].reshape(2, 128).T)
        m["xpwT"] = np.ascontiguousarray(xpw[:, sl].T)
        m["dtw"] = np.ascontiguousarray(dtp[sl, :].T)
        m["dtb_r"] = np.ascontiguousarray(dtb[sl].reshape(2, 128).T)
        m["alog_r"] = np.ascontiguousarray(
            alog[sl].reshape(2, 128, NST).transpose(1, 0, 2).reshape(128, 32))
        m["dp_r"] = np.ascontiguousarray(dpv[sl].reshape(2, 128).T)
        m["opwT_sl"] = np.ascontiguousarray(opw[:, sl].T)
        in_maps.append(m)
    return in_maps


def kernel(**inputs):
    in_maps = make_in_maps(**inputs)
    nc = _get_nc()
    res = run_bass_kernel_spmd(nc, in_maps, core_ids=list(range(NCORES)))
    out = np.empty((B, L, D), dtype=np.float32)
    for c in range(NCORES):
        b_, q = c // 4, c % 4
        out[b_, q * TOK:(q + 1) * TOK, :] = res.results[c]["out"].T
    return out


# version 10
# speedup vs baseline: 1.0787x; 1.0122x over previous
import numpy as np
from contextlib import ExitStack

import concourse.bass as bass
import concourse.mybir as mybir
from concourse.bass_utils import run_bass_kernel_spmd

F32 = mybir.dt.float32
BF16 = mybir.dt.bfloat16
AF = mybir.ActivationFunctionType
ALU = mybir.AluOpType
AX = mybir.AxisListType

NCORES = 8
B, L, D = 2, 8192, 1024
DI, NST, DTR = 2048, 16, 64
E, HID = 8, 4096
TOK = 2048
GT = B * L
DSL = DI // NCORES
NGT = 16
GTW = GT // NGT
EPS = 1e-5
BIGNEG = 100.0


def build_nc(dbg=False):
    nc = bass.Bass("TRN2", target_bir_lowering=False, debug=False)

    def din(name, shape, dt=F32):
        return nc.dram_tensor(name, shape, dt, kind="ExternalInput")

    dkind = "ExternalOutput" if dbg else "Internal"

    xT = din("xT", [D, TOK])
    ident = din("ident", [128, 128])
    idx128 = din("idx128", [128, E])
    gwT = din("gwT", [D, E])
    gb = din("gb", [1, E])
    normw_r = din("normw_r", [128, 8])
    wxiT = din("wxiT", [D, DSL])
    wzT = din("wzT", [D, DSL])
    convw_r = din("convw_r", [128, 8])
    convb_r = din("convb_r", [128, 2])
    xpwT = din("xpwT", [DSL, 96])
    dtw = din("dtw", [DTR, DSL])
    dtb_r = din("dtb_r", [128, 2])
    alog_r = din("alog_r", [128, 32])
    dp_r = din("dp_r", [128, 2])
    opwT_sl = din("opwT_sl", [DSL, D])
    b1T = din("b1T", [128, E * 32])
    b2T = din("b2T", [128, E * 8])
    w1r = din("w1r", [E, 128, 32768])
    w2T = din("w2T", [E, HID, D])

    out_d = nc.dram_tensor("out", [D, TOK], F32, kind="ExternalOutput")

    d_xnbf = nc.dram_tensor("d_xnbf", [D, TOK], BF16, kind=dkind)
    d_m = nc.dram_tensor("d_m", [E, TOK], BF16, kind=dkind)
    d_pm = nc.dram_tensor("d_pm", [E, TOK], BF16, kind=dkind)
    d_rows = nc.dram_tensor("d_rows", [4, TOK], F32)
    ag1_in = nc.dram_tensor("ag1_in", [D, TOK], BF16)
    ag1_out = nc.dram_tensor("ag1_out", [NCORES * D, TOK], BF16)
    ar_in = nc.dram_tensor("ar_in", [96, GT], F32)
    ar_out = nc.dram_tensor("ar_out", [96, GT], F32)
    d_bc = nc.dram_tensor("d_bc", [32, GT], BF16)
    d_xc = nc.dram_tensor("d_xc", [DSL, GT], BF16, kind=dkind)
    d_delta = nc.dram_tensor("d_delta", [DSL, GT], BF16)
    d_silz = nc.dram_tensor("d_silz", [DSL, GT], BF16, kind=dkind)
    d_w1bf = nc.dram_tensor("d_w1bf", [E, 128, 32768], BF16)
    d_w2bf = nc.dram_tensor("d_w2bf", [E, HID, D], BF16)
    rs_in = nc.dram_tensor("rs_in", [NCORES, D, TOK], F32)
    rs_out = nc.dram_tensor("rs_out", [D, TOK], F32)
    if dbg:
        dbg_ar = nc.dram_tensor("dbg_ar", [96, GT], F32, kind="ExternalOutput")
        dbg_rs = nc.dram_tensor("dbg_rs", [D, TOK], F32, kind="ExternalOutput")
        dbg_moe = nc.dram_tensor("dbg_moe", [128, 16384], BF16, kind="ExternalOutput")
        dbg_rsp = nc.dram_tensor("dbg_rsp", [D, TOK], F32, kind="ExternalOutput")
        dbg_bc = nc.dram_tensor("dbg_bc", [32, GT], BF16, kind="ExternalOutput")
        dbg_sc = nc.dram_tensor("dbg_sc", [8, 128, 1024], BF16, kind="ExternalOutput")
        dbg_m2 = nc.dram_tensor("dbg_m2", [4, 128, 4096], BF16, kind="ExternalOutput")

    with ExitStack() as stk:
        EE = stk.enter_context

        A1 = EE(nc.sbuf_tensor("A1", [128, 12288], BF16))
        A2 = EE(nc.sbuf_tensor("A2", [128, 16384], BF16))
        A3 = EE(nc.sbuf_tensor("A3", [128, 16384], BF16))
        A4 = EE(nc.sbuf_tensor("A4", [128, 43008], BF16))
        AM = EE(nc.sbuf_tensor("AM", [128, 11264], BF16))
        AW2 = EE(nc.sbuf_tensor("AW2", [128, 2048], BF16))
        opw_s = EE(nc.sbuf_tensor("opw_s", [128, 2048], BF16))

        ones_s = EE(nc.sbuf_tensor("ones_s", [128, 1], F32))
        ident_s = EE(nc.sbuf_tensor("ident_s", [128, 128], F32))
        idx_s = EE(nc.sbuf_tensor("idx_s", [128, E], F32))
        gw_s = EE(nc.sbuf_tensor("gw_s", [128, 64], F32))
        gbb_s = EE(nc.sbuf_tensor("gbb_s", [128, E], F32))
        normw_s = EE(nc.sbuf_tensor("normw_s", [128, 8], F32))
        convw_s = EE(nc.sbuf_tensor("convw_s", [128, 8], F32))
        convb_s = EE(nc.sbuf_tensor("convb_s", [128, 2], F32))
        dtw_s = EE(nc.sbuf_tensor("dtw_s", [64, DSL], F32))
        dtb_s = EE(nc.sbuf_tensor("dtb_s", [128, 2], F32))
        alog_s = EE(nc.sbuf_tensor("alog_s", [128, 32], F32))
        a_s = EE(nc.sbuf_tensor("a_s", [128, 32], F32))
        dp_s = EE(nc.sbuf_tensor("dp_s", [128, 2], F32))
        b1_s = EE(nc.sbuf_tensor("b1_s", [128, E * 32], F32))
        b2_s = EE(nc.sbuf_tensor("b2_s", [128, E * 8], F32))
        state_s = EE(nc.sbuf_tensor("state_s", [128, 32], BF16))
        cst_s = EE(nc.sbuf_tensor("cst_s", [128, 2], F32))

        ps = [EE(nc.psum_tensor(f"psb{i}", [128, 512], F32)) for i in range(8)]

        sem_names = ["s0", "mm", "vv", "aq", "gg", "cc", "gw1p", "gw2p",
                     "gbc", "smk", "w1r0", "w1r1", "w1r2", "w1r3", "w1r4", "w1r5",
                     "w2r0", "w2r1", "w2r2", "w2r3", "xnb", "pmm",
                     "sr0", "sr1", "sar", "stl0", "stl1",
                     "sb0", "sb1", "sb2", "sb3", "srs0", "srs1",
                     "sf0", "sf1", "sxc", "gp1", "sar1", "ssz"]
        S = {}
        for name in sem_names:
            S[name] = EE(nc.semaphore(name))
        C = {k: 0 for k in S}

        block = EE(nc.Block())
        prog = {"tensor": [], "vector": [], "scalar": [], "gpsimd": [], "sync": []}

        def em(eng, fn):
            prog[eng].append(fn)

        def bump(name, by=1):
            C[name] += by
            return C[name]

        def dma(eng_name, out_ap, in_ap, sem="s0"):
            h = S[sem]
            t = bump(sem, 16)

            def fn(e, out_ap=out_ap, in_ap=in_ap, h=h):
                e.dma_start(out=out_ap, in_=in_ap).then_inc(h, 16)
            em(eng_name, fn)
            return t

        def wait(eng_name, sem, tgt):
            if tgt <= 0:
                return
            h = S[sem]

            def fn(e, h=h, tgt=tgt):
                e.wait_ge(h, tgt)
            em(eng_name, fn)

        def wait_lazy(eng_name, sem, tgt_fn):
            h = S[sem]

            def fn(e, h=h, tgt_fn=tgt_fn):
                t = tgt_fn()
                if t > 0:
                    e.wait_ge(h, t)
            em(eng_name, fn)

        def vop(fb, sem=None):
            if sem is None:
                em("vector", lambda v, f=fb: f(v))
                return None
            h = S[sem]
            t = bump(sem)

            def fn(v, f=fb, h=h):
                f(v).then_inc(h, 1)
            em("vector", fn)
            return t

        def aop(fb, sem=None):
            if sem is None:
                em("scalar", lambda s, f=fb: f(s))
                return None
            h = S[sem]
            t = bump(sem)

            def fn(s, f=fb, h=h):
                f(s).then_inc(h, 1)
            em("scalar", fn)
            return t

        def gop(fb, sem=None):
            if sem is None:
                em("gpsimd", lambda g, f=fb: f(g))
                return None
            h = S[sem]
            t = bump(sem)

            def fn(g, f=fb, h=h):
                f(g).then_inc(h, 1)
            em("gpsimd", fn)
            return t

        def vdr():
            em("vector", lambda v: v.drain())

        def adr():
            em("scalar", lambda s: s.drain())

        def mm_group(mms, sem="mm"):
            t = None
            for i, mb in enumerate(mms):
                if i == len(mms) - 1:
                    h = S[sem]
                    t = bump(sem)

                    def fn(tt_, f=mb, h=h):
                        f(tt_).then_inc(h, 1)
                    em("tensor", fn)
                else:
                    em("tensor", lambda tt_, f=mb: f(tt_))
            return t

        A1b, A2b, A3b, A4b = A1.ap(), A2.ap(), A3.ap(), A4.ap()
        AMb, AW2b = AM.ap(), AW2.ap()
        A1f = A1.ap().bitcast(F32)
        A2f = A2.ap().bitcast(F32)
        A3f = A3.ap().bitcast(F32)
        A4f = A4.ap().bitcast(F32)
        AMf = AM.ap().bitcast(F32)
        opwf = opw_s.ap().bitcast(F32)

        h_sb = A2b
        out_acc = A3b
        xe_sl = [AMb[:, 0:4096], AMb[:, 4096:8192]]
        pm_e = AMb[:, 8192:8704]
        m_e = AMb[:, 8704:9216]
        ystg = [AMb[:, 9216 + i * 512:9216 + (i + 1) * 512] for i in range(2)]
        t1_b = [AMb[:, 10240 + i * 512:10240 + (i + 1) * 512] for i in range(2)]
        w1sl = [A1b[:, i * 2048:(i + 1) * 2048] for i in range(6)]
        w2b = [AW2b[:, i * 512:(i + 1) * 512] for i in range(4)]

        delta = [A4b[:, d2 * 1024:(d2 + 1) * 1024] for d2 in range(2)]
        dA_pp = [A4f[:, 1024 + i * 1024:1024 + (i + 1) * 1024] for i in range(4)]
        y_t = [A4f[:, 5120 + d2 * 1024:5120 + (d2 + 1) * 1024] for d2 in range(2)]
        du_t = [A4b[:, 14336 + d2 * 1024:14336 + (d2 + 1) * 1024] for d2 in range(2)]
        dBu_t = [A4b[:, 16384 + d2 * 1024:16384 + (d2 + 1) * 1024] for d2 in range(2)]
        hs_t = [[A4b[:, 18432 + (np_ * 2 + d2) * 1024:
                     18432 + (np_ * 2 + d2 + 1) * 1024] for d2 in range(2)]
                for np_ in range(2)]
        hC_t = [A4b[:, 22528 + d2 * 1024:22528 + (d2 + 1) * 1024] for d2 in range(2)]
        Bnb = [A4b[:, 24576 + i * 1024:24576 + (i + 1) * 1024] for i in range(4)]
        Cnb = [A4b[:, 28672 + i * 1024:28672 + (i + 1) * 1024] for i in range(4)]
        xc2_t = [[A4b[:, 32768 + (par * 2 + d2) * 1024:
                      32768 + (par * 2 + d2 + 1) * 1024] for d2 in range(2)]
                 for par in range(2)]
        sz2_t = [A4b[:, 36864 + d2 * 1024:36864 + (d2 + 1) * 1024] for d2 in range(2)]
        yop_t = [A4b[:, 38912 + d2 * 1024:38912 + (d2 + 1) * 1024] for d2 in range(2)]
        po2 = [A4f[:, 20480 + i * 512:20480 + (i + 1) * 512] for i in range(2)]
        es_t = po2[0]
        dstage = A4b[:, 41984:42496]
        dt_sb = A4f[0:64, 16384:17408]

        def x_chunk(dc):
            if dc < 3:
                return A1f[:, dc * 2048:(dc + 1) * 2048]
            if dc < 7:
                return A2f[:, (dc - 3) * 2048:(dc - 2) * 2048]
            return A4f[:, 16384:18432]

        def xn_chunk(dc):
            return A4f[:, dc * 2048:(dc + 1) * 2048]

        rstd_b = AMf[:, 0:2048]
        nmr_b = AMf[:, 2048:4096]

        def rw(i):
            return A3f[0:1, i * 2048:(i + 1) * 2048]
        gscr = A3f[:, 0:1024]
        msb = A3b[0:16, 2048:4096]
        xnstg = [A3b[:, 4096 + i * 2048:4096 + (i + 1) * 2048] for i in range(2)]
        ag1stg = [A3b[:, 8192 + i * 2048:8192 + (i + 1) * 2048] for i in range(2)]

        rhs_t = [[A4b[:, (bu * 8 + k) * 1024:(bu * 8 + k + 1) * 1024]
                  for k in range(8)] for bu in range(2)]
        wxi_s = A4b[:, 16384:18432]
        wz_s = A4b[:, 18432:20480]
        xc_t = [A4b[:, 20480 + i * 1024:20480 + (i + 1) * 1024] for i in range(2)]
        sz_st = A4b[:, 22528:23552]
        xpw_s = A4b[:, 23552:23744]
        xi_ext = [A4f[:, 11872:12899], A4f[:, 12899:13926]]
        cv_a = A4f[:, 13926:14950]
        cv_b = A4f[:, 14950:15974]
        po_st = [opwf[:, 0:512], opwf[:, 512:1024]]

        f_rs = [A1f[:, i * 512:(i + 1) * 512] for i in range(2)]
        f_x = [A1f[:, 1024 + i * 512:1024 + (i + 1) * 512] for i in range(2)]
        f_g = [A1f[:, 2048 + i * 512:2048 + (i + 1) * 512] for i in range(2)]
        f_s = [A1f[:, 3072 + i * 512:3072 + (i + 1) * 512] for i in range(2)]

        vop(lambda v: v.memset(ones_s[:, :], 1.0))
        vop(lambda v: v.memset(state_s[:, :], 0.0))
        vop(lambda v: v.memset(cst_s[:, 0:1], EPS))
        cst_vv = vop(lambda v: v.memset(cst_s[:, 1:2], 1.0), sem="vv")
        wait("scalar", "vv", cst_vv)
        dma("sync", ident_s[:, :], ident[:, :])
        dma("sync", idx_s[:, :], idx128[:, :])
        for dc in range(8):
            dma("sync", gw_s[:, dc * 8:(dc + 1) * 8], gwT[dc * 128:(dc + 1) * 128, :])
        dma("sync", gbb_s[:, :], gb.ap().partition_broadcast(128)[:, 0, :])
        dma("sync", normw_s[:, :], normw_r[:, :])
        dma("sync", convw_s[:, :], convw_r[:, :])
        dma("sync", convb_s[:, :], convb_r[:, :])
        dma("sync", dtw_s[:, :], dtw[:, :])
        dma("sync", dtb_s[:, :], dtb_r[:, :])
        dma("sync", alog_s[:, :], alog_r[:, :])
        dma("sync", dp_s[:, :], dp_r[:, :])
        dma("sync", b1_s[:, :], b1T[:, :])
        dma("sync", b2_s[:, :], b2T[:, :])
        consts_ld = C["s0"]

        wait("scalar", "s0", consts_ld)
        a_aq = aop(lambda s: s.activation(out=a_s[:, :], in_=alog_s[:, :],
                                          func=AF.Exp), sem="aq")
        wait("vector", "aq", a_aq)
        vop(lambda v: v.tensor_scalar(out=a_s[:, :], in0=a_s[:, :], scalar1=-1.0,
                                      scalar2=None, op0=ALU.mult))
        vdr()

        w1p_mark = {}
        w2p_mark = {}

        def emit_precast(e_):
            for hf in range(2):
                dma("gpsimd", d_w1bf[e_, :, hf * 16384:(hf + 1) * 16384],
                    w1r[e_, :, hf * 16384:(hf + 1) * 16384], sem="gw1p")
            w1p_mark[e_] = C["gw1p"]
            for hc4 in range(0, 32, 8):
                dma("gpsimd", d_w2bf[e_, hc4 * 128:(hc4 + 8) * 128, :],
                    w2T[e_, hc4 * 128:(hc4 + 8) * 128, :], sem="gw2p")
            w2p_mark[e_] = C["gw2p"]

        emit_precast(0)
        emit_precast(1)

        for dc in range(8):
            dma("sync", x_chunk(dc), xT[dc * 128:(dc + 1) * 128, :])
        x_ld = C["s0"]

        wait("scalar", "s0", x_ld)
        for dc in range(8):
            aop(lambda s, dc=dc: s.activation(out=xn_chunk(dc), in_=x_chunk(dc),
                                              func=AF.Square))
        adr()
        sq_aq = aop(lambda s: s.nop(), sem="aq")

        wait("tensor", "s0", x_ld)
        wait("tensor", "aq", sq_aq)
        for nt in range(4):
            mms = []
            for dc in range(8):
                mms.append(lambda t, nt=nt, dc=dc: t.matmul(
                    ps[6][0:1, :], ones_s[:, :],
                    x_chunk(dc)[:, nt * 512:(nt + 1) * 512],
                    start=(dc == 0), stop=(dc == 7)))
            for dc in range(8):
                mms.append(lambda t, nt=nt, dc=dc: t.matmul(
                    ps[7][0:1, :], ones_s[:, :],
                    xn_chunk(dc)[:, nt * 512:(nt + 1) * 512],
                    start=(dc == 0), stop=(dc == 7)))
            mmv = mm_group(mms)
            wait("vector", "mm", mmv)
            vop(lambda v, nt=nt: v.tensor_copy(
                rw(0)[:, nt * 512:(nt + 1) * 512], ps[6][0:1, :]))
            bank_vv = vop(lambda v, nt=nt: v.tensor_copy(
                rw(1)[:, nt * 512:(nt + 1) * 512], ps[7][0:1, :]), sem="vv")
            wait("tensor", "vv", bank_vv)
        stats_mm = C["mm"]

        vop(lambda v: v.tensor_scalar(out=rw(0), in0=rw(0), scalar1=1.0 / D,
                                      scalar2=None, op0=ALU.mult))
        vdr()
        vop(lambda v: v.tensor_scalar(out=rw(1), in0=rw(1), scalar1=1.0 / D,
                                      scalar2=None, op0=ALU.mult))
        vdr()
        vop(lambda v: v.tensor_tensor(out=rw(2), in0=rw(0), in1=rw(0), op=ALU.mult))
        vdr()
        var_vv = vop(lambda v: v.tensor_tensor(out=rw(1), in0=rw(1), in1=rw(2),
                                               op=ALU.subtract), sem="vv")
        wait("scalar", "vv", var_vv)
        aop(lambda s: s.activation(out=rw(2), in_=rw(1), func=AF.Ln,
                                   bias=cst_s[0:1, 0:1]))
        adr()
        rstd_aq = aop(lambda s: s.activation(out=rw(2), in_=rw(2), func=AF.Exp,
                                             scale=-0.5), sem="aq")
        wait("vector", "aq", rstd_aq)
        vop(lambda v: v.tensor_tensor(out=rw(3), in0=rw(2), in1=rw(2), op=ALU.mult))
        vdr()
        t5_vv = vop(lambda v: v.tensor_tensor(out=rw(3), in0=rw(3), in1=rw(1),
                                              op=ALU.mult), sem="vv")
        wait("scalar", "vv", t5_vv)
        aop(lambda s: s.activation(out=rw(3), in_=rw(3), func=AF.Ln,
                                   bias=cst_s[0:1, 0:1]))
        adr()
        rrms_aq = aop(lambda s: s.activation(out=rw(3), in_=rw(3), func=AF.Exp,
                                             scale=-0.5), sem="aq")
        wait("vector", "aq", rrms_aq)
        nmr_vv = vop(lambda v: v.scalar_tensor_tensor(
            out=rw(0), in0=rw(0), scalar=-1.0, in1=rw(2), op0=ALU.mult,
            op1=ALU.mult), sem="vv")
        wait("sync", "vv", nmr_vv)
        dma("sync", d_rows[0:1, :], rw(2))
        dma("sync", d_rows[1:2, :], rw(0))
        dma("sync", d_rows[2:3, :], rw(3))
        rows_st = C["s0"]
        wait("sync", "s0", rows_st)
        dma("sync", rstd_b, d_rows.ap()[0:1, :].partition_broadcast(128)[:, 0, :])
        dma("sync", nmr_b, d_rows.ap()[1:2, :].partition_broadcast(128)[:, 0, :])
        bco_ld = C["s0"]

        wait("vector", "s0", bco_ld)
        wait("vector", "mm", stats_mm)
        for dc in range(8):
            vop(lambda v, dc=dc: v.tensor_tensor(
                out=xn_chunk(dc), in0=x_chunk(dc), in1=rstd_b, op=ALU.mult))
            vdr()
            vop(lambda v, dc=dc: v.tensor_tensor(
                out=xn_chunk(dc), in0=xn_chunk(dc), in1=nmr_b, op=ALU.add))
            vdr()
        xn_vv = vop(lambda v: v.nop(), sem="vv")

        wait("sync", "vv", xn_vv)
        dma("sync", rstd_b, d_rows.ap()[2:3, :].partition_broadcast(128)[:, 0, :])
        rrms_ld2 = C["s0"]
        wait("vector", "s0", rrms_ld2)
        rrms_b = rstd_b

        wait("vector", "s0", rows_st)
        stg_stq = [0, 0]
        for dc in range(8):
            bu = dc % 2
            if dc >= 2:
                wait("vector", "s0", C["s0"])
            vop(lambda v, dc=dc, bu=bu: v.tensor_copy(
                xnstg[bu][:, :], xn_chunk(dc)))
            st_vv = vop(lambda v, dc=dc, bu=bu: v.scalar_tensor_tensor(
                out=ag1stg[bu][:, :], in0=xn_chunk(dc),
                scalar=normw_s[:, dc:dc + 1], in1=rrms_b, op0=ALU.mult,
                op1=ALU.mult), sem="vv")
            wait("sync", "vv", st_vv)
            dma("sync", d_xnbf[dc * 128:(dc + 1) * 128, :], xnstg[bu][:, :])
            dma("sync", ag1_in[dc * 128:(dc + 1) * 128, :], ag1stg[bu][:, :])
            stg_stq[bu] = C["s0"]
        ag1in_st = C["s0"]

        wait("vector", "s0", rows_st)
        wait("tensor", "vv", xn_vv)
        for tc in range(16):
            mms = []
            for dc in range(8):
                mms.append(lambda t, tc=tc, dc=dc: t.matmul(
                    ps[5][:, 0:8],
                    xn_chunk(dc)[:, tc * 128:(tc + 1) * 128],
                    gw_s[:, dc * 8:(dc + 1) * 8], start=(dc == 0), stop=(dc == 7)))
            mmv = mm_group(mms)
            wait("vector", "mm", mmv)

            def gsl(a, b_):
                return gscr[:, a:b_]
            vop(lambda v: v.tensor_tensor(out=gsl(0, 8), in0=ps[5][:, 0:8],
                                          in1=gbb_s[:, :], op=ALU.add))
            vdr()
            bank_vv = vop(lambda v: v.nop(), sem="vv")
            wait("tensor", "vv", bank_vv)
            vop(lambda v: v.tensor_reduce(out=gsl(8, 9), in_=gsl(0, 8),
                                          axis=AX.X, op=ALU.max, negate=True))
            vdr()
            sm_vv = vop(lambda v: v.nop(), sem="vv")
            wait("scalar", "vv", sm_vv)
            e_aq = aop(lambda s: s.activation(out=gsl(0, 8), in_=gsl(0, 8),
                                              func=AF.Exp, bias=gsl(8, 9)),
                       sem="aq")
            wait("vector", "aq", e_aq)
            vop(lambda v: v.tensor_reduce(out=gsl(8, 9), in_=gsl(0, 8),
                                          axis=AX.X, op=ALU.add))
            vdr()
            vop(lambda v: v.reciprocal(gsl(9, 10), gsl(8, 9)))
            vdr()
            vop(lambda v: v.tensor_scalar(out=gsl(0, 8), in0=gsl(0, 8),
                                          scalar1=gsl(9, 10), scalar2=None,
                                          op0=ALU.mult))
            vdr()
            vop(lambda v: v.tensor_reduce(out=gsl(8, 9), in_=gsl(0, 8),
                                          axis=AX.X, op=ALU.max))
            vdr()
            vop(lambda v: v.tensor_scalar(out=gsl(10, 18), in0=gsl(0, 8),
                                          scalar1=gsl(8, 9), scalar2=None,
                                          op0=ALU.is_ge))
            vdr()
            vop(lambda v: v.scalar_tensor_tensor(
                out=gsl(18, 26), in0=gsl(10, 18), scalar=-BIGNEG, in1=gsl(0, 8),
                op0=ALU.mult, op1=ALU.add))
            vdr()
            vop(lambda v: v.tensor_reduce(out=gsl(8, 9), in_=gsl(18, 26),
                                          axis=AX.X, op=ALU.max))
            vdr()
            vop(lambda v: v.tensor_scalar(out=gsl(18, 26), in0=gsl(0, 8),
                                          scalar1=gsl(8, 9), scalar2=None,
                                          op0=ALU.is_ge))
            vdr()
            vop(lambda v: v.tensor_tensor(out=gsl(18, 26), in0=gsl(18, 26),
                                          in1=idx_s[:, :], op=ALU.mult))
            vdr()
            vop(lambda v: v.tensor_reduce(out=gsl(8, 9), in_=gsl(18, 26),
                                          axis=AX.X, op=ALU.max))
            vdr()
            vop(lambda v: v.tensor_scalar(out=gsl(26, 34), in0=idx_s[:, :],
                                          scalar1=gsl(8, 9), scalar2=None,
                                          op0=ALU.is_equal))
            vdr()
            vop(lambda v: v.tensor_tensor(out=gsl(34, 42), in0=gsl(0, 8),
                                          in1=gsl(26, 34), op=ALU.mult))
            vdr()
            mt_vv = vop(lambda v: v.nop(), sem="vv")
            wait("tensor", "vv", mt_vv)
            tp_mm = mm_group([lambda t: t.transpose(
                ps[4][0:16, 0:128], gsl(26, 42), ident_s[:, :])])
            wait("vector", "mm", tp_mm)
            bank_vv = vop(lambda v, tc=tc: v.tensor_copy(
                msb[:, tc * 128:(tc + 1) * 128], ps[4][0:16, 0:128]), sem="vv")
            wait("tensor", "vv", bank_vv)
        gate_vv = C["vv"]
        gate_mm = C["mm"]
        wait("scalar", "vv", gate_vv)
        dma("scalar", d_m[:, :], msb[0:8, :], sem="smk")
        dma("scalar", d_pm[:, :], msb[8:16, :], sem="smk")
        mask_st = C["smk"]

        wait("gpsimd", "s0", ag1in_st)
        em("gpsimd", lambda g: g.collective_compute(
            "AllGather", ALU.bypass, replica_groups=[list(range(NCORES))],
            ins=[ag1_in.ap().opt()], outs=[ag1_out.ap().opt()]).then_inc(S["cc"], 1))
        bump("cc")

        wait("vector", "smk", mask_st)
        outacc_zero_vv = vop(lambda v: v.memset(A3[:, :], 0.0), sem="vv")

        wait("sync", "smk", mask_st)
        for dc in range(8):
            dma("sync", xe_sl[0][:, dc * 512:(dc + 1) * 512],
                d_xnbf[dc * 128:(dc + 1) * 128, 0:512], sem="xnb")
        dma("sync", pm_e, d_pm.ap()[0:1, 0:512].partition_broadcast(128)[:, 0, :],
            sem="pmm")
        dma("sync", m_e, d_m.ap()[0:1, 0:512].partition_broadcast(128)[:, 0, :],
            sem="pmm")

        w1_slot_reader = [0] * 6
        w1_slot_loaded = [0] * 6
        w1_seq = []
        w1_emit = [0]
        w1_next_chunk = [0]

        def ensure_w1_entries(upto_cyc):
            while w1_next_chunk[0] <= min(upto_cyc, 31):
                ee = w1_next_chunk[0] // 4
                w1_seq.extend((ee, hg) for hg in range(16))
                w1_next_chunk[0] += 1
        w2_slot_reader = [0, 0, 0, 0]
        w2_slot_loaded = [0, 0, 0, 0]
        w2_i = 0
        xe_done_gg = [0, 0]
        hpass_done_mm = {}
        ypass_bank_add_vv = [outacc_zero_vv] * 4
        relu_aq = {}
        ystg_free_gg = [0, 0]
        t1_free_vv = [outacc_zero_vv, outacc_zero_vv]
        xnbf_loaded = {0: C["xnb"]}
        pmm_loaded = {0: C["pmm"]}
        moe_state = {"mm_total": 0, "last_t1_gg": 0}

        def w1_emit_loads(upto):
            while w1_emit[0] <= upto and w1_emit[0] < len(w1_seq):
                i = w1_emit[0]
                e_, hg = w1_seq[i]
                sl = i % 6
                wait("sync", "mm", w1_slot_reader[sl])
                if hg == 0:
                    wait("sync", "gw1p", w1p_mark[e_])
                w1_slot_loaded[sl] = dma(
                    "sync", w1sl[sl], d_w1bf[e_, :, hg * 2048:(hg + 1) * 2048],
                    sem="w1r%d" % sl)
                w1_emit[0] = i + 1

        def emit_moe_chunk(cyc):
            e_, tt = divmod(cyc, 4)
            nonlocal w2_i
            ttp = cyc % 2
            w1_base = 16 * cyc
            ensure_w1_entries(cyc + 1)
            wait("gpsimd", "xnb", xnbf_loaded[cyc])
            wait("gpsimd", "pmm", pmm_loaded[cyc])
            for dc in range(8):
                xe_done_gg[ttp] = gop(lambda g, dc=dc, ttp=ttp: g.tensor_tensor(
                    out=xe_sl[ttp][:, dc * 512:(dc + 1) * 512],
                    in0=xe_sl[ttp][:, dc * 512:(dc + 1) * 512], in1=pm_e,
                    op=ALU.mult), sem="gg")
            if cyc + 1 < 32:
                e2_, tt2 = divmod(cyc + 1, 4)
                tsl2 = slice(tt2 * 512, (tt2 + 1) * 512)
                if cyc >= 1:
                    wait("sync", "mm", hpass_done_mm[cyc - 1])
                for dc in range(8):
                    dma("sync", xe_sl[(cyc + 1) % 2][:, dc * 512:(dc + 1) * 512],
                        d_xnbf[dc * 128:(dc + 1) * 128, tsl2], sem="xnb")
                xnbf_loaded[cyc + 1] = C["xnb"]
            for hc in range(32):
                ghg = w1_base + hc // 2
                sl = ghg % 6
                bank = hc % 2
                if hc == 0:
                    wait("tensor", "gg", xe_done_gg[ttp])
                if hc % 2 == 0:
                    w1_emit_loads(ghg + 4)
                    wait("tensor", "w1r%d" % sl, w1_slot_loaded[sl])
                if (cyc, hc - 2) in relu_aq:
                    wait("tensor", "aq", relu_aq[(cyc, hc - 2)])
                elif hc < 2 and (cyc - 1, 30 + hc) in relu_aq:
                    wait("tensor", "aq", relu_aq[(cyc - 1, 30 + hc)])
                mms = []
                for kd in range(8):
                    mms.append(lambda t, hc=hc, kd=kd, bank=bank, sl=sl, ttp=ttp:
                               t.matmul(
                                   ps[bank][:, :],
                                   w1sl[sl][:, (hc % 2) * 1024 + kd * 128:
                                            (hc % 2) * 1024 + (kd + 1) * 128],
                                   xe_sl[ttp][:, kd * 512:(kd + 1) * 512],
                                   start=(kd == 0), stop=(kd == 7)))
                mmv = mm_group(mms)
                if hc % 2 == 1:
                    w1_slot_reader[sl] = mmv
                wait("scalar", "mm", mmv)
                relu_aq[(cyc, hc)] = aop(
                    lambda s, hc=hc, e_=e_, bank=bank: s.activation(
                        out=h_sb[:, hc * 512:(hc + 1) * 512], in_=ps[bank][:, :],
                        func=AF.Relu, bias=b1_s[:, e_ * 32 + hc:e_ * 32 + hc + 1]),
                    sem="aq")
            hpass_done_mm[cyc] = C["mm"]
            h_done = relu_aq[(cyc, 31)]
            if dbg and cyc in (0, 3):
                wait("scalar", "gg", xe_done_gg[ttp])
                dma("scalar", dbg_m2[0 + (cyc == 3), :, :], xe_sl[ttp], sem="smk")
                wait("scalar", "aq", h_done)
                dma("scalar", dbg_m2[2 + (cyc == 3), :, :], h_sb[:, 0:4096], sem="smk")
            for dcg in range(2):
                for dc4 in range(4):
                    wait("tensor", "vv", ypass_bank_add_vv[dc4])
                for hc in range(32):
                    wi = w2_i % 4
                    wait("sync", "mm", w2_slot_reader[wi])
                    if hc == 0 and dcg == 0 and tt == 0:
                        wait("sync", "gw2p", w2p_mark[e_])
                    w2_slot_loaded[wi] = dma(
                        "sync", w2b[wi],
                        d_w2bf[e_, hc * 128:(hc + 1) * 128,
                               dcg * 512:(dcg + 1) * 512], sem="w2r%d" % wi)
                    w2_i += 1
                    wait("tensor", "w2r%d" % wi, w2_slot_loaded[wi])
                    if hc == 0:
                        wait("tensor", "aq", h_done)
                    mms = []
                    for dc4 in range(4):
                        mms.append(lambda t, wi=wi, dc4=dc4, hc=hc: t.matmul(
                            ps[2 + dc4][:, :],
                            w2b[wi][:, dc4 * 128:(dc4 + 1) * 128],
                            h_sb[:, hc * 512:(hc + 1) * 512],
                            start=(hc == 0), stop=(hc == 31)))
                    mmv = mm_group(mms)
                    w2_slot_reader[wi] = mmv
                for dc4 in range(4):
                    dc = dcg * 4 + dc4
                    pp = dc4 % 2
                    wait("scalar", "mm", mmv)
                    wait("scalar", "gg", ystg_free_gg[pp])
                    ev_aq = aop(lambda s, e_=e_, dc=dc, dc4=dc4, pp=pp:
                                s.activation(
                                    out=ystg[pp][:, :], in_=ps[2 + dc4][:, :],
                                    func=AF.Identity,
                                    bias=b2_s[:, e_ * 8 + dc:e_ * 8 + dc + 1]),
                                sem="aq")
                    wait("gpsimd", "aq", ev_aq)
                    wait("gpsimd", "vv", t1_free_vv[pp])
                    ystg_free_gg[pp] = gop(lambda g, pp=pp: g.tensor_tensor(
                        out=t1_b[pp][:, :], in0=ystg[pp][:, :], in1=m_e,
                        op=ALU.mult), sem="gg")
                    wait("vector", "gg", ystg_free_gg[pp])
                    osl = slice(dc * 2048 + tt * 512, dc * 2048 + (tt + 1) * 512)
                    t1_free_vv[pp] = vop(lambda v, osl=osl, pp=pp: v.tensor_tensor(
                        out=out_acc[:, osl], in0=out_acc[:, osl],
                        in1=t1_b[pp][:, :], op=ALU.add), sem="vv")
                    ypass_bank_add_vv[dc4] = t1_free_vv[pp]
            w1_emit_loads(w1_base + 19)
            moe_state["mm_total"] = C["mm"]
            moe_state["last_t1_gg"] = max(ystg_free_gg)
            if cyc + 1 < 32:
                e2_, tt2 = divmod(cyc + 1, 4)
                tsl2 = slice(tt2 * 512, (tt2 + 1) * 512)
                wait("sync", "gg", moe_state["last_t1_gg"])
                dma("sync", pm_e,
                    d_pm.ap()[e2_:e2_ + 1, tsl2].partition_broadcast(128)[:, 0, :],
                    sem="pmm")
                dma("sync", m_e,
                    d_m.ap()[e2_:e2_ + 1, tsl2].partition_broadcast(128)[:, 0, :],
                    sem="pmm")
                pmm_loaded[cyc + 1] = C["pmm"]
            if tt == 1 and e_ + 2 < E:
                emit_precast(e_ + 2)

        p1 = {
            "rhs_ld": [0, 0], "rhs_free": [0, 0], "bank_free": [0, 0],
            "xi_free": 0, "sz_free": 0, "first": True,
        }

        def emit_p1_gt(gt):
            bu = gt % 2
            rank = gt // 2
            lc0 = (gt % 2) * 1024
            if p1["first"]:
                p1["first"] = False
                wait("gpsimd", "mm", gate_mm)
                for dc in range(8):
                    dma("gpsimd", wxi_s[:, dc * DSL:(dc + 1) * DSL],
                        wxiT[dc * 128:(dc + 1) * 128, :], sem="gp1")
                    dma("gpsimd", wz_s[:, dc * DSL:(dc + 1) * DSL],
                        wzT[dc * 128:(dc + 1) * 128, :], sem="gp1")
                for kc in range(2):
                    dma("gpsimd", xpw_s[:, kc * 96:(kc + 1) * 96],
                        xpwT[kc * 128:(kc + 1) * 128, :], sem="gp1")
                p1["w_ld"] = C["gp1"]
                wait("sync", "cc", 1)
                wait("tensor", "gp1", p1["w_ld"])
                wait("vector", "gp1", p1["w_ld"])
                vop(lambda v: v.memset(xi_ext[0][:, 0:3], 0.0))
                vop(lambda v: v.memset(xi_ext[1][:, 0:3], 0.0))
                vdr()
            wait("sync", "mm", p1["rhs_free"][bu])
            for k in range(8):
                dma("sync", rhs_t[bu][k],
                    ag1_out[rank * D + k * 128:rank * D + (k + 1) * 128,
                            lc0:lc0 + 1024], sem="sr%d" % bu)
            p1["rhs_ld"][bu] = C["sr%d" % bu]
            wait("scalar", "vv", p1["xi_free"])
            for mg in range(4):
                w_s = wxi_s if mg < 2 else wz_s
                mc = mg % 2
                for nt in range(2):
                    bp = (mg * 2 + nt) % 2
                    bk = 6 + bp
                    wait("tensor", "aq", p1["bank_free"][bp])
                    if mg == 0 and nt == 0:
                        wait("tensor", "sr%d" % bu, p1["rhs_ld"][bu])
                    mms = []
                    for kd in range(8):
                        mms.append(lambda t, w_s=w_s, mc=mc, kd=kd, nt=nt, bk=bk,
                                   bu=bu: t.matmul(
                                       ps[bk][:, :],
                                       w_s[:, kd * DSL + mc * 128:
                                           kd * DSL + (mc + 1) * 128],
                                       rhs_t[bu][kd][:, nt * 512:(nt + 1) * 512],
                                       start=(kd == 0), stop=(kd == 7)))
                    mmv = mm_group(mms)
                    wait("scalar", "mm", mmv)
                    if mg < 2:
                        p1["bank_free"][bp] = aop(
                            lambda s, mc=mc, nt=nt, bk=bk: s.copy(
                                xi_ext[mc][:, 3 + nt * 512:3 + (nt + 1) * 512],
                                ps[bk][:, :]), sem="aq")
                    else:
                        if nt == 0:
                            wait("scalar", "ssz", C["ssz"])
                        p1["bank_free"][bp] = aop(
                            lambda s, nt=nt, bk=bk: s.activation(
                                out=sz_st[:, nt * 512:(nt + 1) * 512],
                                in_=ps[bk][:, :], func=AF.Silu), sem="aq")
                if mg >= 2:
                    wait("scalar", "aq", C["aq"])
                    dma("scalar",
                        d_silz[mc * 128:(mc + 1) * 128, gt * GTW:(gt + 1) * GTW],
                        sz_st[:, 0:1024], sem="ssz")
            xi_aq = C["aq"]
            p1["rhs_free"][bu] = C["mm"]
            for mc in range(2):
                wait("vector", "aq", xi_aq)
                src = xi_ext[mc]
                vop(lambda v, src=src, mc=mc: v.tensor_scalar(
                    out=cv_a, in0=src[:, 0:1024],
                    scalar1=convw_s[:, mc * 4:mc * 4 + 1],
                    scalar2=None, op0=ALU.mult))
                vdr()
                for k in range(1, 4):
                    a_buf, b_buf = (cv_a, cv_b) if k % 2 == 1 else (cv_b, cv_a)
                    vop(lambda v, src=src, mc=mc, k=k, a_buf=a_buf, b_buf=b_buf:
                        v.scalar_tensor_tensor(
                            out=b_buf, in0=src[:, k:k + 1024],
                            scalar=convw_s[:, mc * 4 + k:mc * 4 + k + 1],
                            in1=a_buf, op0=ALU.mult, op1=ALU.add))
                    vdr()
                if gt != NGT - 1:
                    if gt == 7:
                        vop(lambda v, src=src: v.memset(src[:, 0:3], 0.0))
                    else:
                        vop(lambda v, src=src: v.tensor_copy(src[:, 0:3],
                                                             src[:, 1024:1027]))
                vdr()
                cv_vv = vop(lambda v: v.nop(), sem="vv")
                wait("scalar", "vv", cv_vv)
                wait("scalar", "sxc", C["sxc"])
                xc_aq = aop(lambda s, mc=mc: s.activation(
                    out=xc_t[mc][:, :], in_=cv_b, func=AF.Silu,
                    bias=convb_s[:, mc:mc + 1]), sem="aq")
                wait("scalar", "aq", xc_aq)
                dma("scalar", d_xc[mc * 128:(mc + 1) * 128, gt * GTW:(gt + 1) * GTW],
                    xc_t[mc][:, :], sem="sxc")
                wait("vector", "aq", xc_aq)
            p1["xi_free"] = C["vv"]
            wait("tensor", "aq", C["aq"])
            for nt in range(2):
                bk = 6 + nt
                sarn = "sar" if nt == 0 else "sar1"
                wait("tensor", "aq", p1["bank_free"][nt])
                mms = []
                for mc in range(2):
                    mms.append(lambda t, mc=mc, nt=nt, bk=bk: t.matmul(
                        ps[bk][0:96, :], xpw_s[:, mc * 96:(mc + 1) * 96],
                        xc_t[mc][:, nt * 512:(nt + 1) * 512],
                        start=(mc == 0), stop=(mc == 1)))
                mmv = mm_group(mms)
                wait("scalar", "mm", mmv)
                wait("scalar", sarn, C[sarn])
                p1["bank_free"][nt] = aop(lambda s, bk=bk, nt=nt: s.copy(
                    po_st[nt][0:96, :], ps[bk][0:96, :]), sem="aq")
                wait("scalar", "aq", p1["bank_free"][nt])
                dma("scalar",
                    ar_in[:, gt * GTW + nt * 512:gt * GTW + (nt + 1) * 512],
                    po_st[nt][0:96, :], sem=sarn)

        sc = {
            "dt_mm": 0, "tl_ld": {}, "du_vv": {}, "dl_ld": {},
            "dA_aq": {}, "scan_vv": {}, "state_gg": {}, "hC_gg": {},
            "yacc_vv": {}, "fin_vv": {}, "bc_ld": {},
            "yop_rd_mm": {}, "b67": [0, 0], "opw_ld": 0,
        }

        def emit_scan_loads(j):
            if j >= 16:
                return
            par = j % 2
            cols = slice(j * GTW, (j + 1) * GTW)
            if j == 0:
                wait("scalar", "smk", C["smk"])
            if j >= 2:
                wait("scalar", "vv", sc["fin_vv"][j - 2])
            for mc in range(2):
                dma("scalar", xc2_t[par][mc], d_xc[mc * 128:(mc + 1) * 128, cols],
                    sem="stl%d" % par)
            sc["tl_ld"][j] = C["stl%d" % par]

        def emit_dt_burst():
            wait("scalar", "cc", 2)
            wait("scalar", "sxc", C["sxc"])
            wait("scalar", "ssz", C["ssz"])
            dma("gpsimd", opw_s[:, 0:D], opwT_sl[0:128, :], sem="gp1")
            dma("gpsimd", opw_s[:, D:2 * D], opwT_sl[128:256, :], sem="gp1")
            sc["opw_ld"] = C["gp1"]
            wait("tensor", "gp1", sc["opw_ld"])
            for jj in range(16):
                cols = slice(jj * GTW, (jj + 1) * GTW)
                wait("scalar", "mm", sc["dt_mm"])
                dma("scalar", dt_sb[:, :], ar_out[0:64, cols], sem="stl0")
                dtld = C["stl0"]
                wait("tensor", "stl0", dtld)
                for d2 in range(2):
                    for nt in range(2):
                        bp = nt
                        bk = 6 + bp
                        wait("tensor", "aq", sc["b67"][bp])
                        mmv = mm_group([lambda t, d2=d2, nt=nt, bk=bk: t.matmul(
                            ps[bk][:, :], dtw_s[:, d2 * 128:(d2 + 1) * 128],
                            dt_sb[:, nt * 512:(nt + 1) * 512],
                            start=True, stop=True)])
                        wait("scalar", "mm", mmv)
                        wait("scalar", "smk", C["smk"])
                        sc["b67"][bp] = aop(lambda s, d2=d2, bk=bk: s.activation(
                            out=es_t[:, :], in_=ps[bk][:, :], func=AF.Exp,
                            bias=dtb_s[:, d2:d2 + 1]), sem="aq")
                        adr()
                        aop(lambda s: s.activation(
                            out=dstage[:, :], in_=es_t[:, :], func=AF.Ln,
                            bias=cst_s[:, 1:2]))
                        adr()
                        dma("scalar",
                            d_delta[d2 * 128:(d2 + 1) * 128,
                                    jj * GTW + nt * 512:jj * GTW + (nt + 1) * 512],
                            dstage[:, :], sem="smk")
                sc["dt_mm"] = C["mm"]

        def emit_scan_tile(j):
            par = j % 2
            st0 = (j == 0 or j == 8)
            last_tile = (j == 7 or j == 15)
            cols_j = slice(j * GTW, (j + 1) * GTW)
            sc["sz_ld"] = sc.get("sz_ld", {})
            tl = max(sc["tl_ld"][j], sc["dl_ld"][j])
            wait("vector", "stl%d" % par, tl)
            wait("scalar", "stl%d" % par, tl)
            for d2 in range(2):
                vop(lambda v, d2=d2, par=par: v.tensor_tensor(
                    out=du_t[d2], in0=delta[d2], in1=xc2_t[par][d2],
                    op=ALU.mult))
            vdr()
            sc["du_vv"][j] = vop(lambda v: v.nop(), sem="vv")
            if dbg and j == 0:
                wait("scalar", "vv", sc["du_vv"][j])
                for d2 in range(2):
                    dma("scalar", dbg_sc[0 + d2, :, :], delta[d2], sem="smk")
                    dma("scalar", dbg_sc[2 + d2, :, :], du_t[d2], sem="smk")
            for n in range(NST):
                nb = n % 4
                for d2 in range(2):
                    ab = (2 * n + d2) % 4
                    if (j, n - 2, d2) in sc["scan_vv"]:
                        wait("scalar", "vv", sc["scan_vv"][(j, n - 2, d2)])
                    elif n < 2 and (j - 1, NST - 2 + n, d2) in sc["scan_vv"]:
                        wait("scalar", "vv", sc["scan_vv"][(j - 1, NST - 2 + n, d2)])
                    sc["dA_aq"][(j, n, d2)] = aop(
                        lambda s, n=n, d2=d2, ab=ab: s.activation(
                            out=dA_pp[ab], in_=delta[d2], func=AF.Exp,
                            scale=a_s[:, d2 * 16 + n:d2 * 16 + n + 1]), sem="aq")
                wait("vector", "sb%d" % nb, sc["bc_ld"][(j, n)])
                for d2 in range(2):
                    vop(lambda v, d2=d2, nb=nb: v.tensor_tensor(
                        out=dBu_t[d2], in0=du_t[d2], in1=Bnb[nb], op=ALU.mult))
                vdr()
                hp = n % 2
                for d2 in range(2):
                    ab = (2 * n + d2) % 4
                    wait("vector", "aq", sc["dA_aq"][(j, n, d2)])
                    for key in [(j, n - 2, d2), (j - 1, NST - 2 + n, d2)]:
                        if key in sc["state_gg"]:
                            wait("vector", "gg", sc["state_gg"][key])
                            break
                        if key in sc["hC_gg"]:
                            wait("vector", "gg", sc["hC_gg"][key])
                            break
                    if st0:
                        sc["scan_vv"][(j, n, d2)] = vop(
                            lambda v, d2=d2, ab=ab, hp=hp: v.tensor_tensor_scan(
                                out=hs_t[hp][d2], data0=dA_pp[ab], data1=dBu_t[d2],
                                initial=0.0, op0=ALU.mult, op1=ALU.add), sem="vv")
                    else:
                        col = state_s[:, d2 * 16 + n:d2 * 16 + n + 1]
                        sc["scan_vv"][(j, n, d2)] = vop(
                            lambda v, d2=d2, col=col, ab=ab, hp=hp:
                            v.tensor_tensor_scan(
                                out=hs_t[hp][d2], data0=dA_pp[ab], data1=dBu_t[d2],
                                initial=col, op0=ALU.mult, op1=ALU.add), sem="vv")
                if n >= 1:
                    for d2 in range(2):
                        wait("vector", "gg", sc["hC_gg"][(j, n - 1, d2)])
                        if n == 1:
                            sc["yacc_vv"][(j, 0, d2)] = vop(
                                lambda v, d2=d2: v.tensor_copy(
                                    y_t[d2], hC_t[d2]), sem="vv")
                        else:
                            sc["yacc_vv"][(j, n - 1, d2)] = vop(
                                lambda v, d2=d2: v.tensor_tensor(
                                    out=y_t[d2], in0=y_t[d2], in1=hC_t[d2],
                                    op=ALU.add), sem="vv")
                for d2 in range(2):
                    wait("gpsimd", "vv", sc["scan_vv"][(j, n, d2)])
                    if (j, n - 1, d2) in sc["yacc_vv"]:
                        wait("gpsimd", "vv", sc["yacc_vv"][(j, n - 1, d2)])
                    sc["hC_gg"][(j, n, d2)] = gop(
                        lambda g, d2=d2, nb=nb, hp=hp: g.tensor_tensor(
                            out=hC_t[d2], in0=hs_t[hp][d2], in1=Cnb[nb],
                            op=ALU.mult), sem="gg")
                    if not last_tile:
                        sc["state_gg"][(j, n, d2)] = gop(
                            lambda g, n=n, d2=d2, hp=hp: g.tensor_copy(
                                state_s[:, d2 * 16 + n:d2 * 16 + n + 1],
                                hs_t[hp][d2][:, 1023:1024]), sem="gg")
                if n == 4:
                    if j >= 1:
                        wait("scalar", "vv", sc["fin_vv"][j - 1])
                    for mc in range(2):
                        dma("scalar", sz2_t[mc],
                            d_silz[mc * 128:(mc + 1) * 128, cols_j],
                            sem="stl%d" % par)
                    sc["sz_ld"][j] = C["stl%d" % par]
                if n <= NST - 3:
                    if n >= 2:
                        emit_bc_load(j, n + 2, j, n - 2)
                    else:
                        emit_bc_load(j, n + 2, j - 1, n + 14)
            for d2 in range(2):
                wait("vector", "gg", sc["hC_gg"][(j, NST - 1, d2)])
                sc["yacc_vv"][(j, NST - 1, d2)] = vop(
                    lambda v, d2=d2: v.tensor_tensor(
                        out=y_t[d2], in0=y_t[d2], in1=hC_t[d2], op=ALU.add),
                    sem="vv")
            vdr()
            if j + 1 < 16:
                emit_bc_load(j + 1, 0, j, NST - 4)
                emit_bc_load(j + 1, 1, j, NST - 3)
                wait("scalar", "vv", sc["du_vv"][j])
                colsn = slice((j + 1) * GTW, (j + 2) * GTW)
                for mc in range(2):
                    dma("scalar", delta[mc],
                        d_delta[mc * 128:(mc + 1) * 128, colsn],
                        sem="stl%d" % ((j + 1) % 2))
                sc["dl_ld"][j + 1] = C["stl%d" % ((j + 1) % 2)]
            if j >= 1:
                wait_lazy("vector", "mm",
                          lambda jj=j: sc["yop_rd_mm"].get(jj - 1, 0))
            wait("vector", "stl%d" % par, sc["sz_ld"][j])
            for d2 in range(2):
                if not last_tile:
                    wait("vector", "gg", sc["state_gg"][(j, NST - 1, d2)])
                wait("vector", "gg", sc["hC_gg"][(j, NST - 1, d2)])
                vop(lambda v, d2=d2, par=par: v.scalar_tensor_tensor(
                    out=hs_t[0][d2], in0=xc2_t[par][d2],
                    scalar=dp_s[:, d2:d2 + 1],
                    in1=y_t[d2], op0=ALU.mult, op1=ALU.add))
            vdr()
            for d2 in range(2):
                vop(lambda v, d2=d2: v.tensor_tensor(
                    out=yop_t[d2], in0=hs_t[0][d2], in1=sz2_t[d2], op=ALU.mult))
            vdr()
            sc["fin_vv"][j] = vop(lambda v: v.nop(), sem="vv")
            if dbg and j == 0:
                wait("scalar", "vv", sc["fin_vv"][j])
                for d2 in range(2):
                    dma("scalar", dbg_sc[4 + d2, :, :], yop_t[d2], sem="smk")
                    dma("scalar", dbg_sc[6 + d2, :, :], xc2_t[par][d2], sem="smk")

        def emit_bc_load(jn, n, gj, gn):
            nb = n % 4
            cols = slice(jn * GTW, (jn + 1) * GTW)
            if (gj, gn, 1) in sc["scan_vv"]:
                wait("scalar", "vv", sc["scan_vv"][(gj, gn, 1)])
            dma("scalar", Bnb[nb],
                d_bc.ap()[n:n + 1, cols].partition_broadcast(128)[:, 0, :],
                sem="sb%d" % nb)
            if (gj, gn, 1) in sc["hC_gg"]:
                wait("scalar", "gg", sc["hC_gg"][(gj, gn, 1)])
            dma("scalar", Cnb[nb],
                d_bc.ap()[16 + n:17 + n, cols].partition_broadcast(128)[:, 0, :],
                sem="sb%d" % nb)
            sc["bc_ld"][(jn, n)] = C["sb%d" % nb]

        def emit_bc_preload(j):
            wait("scalar", "gbc", C["gbc"])
            for n in range(2):
                emit_bc_load(j, n, -1, 0)

        def emit_out_proj(j):
            wait("tensor", "vv", sc["fin_vv"][j])
            shard = j // 2
            lc0 = (j % 2) * 1024
            for mc8 in range(8):
                for nt in range(2):
                    bp = (mc8 * 2 + nt) % 2
                    bk = 6 + bp
                    wait("tensor", "aq", sc["b67"][bp])
                    mms = []
                    for d2 in range(2):
                        mms.append(lambda t, d2=d2, mc8=mc8, nt=nt, bk=bk:
                                   t.matmul(
                                       ps[bk][:, :],
                                       opw_s[:, d2 * D + mc8 * 128:
                                             d2 * D + (mc8 + 1) * 128],
                                       yop_t[d2][:, nt * 512:(nt + 1) * 512],
                                       start=(d2 == 0), stop=(d2 == 1)))
                    mmv = mm_group(mms)
                    wait("scalar", "mm", mmv)
                    wait("scalar", "srs%d" % bp, C["srs%d" % bp])
                    ev = aop(lambda s, bk=bk, bp=bp: s.copy(
                        po2[bp][:, :], ps[bk][:, :]), sem="aq")
                    sc["b67"][bp] = ev
                    wait("scalar", "aq", ev)
                    dma("scalar",
                        rs_in[shard, mc8 * 128:(mc8 + 1) * 128,
                              lc0 + nt * 512:lc0 + (nt + 1) * 512],
                        po2[bp][:, :], sem="srs%d" % bp)
            sc["yop_rd_mm"][j] = C["mm"]

        scan_at = {}
        for jj in range(16):
            scan_at.setdefault(9 + jj * 11 // 8, []).append(jj)

        for cyc in range(32):
            emit_moe_chunk(cyc)
            if 4 <= cyc <= 7:
                for k in range(4):
                    emit_p1_gt((cyc - 4) * 4 + k)
                if cyc == 7:
                    wait("gpsimd", "sar", C["sar"])
                    wait("gpsimd", "sar1", C["sar1"])
                    em("gpsimd", lambda g: g.collective_compute(
                        "AllReduce", ALU.add, replica_groups=[list(range(NCORES))],
                        ins=[ar_in.ap().opt()],
                        outs=[ar_out.ap().opt()]).then_inc(S["cc"], 1))
                    bump("cc")
                    wait("gpsimd", "cc", 2)
                    dma("gpsimd", d_bc[:, :], ar_out[64:96, :], sem="gbc")
                    if dbg:
                        dma("gpsimd", dbg_ar[:, :], ar_out[:, :], sem="gbc")
            if cyc == 8:
                sc["b67"] = list(p1["bank_free"])
                emit_dt_burst()
                wait("scalar", "smk", C["smk"])
                for mc in range(2):
                    dma("scalar", delta[mc], d_delta[mc * 128:(mc + 1) * 128, 0:GTW],
                        sem="stl0")
                sc["dl_ld"][0] = C["stl0"]
                emit_scan_loads(0)
                emit_bc_preload(0)
                emit_scan_loads(1)
            for jj in scan_at.get(cyc, []):
                emit_scan_tile(jj)
                if jj >= 1:
                    emit_out_proj(jj - 1)
                emit_scan_loads(jj + 2)
        emit_out_proj(15)
        rs_total0 = C["srs0"]
        rs_total1 = C["srs1"]

        wait("gpsimd", "srs0", rs_total0)
        wait("gpsimd", "srs1", rs_total1)
        em("gpsimd", lambda g: g.collective_compute(
            "ReduceScatter", ALU.add, replica_groups=[list(range(NCORES))],
            ins=[rs_in.ap().opt()], outs=[rs_out.ap().opt()]).then_inc(S["cc"], 1))
        bump("cc")

        wait("sync", "cc", 3)
        wait("sync", "mm", moe_state["mm_total"])
        if dbg:
            dma("sync", dbg_rs[:, :], rs_out[:, :], sem="sf0")
            dma("sync", dbg_rsp[:, :], rs_in[0, :, :], sem="sf0")
            dma("sync", dbg_bc[:, :], d_bc[:, :], sem="sf0")
            wait("sync", "vv", C["vv"])
            dma("sync", dbg_moe[:, :], out_acc[:, :], sem="sf0")
        dma("sync", rstd_b, d_rows.ap()[0:1, :].partition_broadcast(128)[:, 0, :],
            sem="sf0")
        dma("sync", nmr_b, d_rows.ap()[1:2, :].partition_broadcast(128)[:, 0, :],
            sem="sf0")
        p3b_ld = C["sf0"]
        last_acc_vv = C["vv"]
        f_ld = [p3b_ld, 0]
        f_free = [0, 0]
        f_st = [0, 0]
        i_t = 0
        for tt in range(4):
            for dc in range(8):
                bu = i_t % 2
                rsl = slice(dc * 128, (dc + 1) * 128)
                csl = slice(tt * 512, (tt + 1) * 512)
                wait("sync", "vv", f_free[bu])
                dma("sync", f_rs[bu], rs_out[rsl, csl], sem="sf%d" % bu)
                dma("sync", f_x[bu], xT[rsl, csl], sem="sf%d" % bu)
                f_ld[bu] = C["sf%d" % bu]
                asl = slice(dc * 2048 + tt * 512, dc * 2048 + (tt + 1) * 512)
                wait("scalar", "vv", max(f_free[bu], last_acc_vv))
                g_aq = aop(lambda s, asl=asl, bu=bu: s.activation(
                    out=f_g[bu], in_=out_acc[:, asl], func=AF.Silu), sem="aq")
                wait("vector", "sf%d" % bu, max(f_ld[bu], f_st[bu]))
                wait("vector", "aq", g_aq)
                vop(lambda v, bu=bu, csl=csl: v.tensor_tensor(
                    out=f_s[bu], in0=f_x[bu], in1=rstd_b[:, csl], op=ALU.mult))
                vdr()
                vop(lambda v, bu=bu, csl=csl: v.tensor_tensor(
                    out=f_s[bu], in0=f_s[bu], in1=nmr_b[:, csl], op=ALU.add))
                vdr()
                vop(lambda v, bu=bu: v.tensor_tensor(
                    out=f_s[bu], in0=f_s[bu], in1=f_rs[bu], op=ALU.add))
                vdr()
                vop(lambda v, bu=bu: v.tensor_tensor(
                    out=f_s[bu], in0=f_s[bu], in1=f_g[bu], op=ALU.mult))
                vdr()
                o_vv = vop(lambda v, bu=bu: v.tensor_tensor(
                    out=f_s[bu], in0=f_s[bu], in1=f_x[bu], op=ALU.add), sem="vv")
                f_free[bu] = o_vv
                wait("sync", "vv", o_vv)
                dma("sync", out_d[rsl, csl], f_s[bu], sem="sf%d" % bu)
                f_st[bu] = C["sf%d" % bu]
                i_t += 1
        wait("sync", "sf0", C["sf0"])
        wait("sync", "sf1", C["sf1"])

        @block.tensor
        def _(t):
            for fn in prog["tensor"]:
                fn(t)

        @block.vector
        def _(v):
            for fn in prog["vector"]:
                fn(v)

        @block.scalar
        def _(s):
            for fn in prog["scalar"]:
                fn(s)

        @block.gpsimd
        def _(g):
            for fn in prog["gpsimd"]:
                fn(g)

        @block.sync
        def _(sy):
            for fn in prog["sync"]:
                fn(sy)

    return nc


_NC_CACHE = None


def _get_nc():
    global _NC_CACHE
    if _NC_CACHE is None:
        _NC_CACHE = build_nc()
    return _NC_CACHE


def make_in_maps(L0_attn_x, L0_attn_norm_w, L0_attn_in_proj_w, L0_attn_conv_w,
                 L0_attn_conv_b, L0_attn_x_proj_w, L0_attn_dt_proj_w,
                 L0_attn_dt_proj_b, L0_attn_A_log, L0_attn_Dp, L0_attn_out_proj_w,
                 L0_ffn_gate_w, L0_ffn_gate_b, L0_ffn_w1, L0_ffn_b1, L0_ffn_w2,
                 L0_ffn_b2):
    f = np.float32
    x = np.asarray(L0_attn_x, f)
    w1 = np.asarray(L0_ffn_w1, f)
    w1r = np.ascontiguousarray(
        w1.reshape(E, 32, 128, 8, 128).transpose(0, 4, 1, 3, 2)
        .reshape(E, 128, 32768))
    shared = {
        "ident": np.eye(128, dtype=f),
        "idx128": np.tile(np.arange(E, dtype=f), (128, 1)),
        "gwT": np.ascontiguousarray(np.asarray(L0_ffn_gate_w, f).T),
        "gb": np.asarray(L0_ffn_gate_b, f).reshape(1, E),
        "normw_r": np.ascontiguousarray(np.asarray(L0_attn_norm_w, f).reshape(8, 128).T),
        "b1T": np.ascontiguousarray(
            np.asarray(L0_ffn_b1, f).reshape(E, 32, 128).transpose(2, 0, 1).reshape(128, E * 32)),
        "b2T": np.ascontiguousarray(
            np.asarray(L0_ffn_b2, f).reshape(E, 8, 128).transpose(2, 0, 1).reshape(128, E * 8)),
        "w1r": w1r,
        "w2T": np.ascontiguousarray(np.asarray(L0_ffn_w2, f).transpose(0, 2, 1)),
    }
    ipw = np.asarray(L0_attn_in_proj_w, f)
    cw = np.asarray(L0_attn_conv_w, f)
    cb = np.asarray(L0_attn_conv_b, f)
    xpw = np.asarray(L0_attn_x_proj_w, f)
    dtp = np.asarray(L0_attn_dt_proj_w, f)
    dtb = np.asarray(L0_attn_dt_proj_b, f)
    alog = np.asarray(L0_attn_A_log, f)
    dpv = np.asarray(L0_attn_Dp, f)
    opw = np.asarray(L0_attn_out_proj_w, f)

    in_maps = []
    for c in range(NCORES):
        b_, q = c // 4, c % 4
        sl = slice(DSL * c, DSL * (c + 1))
        m = dict(shared)
        m["xT"] = np.ascontiguousarray(x[b_, q * TOK:(q + 1) * TOK, :].T)
        m["wxiT"] = np.ascontiguousarray(ipw[sl, :].T)
        m["wzT"] = np.ascontiguousarray(ipw[DI + DSL * c:DI + DSL * (c + 1), :].T)
        m["convw_r"] = np.ascontiguousarray(
            cw[sl].reshape(2, 128, 4).transpose(1, 0, 2).reshape(128, 8))
        m["convb_r"] = np.ascontiguousarray(cb[sl].reshape(2, 128).T)
        m["xpwT"] = np.ascontiguousarray(xpw[:, sl].T)
        m["dtw"] = np.ascontiguousarray(dtp[sl, :].T)
        m["dtb_r"] = np.ascontiguousarray(dtb[sl].reshape(2, 128).T)
        m["alog_r"] = np.ascontiguousarray(
            alog[sl].reshape(2, 128, NST).transpose(1, 0, 2).reshape(128, 32))
        m["dp_r"] = np.ascontiguousarray(dpv[sl].reshape(2, 128).T)
        m["opwT_sl"] = np.ascontiguousarray(opw[:, sl].T)
        in_maps.append(m)
    return in_maps


def kernel(**inputs):
    in_maps = make_in_maps(**inputs)
    nc = _get_nc()
    res = run_bass_kernel_spmd(nc, in_maps, core_ids=list(range(NCORES)))
    out = np.empty((B, L, D), dtype=np.float32)
    for c in range(NCORES):
        b_, q = c // 4, c % 4
        out[b_, q * TOK:(q + 1) * TOK, :] = res.results[c]["out"].T
    return out
